# version 02
# baseline (speedup 1.0000x reference)
import contextlib
import math
from collections import defaultdict

import numpy as np

import concourse.bass as bass
import concourse.mybir as mybir
from concourse.bass_utils import run_bass_kernel_spmd

F32 = mybir.dt.float32
BF16 = mybir.dt.bfloat16
I16 = mybir.dt.int16
I32 = mybir.dt.int32
AF = mybir.ActivationFunctionType
ALU = mybir.AluOpType
AX = mybir.AxisListType.X

T = 2048
D = 1024
DC = D // 128
E2 = 2048
EC = E2 // 128
H = 4096
HC = H // 128
NTT = 4
TTW = T // NTT

N1 = 8 * T * D
N2 = 8 * T * H

RG = [[0, 1, 2, 3, 4, 5, 6, 7]]


def ceil_div(a, b):
    return (a + b - 1) // b
BIG = 1.0e30
EPS = 1.0e-5
KTAY = 8

B_SAMP = 26
B_EXACT = 30

C_200, C_ONE, C_N1, C_N2, C_P23, C_HALF, C_D, C_S127, C_ZERO, C_16384, \
    C_LO16, C_HI16, C_G1HALFSC, C_RS1, C_RS2, C_NW1, C_NW2, C_EPSC, C_DINV, C_HLFD = range(20)
NCONST = 24


def build_program(taps=(), n_cores=8):
    taps = set(taps)
    nc = bass.Bass("TRN2", target_bir_lowering=False, debug=False,
                   detect_race_conditions=False, dynamic_dma_scratch_size=4096)

    dx = nc.dram_tensor("x", [T, D], F32, kind="ExternalInput").ap()
    dwxT = nc.dram_tensor("wxT", [D, E2], F32, kind="ExternalInput").ap()
    dwoT = nc.dram_tensor("woT", [D, D], F32, kind="ExternalInput").ap()
    dw1T = nc.dram_tensor("w1T", [D, H], F32, kind="ExternalInput").ap()
    dw2T = nc.dram_tensor("w2T", [H, D], F32, kind="ExternalInput").ap()
    dbxp = nc.dram_tensor("bxp", [128, EC], F32, kind="ExternalInput").ap()
    dafm = nc.dram_tensor("afm", [128, DC * 16], F32, kind="ExternalInput").ap()
    dvecs = nc.dram_tensor("vecs", [128, 6 * DC], F32, kind="ExternalInput").ap()
    dident = nc.dram_tensor("ident", [128, 128], F32, kind="ExternalInput").ap()
    dconsts = nc.dram_tensor("consts", [128, NCONST], F32, kind="ExternalInput").ap()
    dy = nc.dram_tensor("y", [T, D], F32, kind="ExternalOutput").ap()

    d_x0 = nc.dram_tensor("d_x0", [128, DC * T], F32, kind="Internal")
    d_x1 = nc.dram_tensor("d_x1", [128, DC * T], F32, kind="Internal")
    d_h2 = nc.dram_tensor("d_h2", [128, HC * T], F32, kind="Internal")
    d_wxb = nc.dram_tensor("d_wxb", [128, DC * E2], BF16, kind="Internal")
    d_wob = nc.dram_tensor("d_wob", [128, DC * D], BF16, kind="Internal")
    d_w1q = nc.dram_tensor("d_w1q", [128, DC * H], BF16, kind="Internal")
    d_w2q = nc.dram_tensor("d_w2q", [128, HC * D], BF16, kind="Internal")

    cc_n = [0]

    def new_cc(width):
        i = cc_n[0]
        cc_n[0] += 1
        ci = nc.dram_tensor(f"cci{i}", [1, width], F32, kind="Internal")
        co = nc.dram_tensor(f"cco{i}", [n_cores, width], F32, kind="Internal")
        return ci, co

    tap_outs = {}

    def tap(name, shape):
        if name in taps:
            tap_outs[name] = nc.dram_tensor("tap_" + name, shape, F32,
                                            kind="ExternalOutput").ap()
            return tap_outs[name]
        return None

    with contextlib.ExitStack() as ctx:
        sb = lambda n_, s_, dt=F32: ctx.enter_context(nc.sbuf_tensor(n_, s_, dt))
        psum = lambda n_, s_: ctx.enter_context(nc.psum_tensor(n_, s_, F32))

        big0 = sb("big0", [128, DC * T])
        big1 = sb("big1", [128, DC * T])
        bigb = sb("bigb", [128, DC * T], BF16)
        wsb0 = sb("wsb0", [128, 1024], BF16)
        wsb1 = sb("wsb1", [128, 1024], BF16)
        wfp = sb("wfp", [128, 2048])
        ibuf = sb("ibuf", [128, 512], I16)
        sqs = sb("sqs", [128, 1024])
        bc_s = sb("bc_s", [128, TTW])
        bc_t = sb("bc_t", [128, TTW])
        ones512 = sb("ones512", [128, 512])
        ident = sb("identt", [128, 128])
        onesc = sb("onesct", [128, 128])
        ones1 = sb("ones1t", [1, 128])
        cin = sb("cint", [128, NCONST])
        bxp = sb("bxpt", [128, EC])
        afm = sb("afmt", [128, DC * 16])
        vecs = sb("vecst", [128, 6 * DC])
        coef = sb("coeft", [128, KTAY * DC])
        carry = sb("carryt", [128, DC])
        acc = sb("acct", [128, 64])
        nrw = sb("nrwt", [128, 8])
        jnk = sb("jnkt", [128, 64])
        scl = sb("sclt", [128, 20])
        gat = sb("gatt", [1, 256])
        rows = sb("rowst", [1, 4096])

        sc_e = big1[:, 24 * 512:25 * 512]
        bc_u = big1[:, 25 * 512:26 * 512]
        bc_v = big1[:, 26 * 512:27 * 512]

        ps_a = psum("ps_a", [128, 512])
        ps_b = psum("ps_b", [128, 512])
        ps_t = psum("ps_t", [128, 128])
        ps_s = psum("ps_s", [1, 512])
        ps_bc = psum("ps_bc", [128, 512])
        ps_c = psum("ps_c", [128, 1])

        R_S1 = big0[0:1, 8192 + 0 * T:8192 + 1 * T]
        R_S2 = big0[0:1, 8192 + 1 * T:8192 + 2 * T]
        R_SS = big0[0:1, 8192 + 2 * T:8192 + 3 * T]
        R_TT = big0[0:1, 8192 + 3 * T:8192 + 4 * T]
        R_T1 = R_S1
        R_R1 = R_S2
        RQ = rows[0:1, 0:4096]
        RW = 512

        def Q(i):
            return RQ[0:1, i * RW:(i + 1) * RW]

        def Q1(i):
            return RQ[0:1, i * RW:i * RW + 1]

        QV_LO, QV_HI, QV_MID, QV_CNT, QV_GE, QV_T1, QV_T2, QV_RES = range(8)

        SCL_MID, SCL_M1, SCL_S, SCL_GS, SCL_I, SCL_FRAC, SCL_NT, SCL_NTM1, \
            SCL_AMAX, SCL_T2, SCL_TMP, SCL_SHIFT = range(12)
        SCL_NEG = 16

        sems = {}
        for name in ["ld", "gd", "mm", "ve", "ac", "cc", "st"]:
            sems[name] = [ctx.enter_context(nc.semaphore("s_" + name)), 0]

        ops = defaultdict(list)

        def add(e, fn):
            ops[e].append(fn)

        def _mk(e, key):
            def op(fn):
                sem = sems[key][0]

                def w(eng):
                    fn(eng).then_inc(sem, 1)
                add(e, w)
                sems[key][1] += 1
            return op

        vop = _mk("vector", "ve")
        aop = _mk("scalar", "ac")
        mmop = _mk("tensor", "mm")
        gop = _mk("gpsimd", "ve")

        def wt(e, key, val=None):
            sem, v = sems[key][0], sems[key][1]
            if val is not None:
                v = val
            add(e, lambda eng: eng.wait_ge(sem, v))

        def dma(e, key, out, in_):
            sem = sems[key][0]

            def w(eng):
                eng.dma_start(out=out, in_=in_).then_inc(sem, 16)
            add(e, w)
            sems[key][1] += 16

        def sdma(out, in_):
            dma("sync", "ld", out, in_)

        def vspace():
            vop(lambda v: v.memset(sqs[:, 512:1024], 0.0))
            vop(lambda v: v.memset(sqs[:, 512:1024], 0.0))

        def nvop(fn):
            vop(fn)
            vspace()

        def apad():
            aop(lambda s: s.activation(out=jnk[:, 0:64], in_=jnk[:, 0:64], func=AF.Identity))
            aop(lambda s: s.activation(out=jnk[:, 0:64], in_=jnk[:, 0:64], func=AF.Identity))

        def gdma(out, in_):
            dma("gpsimd", "gd", out, in_)
            wt("gpsimd", "gd")

        def allgather(src_ap, width, dst_ap):
            ci, co = new_cc(width)
            wt("gpsimd", "ve")
            wt("gpsimd", "ld")
            gdma(ci.ap()[:], src_ap)
            sem = sems["cc"][0]

            def w(g):
                g.collective_compute("AllGather", ALU.bypass, replica_groups=RG,
                                     ins=[ci.ap()[:]], outs=[co.ap()[:]]).then_inc(sem, 1)
            add("gpsimd", w)
            sems["cc"][1] += 1
            wt("gpsimd", "cc")
            gdma(dst_ap, co.ap().rearrange("a b -> (a b)")[None, :])
            wt("vector", "gd")

        def bcast_row_to_tile(row_ap, width, dst):
            wt("tensor", "ve")
            wt("tensor", "ld")
            mmop(lambda t: t.matmul(ps_bc[:, 0:width], ones1[:], row_ap,
                                    start=True, stop=True))
            wt("vector", "mm")
            vop(lambda v: v.tensor_copy(out=dst, in_=ps_bc[:, 0:width]))

        def spread_scl(row11, col):
            wt("tensor", "ve")
            wt("tensor", "ac")
            mmop(lambda t: t.matmul(ps_c[:], ones1[:], row11, start=True, stop=True))
            wt("scalar", "mm")
            aop(lambda s: s.activation(out=nrw[:, 0:1], in_=ps_c[:], func=AF.Identity))
            apad()
            wt("vector", "ac")
            vop(lambda v, col=col: v.tensor_scalar(out=scl[:, col:col + 1], in0=nrw[:, 0:1],
                                                   scalar1=1.0, scalar2=None, op0=ALU.mult))
            vop(lambda v: v.tensor_scalar(out=scl[:, SCL_NEG:SCL_NEG + 1], in0=nrw[:, 0:1],
                                          scalar1=-1.0, scalar2=None, op0=ALU.mult))
            vspace()

        sdma(big1.rearrange("p (a d) -> p a d", a=16), dx.rearrange("(a p) d -> p a d", p=128))
        sdma(ident[:], dident)
        sdma(cin[:], dconsts)
        sdma(bxp[:], dbxp)
        sdma(afm[:], dafm)
        sdma(vecs[:], dvecs)
        wt("vector", "ld")

        vop(lambda v: v.memset(rows[:], 0.0))
        vop(lambda v: v.memset(scl[:], 0.0))
        vop(lambda v: v.memset(acc[:], 0.0))
        vop(lambda v: v.memset(gat[:], 0.0))
        vop(lambda v: v.memset(onesc[:], 1.0))
        vop(lambda v: v.memset(ones512[:], 1.0))
        vop(lambda v: v.memset(ones1[:], 1.0))
        vop(lambda v: v.memset(carry[:], 0.0))
        nvop(lambda v: v.tensor_scalar(out=scl[:, SCL_TMP:SCL_TMP + 1], in0=cin[:, C_P23:C_P23 + 1],
                                       scalar1=cin[:, C_HALF:C_HALF + 1], scalar2=None, op0=ALU.add))
        nvop(lambda v: v.tensor_tensor(out=scl[:, SCL_TMP:SCL_TMP + 1], in0=scl[:, SCL_TMP:SCL_TMP + 1],
                                       in1=cin[:, C_P23:C_P23 + 1], op=ALU.subtract))
        nvop(lambda v: v.tensor_scalar(out=scl[:, SCL_SHIFT:SCL_SHIFT + 1], in0=scl[:, SCL_TMP:SCL_TMP + 1],
                                       scalar1=cin[:, C_16384:C_16384 + 1], scalar2=None, op0=ALU.add))

        vop(lambda v: v.tensor_copy(out=sqs[:, 0:DC * 16], in_=afm[:]))
        for k in range(1, KTAY + 1):
            if k > 1:
                vop(lambda v: v.tensor_tensor(out=sqs[:, 0:DC * 16], in0=sqs[:, 0:DC * 16],
                                              in1=afm[:], op=ALU.mult))
            vop(lambda v, k=k: v.tensor_reduce(
                out=coef[:, (k - 1) * DC:k * DC],
                in_=sqs[:, 0:DC * 16].rearrange("p (c s) -> p c s", s=16),
                axis=AX, op=ALU.add))
        nvop(lambda v: v.tensor_scalar(out=scl[:, SCL_TMP:SCL_TMP + 1], in0=cin[:, C_ONE:C_ONE + 1],
                                       scalar1=1.0, scalar2=None, op0=ALU.mult))
        for k in range(2, KTAY + 1):
            nvop(lambda v, k=k: v.tensor_scalar(out=scl[:, SCL_TMP:SCL_TMP + 1],
                                                in0=scl[:, SCL_TMP:SCL_TMP + 1],
                                                scalar1=float(k), scalar2=None, op0=ALU.mult))
            nvop(lambda v: v.reciprocal(out=nrw[:, 1:2], in_=scl[:, SCL_TMP:SCL_TMP + 1]))
            nvop(lambda v, k=k: v.tensor_scalar(out=coef[:, (k - 1) * DC:k * DC],
                                                in0=coef[:, (k - 1) * DC:k * DC],
                                                scalar1=nrw[:, 1:2], scalar2=None, op0=ALU.mult))


        def ln_rows():
            A_, B_, C_, D_ = R_S1, R_S2, R_SS, R_TT
            vop(lambda v: v.tensor_scalar(out=A_, in0=A_, scalar1=cin[:1, C_DINV:C_DINV + 1],
                                          scalar2=None, op0=ALU.mult))
            vop(lambda v: v.tensor_scalar(out=B_, in0=B_, scalar1=cin[:1, C_DINV:C_DINV + 1],
                                          scalar2=None, op0=ALU.mult))
            vop(lambda v: v.tensor_tensor(out=D_, in0=A_, in1=A_, op=ALU.mult))
            vop(lambda v: v.tensor_tensor(out=B_, in0=B_, in1=D_, op=ALU.subtract))
            vop(lambda v: v.tensor_scalar(out=D_, in0=B_, scalar1=EPS, scalar2=None,
                                          op0=ALU.add))
            vop(lambda v: v.reciprocal(out=C_, in_=D_))
            wt("scalar", "ve")
            aop(lambda s: s.activation(out=C_, in_=C_, func=AF.Sqrt))
            wt("vector", "ac")
            vop(lambda v: v.tensor_tensor(out=D_, in0=C_, in1=C_, op=ALU.mult))
            vop(lambda v: v.tensor_tensor(out=D_, in0=D_, in1=B_, op=ALU.mult))
            vop(lambda v: v.tensor_copy(out=B_, in_=C_))
            vop(lambda v: v.tensor_scalar(out=D_, in0=D_, scalar1=EPS, scalar2=None,
                                          op0=ALU.add))
            vop(lambda v: v.reciprocal(out=C_, in_=D_))
            wt("scalar", "ve")
            aop(lambda s: s.activation(out=D_, in_=C_, func=AF.Sqrt))
            wt("vector", "ac")
            vop(lambda v: v.tensor_tensor(out=C_, in0=B_, in1=D_, op=ALU.mult))
            vop(lambda v: v.tensor_tensor(out=A_, in0=A_, in1=B_, op=ALU.mult))
            vop(lambda v: v.tensor_scalar(out=A_, in0=A_, scalar1=-1.0,
                                          scalar2=None, op0=ALU.mult))
            vop(lambda v: v.tensor_tensor(out=D_, in0=A_, in1=D_, op=ALU.mult))

        wt("tensor", "ve")
        wt("tensor", "ld")
        tve_tp = [sems["ve"][1], sems["ve"][1]]
        for idx_tp in range(16 * DC):
            tile, dc2 = divmod(idx_tp, DC)
            bank = ps_t[:] if idx_tp % 2 == 0 else ps_bc[:, 0:128]
            wt("tensor", "ve", tve_tp[idx_tp % 2])
            mmop(lambda t, tile=tile, dc2=dc2, bank=bank: t.transpose(
                bank, big1[:, tile * D + dc2 * 128: tile * D + dc2 * 128 + 128], ident[:]))
            wt("vector", "mm")
            vop(lambda v, tile=tile, dc2=dc2, bank=bank: v.tensor_copy(
                out=big0[:, dc2 * T + tile * 128: dc2 * T + tile * 128 + 128], in_=bank))
            tve_tp[idx_tp % 2] = sems["ve"][1]
        wt("sync", "ve")
        sdma(d_x0.ap()[:], big0[:])
        tp_x0 = tap("x0fm", [128, DC * T])
        if tp_x0 is not None:
            dma("sync", "st", tp_x0[:], big0[:])
        for tile in range(16):
            vop(lambda v, tile=tile: v.tensor_reduce(
                out=acc[:, 2 * tile:2 * tile + 1], in_=big1[:, tile * D:(tile + 1) * D],
                axis=AX, op=ALU.add))
            wt("scalar", "ve")
            aop(lambda s, tile=tile: s.activation(
                out=sqs[:, 0:1024], in_=big1[:, tile * D:(tile + 1) * D], func=AF.Square))
            wt("vector", "ac")
            vop(lambda v, tile=tile: v.tensor_reduce(
                out=acc[:, 2 * tile + 1:2 * tile + 2], in_=sqs[:, 0:1024], axis=AX, op=ALU.add))
            wt("sync", "ve")
            wt("sync", "ld")
            sdma(R_S1[0:1, tile * 128:(tile + 1) * 128], acc[:, 2 * tile:2 * tile + 1])
            sdma(R_S2[0:1, tile * 128:(tile + 1) * 128], acc[:, 2 * tile + 1:2 * tile + 2])
        wt("vector", "ld")
        tp_rraw = tap("rows_raw", [1, 2 * T])
        if tp_rraw is not None:
            wt("sync", "ve")
            dma("sync", "st", tp_rraw[0:1, 0:T], R_S1)
            dma("sync", "st", tp_rraw[0:1, T:2 * T], R_S2)
            wt("vector", "st")
        ln_rows()
        tp_rows = tap("rows_ln", [1, 4 * T])
        if tp_rows is not None:
            wt("sync", "ve")
            wt("sync", "ac")
            dma("sync", "st", tp_rows[0:1, 0:T], R_S1)
            dma("sync", "st", tp_rows[0:1, T:2 * T], R_S2)
            dma("sync", "st", tp_rows[0:1, 2 * T:3 * T], R_SS)
            dma("sync", "st", tp_rows[0:1, 3 * T:4 * T], R_TT)
            wt("vector", "st")

        for kc in range(DC):
            wt("sync", "ve")
            wt("sync", "ld")
            sdma(wfp[:, 0:2048], dwxT[kc * 128:(kc + 1) * 128, :])
            wt("vector", "ld")
            vop(lambda v: v.tensor_copy(out=bigb[:, 0:2048], in_=wfp[:, 0:2048]))
            wt("sync", "ve")
            sdma(d_wxb.ap()[:, kc * E2:(kc + 1) * E2], bigb[:, 0:2048])
            sdma(wfp[:, 0:1024], dwoT[kc * 128:(kc + 1) * 128, :])
            wt("vector", "ld")
            vop(lambda v: v.tensor_copy(out=bigb[:, 2048:3072], in_=wfp[:, 0:1024]))
            wt("sync", "ve")
            sdma(d_wob.ap()[:, kc * D:(kc + 1) * D], bigb[:, 2048:3072])

        def wprep(dsrc, ddst, n_chunks, width, acc_col, gcol, nwcol):
            pieces = ceil_div(width, 2048)
            pw = width // pieces
            ncols = n_chunks * pieces
            for ch in range(n_chunks):
                for pc in range(pieces):
                    j = ch * pieces + pc
                    wt("sync", "ve")
                    sdma(wfp[:, 0:pw], dsrc[ch * 128:(ch + 1) * 128, pc * pw:(pc + 1) * pw])
                    wt("vector", "ld")
                    vop(lambda v: v.scalar_tensor_tensor(
                        out=big1[:, 0:pw], in0=wfp[:, 0:pw], scalar=-1.0,
                        in1=wfp[:, 0:pw], op0=ALU.mult, op1=ALU.max))
                    vop(lambda v, j=j: v.tensor_reduce(out=acc[:, acc_col + j:acc_col + j + 1],
                                                       in_=big1[:, 0:pw], axis=AX, op=ALU.add))
            vspace()
            vop(lambda v: v.tensor_reduce(out=acc[:, acc_col + ncols:acc_col + ncols + 1],
                                          in_=acc[:, acc_col:acc_col + ncols], axis=AX, op=ALU.add))
            vspace()
            wt("tensor", "ve")
            wt("tensor", "ac")
            mmop(lambda t: t.matmul(ps_c[:], onesc[:], acc[:, acc_col + ncols:acc_col + ncols + 1],
                                    start=True, stop=True))
            wt("vector", "mm")
            nvop(lambda v: v.tensor_scalar(out=scl[:, gcol:gcol + 1], in0=ps_c[:],
                                           scalar1=cin[:, nwcol:nwcol + 1],
                                           scalar2=cin[:, C_EPSC:C_EPSC + 1],
                                           op0=ALU.mult, op1=ALU.add))
            nvop(lambda v: v.tensor_scalar(out=scl[:, gcol + 1:gcol + 2], in0=scl[:, gcol:gcol + 1],
                                           scalar1=0.5, scalar2=None, op0=ALU.mult))
            for ch in range(n_chunks):
                for pc in range(pieces):
                    wt("sync", "ve")
                    wt("sync", "ac")
                    sdma(wfp[:, 0:pw], dsrc[ch * 128:(ch + 1) * 128, pc * pw:(pc + 1) * pw])
                    wt("vector", "ld")
                    wt("scalar", "ld")
                    wt("scalar", "ve")
                    aop(lambda sx: sx.activation(out=big0[:, 0:pw], in_=wfp[:, 0:pw], func=AF.Sign))
                    vop(lambda v: v.scalar_tensor_tensor(
                        out=big1[:, 0:pw], in0=wfp[:, 0:pw], scalar=-1.0,
                        in1=wfp[:, 0:pw], op0=ALU.mult, op1=ALU.max))
                    vop(lambda v: v.tensor_scalar(out=big1[:, 0:pw], in0=big1[:, 0:pw],
                                                  scalar1=scl[:, gcol + 1:gcol + 2], scalar2=None,
                                                  op0=ALU.is_gt))
                    wt("vector", "ac")
                    vop(lambda v: v.tensor_tensor(out=bigb[:, 0:pw],
                                                  in0=big1[:, 0:pw], in1=big0[:, 0:pw], op=ALU.mult))
                    wt("sync", "ve")
                    sdma(ddst.ap()[:, ch * width + pc * pw:ch * width + (pc + 1) * pw], bigb[:, 0:pw])

        wprep(dw1T, d_w1q, DC, H, 8, 12, C_NW1)
        wprep(dw2T, d_w2q, HC, D, 25, 14, C_NW2)

        tp_xn = tap("xn", [128, DC * T])
        tp_dl = tap("delta", [128, DC * T])
        tp_gate = tap("gate", [128, DC * T])
        tp_h = tap("hcs", [128, DC * T])
        tp_ao = tap("attn_out", [128, DC * T])
        tp_x1 = tap("x1", [128, DC * T])

        X0T = lambda c: big0[:, c * TTW:(c + 1) * TTW]
        AOT = lambda c: big0[:, DC * TTW + c * TTW: DC * TTW + (c + 1) * TTW]
        XNT = lambda c: big1[:, c * TTW:(c + 1) * TTW]
        XNB = lambda c: bigb[:, c * TTW:(c + 1) * TTW]
        sc_a = big1[:, DC * TTW:2 * DC * TTW]
        sc_b = big1[:, 2 * DC * TTW:3 * DC * TTW]
        mixb = bigb[:, DC * TTW:2 * DC * TTW]

        for tt in range(NTT):
            t0 = tt * TTW
            wt("sync", "ve")
            wt("sync", "ld")
            sdma(big0[:, 0:DC * TTW], d_x0.ap().rearrange("p (c t) -> p c t", c=DC)[:, :, t0:t0 + TTW])
            bcast_row_to_tile(R_SS[0:1, t0:t0 + TTW], TTW, bc_s[:])
            bcast_row_to_tile(R_TT[0:1, t0:t0 + TTW], TTW, bc_t[:])
            bcast_row_to_tile(R_R1[0:1, t0:t0 + TTW], TTW, bc_u[:])
            bcast_row_to_tile(R_T1[0:1, t0:t0 + TTW], TTW, bc_v[:])
            wt("vector", "ld")
            for c in range(DC):
                vop(lambda v, c=c: v.tensor_tensor(out=XNT(c), in0=X0T(c), in1=bc_s[:], op=ALU.mult))
                vop(lambda v, c=c: v.tensor_tensor(out=XNT(c), in0=XNT(c), in1=bc_t[:], op=ALU.add))
                vop(lambda v, c=c: v.tensor_scalar(out=XNT(c), in0=XNT(c),
                                                   scalar1=vecs[:, DC + c:DC + c + 1],
                                                   scalar2=vecs[:, 2 * DC + c:2 * DC + c + 1],
                                                   op0=ALU.mult, op1=ALU.add))
                vop(lambda v, c=c: v.tensor_copy(out=XNB(c), in_=XNT(c)))
            if tp_xn is not None:
                wt("sync", "ve")
                for c in range(DC):
                    dma("sync", "st", tp_xn[:, c * T + t0:c * T + t0 + TTW], XNT(c))

            wt("vector", "mm")
            wt("tensor", "ve")
            for mc in range(EC):
                wbuf = wsb0 if mc % 2 == 0 else wsb1
                wt("sync", "mm")
                sdma(wbuf[:, 0:DC * 128],
                     d_wxb.ap().rearrange("p (k e) -> p k e", k=DC)[:, :, mc * 128:(mc + 1) * 128])
                wt("tensor", "ld")
                wt("tensor", "ac", sems["ac"][1])
                for kc in range(DC):
                    mmop(lambda t, mc=mc, kc=kc, wbuf=wbuf: t.matmul(
                        ps_a[:], wbuf[:, kc * 128:(kc + 1) * 128], XNB(kc),
                        start=(kc == 0), stop=(kc == DC - 1)))
                wt("scalar", "mm")
                if mc < DC:
                    aop(lambda s, mc=mc: s.activation(
                        out=sc_e[:], in_=ps_a[:], func=AF.Exp,
                        bias=bxp[:, mc:mc + 1], scale=1.0))
                    aop(lambda s, mc=mc: s.activation(
                        out=sc_a[:, mc * TTW:(mc + 1) * TTW], in_=sc_e[:],
                        func=AF.Ln, bias=1.0, scale=1.0))
                else:
                    aop(lambda s, mc=mc: s.activation(
                        out=sc_b[:, (mc - DC) * TTW:(mc - DC + 1) * TTW], in_=ps_a[:],
                        func=AF.Identity, bias=bxp[:, mc:mc + 1], scale=1.0))
            if tp_dl is not None:
                wt("sync", "ac")
                for c in range(DC):
                    dma("sync", "st", tp_dl[:, c * T + t0:c * T + t0 + TTW],
                        sc_a[:, c * TTW:(c + 1) * TTW])

            wt("vector", "ac")
            for c in range(DC):
                DL = sc_a[:, c * TTW:(c + 1) * TTW]
                BGV = sc_b[:, c * TTW:(c + 1) * TTW]
                vop(lambda v, c=c, DL=DL: v.tensor_scalar(
                    out=sc_e[:], in0=DL,
                    scalar1=coef[:, (KTAY - 1) * DC + c:(KTAY - 1) * DC + c + 1],
                    scalar2=None, op0=ALU.mult))
                for k in range(KTAY - 1, 0, -1):
                    vop(lambda v, c=c, k=k, DL=DL: v.scalar_tensor_tensor(
                        out=sc_e[:], in0=sc_e[:],
                        scalar=coef[:, (k - 1) * DC + c:(k - 1) * DC + c + 1],
                        in1=DL, op0=ALU.add, op1=ALU.mult))
                if tp_gate is not None:
                    vop(lambda v: v.tensor_scalar(out=sqs[:, 0:TTW], in0=sc_e[:],
                                                  scalar1=16.0, scalar2=None, op0=ALU.add))
                    wt("sync", "ve")
                    dma("sync", "st", tp_gate[:, c * T + t0:c * T + t0 + TTW], sqs[:, 0:TTW])
                vop(lambda v, c=c, BGV=BGV: v.tensor_tensor(out=BGV, in0=BGV, in1=XNT(c), op=ALU.mult))
                vop(lambda v, c=c, BGV=BGV: v.tensor_tensor_scan(
                    out=BGV, data0=ones512[:],
                    data1=BGV, initial=carry[:, c:c + 1], op0=ALU.mult, op1=ALU.add))
                vspace()
                vop(lambda v, c=c, BGV=BGV: v.tensor_copy(out=carry[:, c:c + 1],
                                                          in_=BGV[:, TTW - 1:TTW]))
                if tp_h is not None:
                    wt("sync", "ve")
                    dma("sync", "st", tp_h[:, c * T + t0:c * T + t0 + TTW], BGV)
                vop(lambda v, c=c, BGV=BGV: v.scalar_tensor_tensor(
                    out=sc_e[:], in0=sc_e[:], scalar=16.0, in1=BGV,
                    op0=ALU.add, op1=ALU.mult))
                vop(lambda v, c=c: v.scalar_tensor_tensor(
                    out=sc_e[:], in0=XNT(c), scalar=vecs[:, c:c + 1], in1=sc_e[:],
                    op0=ALU.mult, op1=ALU.add))
                vop(lambda v, c=c: v.tensor_copy(out=mixb[:, c * TTW:(c + 1) * TTW], in_=sc_e[:]))

            wt("tensor", "ve")
            for mc in range(DC):
                wbuf = wsb0 if mc % 2 == 0 else wsb1
                wt("sync", "mm")
                sdma(wbuf[:, 0:DC * 128],
                     d_wob.ap().rearrange("p (k e) -> p k e", k=DC)[:, :, mc * 128:(mc + 1) * 128])
                wt("tensor", "ld")
                wt("tensor", "ve", sems["ve"][1])
                for kc in range(DC):
                    mmop(lambda t, mc=mc, kc=kc, wbuf=wbuf: t.matmul(
                        ps_b[:], wbuf[:, kc * 128:(kc + 1) * 128],
                        mixb[:, kc * TTW:(kc + 1) * TTW],
                        start=(kc == 0), stop=(kc == DC - 1)))
                wt("vector", "mm")
                vop(lambda v, mc=mc: v.tensor_scalar(
                    out=AOT(mc), in0=ps_b[:],
                    scalar1=vecs[:, 5 * DC + mc:5 * DC + mc + 1], scalar2=None, op0=ALU.add))
                vop(lambda v, mc=mc: v.tensor_tensor(out=sc_e[:], in0=X0T(mc), in1=bc_u[:], op=ALU.mult))
                vop(lambda v, mc=mc: v.tensor_tensor(out=sc_e[:], in0=sc_e[:], in1=bc_v[:], op=ALU.add))
                vop(lambda v, mc=mc: v.tensor_tensor(out=AOT(mc), in0=AOT(mc), in1=sc_e[:], op=ALU.add))
            if tp_ao is not None:
                wt("sync", "ve")
                for c in range(DC):
                    dma("sync", "st", tp_ao[:, c * T + t0:c * T + t0 + TTW], AOT(c))

            wt("tensor", "ve")
            for c in range(DC):
                mmop(lambda t, c=c: t.matmul(ps_s[:], onesc[:, 0:1], AOT(c),
                                             start=(c == 0), stop=(c == DC - 1)))
            wt("vector", "mm")
            vop(lambda v: v.tensor_copy(out=RQ[0:1, 0:TTW], in_=ps_s[0:1, :]))
            for c in range(DC):
                wt("scalar", "ve")
                wt("scalar", "mm")
                aop(lambda s, c=c: s.activation(out=sqs[:, 0:TTW], in_=AOT(c), func=AF.Square))
                wt("tensor", "ac")
                mmop(lambda t, c=c: t.matmul(ps_s[:], onesc[:, 0:1], sqs[:, 0:TTW],
                                             start=(c == 0), stop=(c == DC - 1)))
            wt("vector", "mm")
            vop(lambda v: v.tensor_copy(out=RQ[0:1, 512:512 + TTW], in_=ps_s[0:1, :]))
            S1s = RQ[0:1, 0:TTW]
            S2s = RQ[0:1, 512:512 + TTW]
            vop(lambda v: v.tensor_scalar(out=S1s, in0=S1s, scalar1=cin[:1, C_DINV:C_DINV + 1],
                                          scalar2=None, op0=ALU.mult))
            vop(lambda v: v.tensor_scalar(out=S2s, in0=S2s, scalar1=cin[:1, C_DINV:C_DINV + 1],
                                          scalar2=None, op0=ALU.mult))
            vop(lambda v: v.tensor_tensor(out=RQ[0:1, 1024:1024 + TTW],
                                          in0=S1s, in1=S1s, op=ALU.mult))
            vop(lambda v: v.tensor_tensor(out=S2s, in0=S2s, in1=RQ[0:1, 1024:1024 + TTW], op=ALU.subtract))
            vop(lambda v: v.tensor_scalar(out=RQ[0:1, 1024:1024 + TTW], in0=S2s,
                                          scalar1=EPS, scalar2=None, op0=ALU.add))
            vop(lambda v: v.reciprocal(out=S2s, in_=RQ[0:1, 1024:1024 + TTW]))
            wt("scalar", "ve")
            aop(lambda s: s.activation(out=S2s, in_=S2s, func=AF.Sqrt))
            wt("vector", "ac")
            vop(lambda v: v.tensor_tensor(out=S1s, in0=S1s, in1=S2s, op=ALU.mult))
            vop(lambda v: v.tensor_scalar(out=S1s, in0=S1s, scalar1=-1.0, scalar2=None, op0=ALU.mult))
            bcast_row_to_tile(S2s, TTW, bc_u[:])
            bcast_row_to_tile(S1s, TTW, bc_v[:])
            for c in range(DC):
                vop(lambda v, c=c: v.tensor_tensor(out=AOT(c), in0=AOT(c), in1=bc_u[:], op=ALU.mult))
                vop(lambda v, c=c: v.tensor_tensor(out=AOT(c), in0=AOT(c), in1=bc_v[:], op=ALU.add))
                vop(lambda v, c=c: v.tensor_tensor(out=AOT(c), in0=AOT(c), in1=X0T(c), op=ALU.add))
            wt("sync", "ve")
            sdma(d_x1.ap().rearrange("p (c t) -> p c t", c=DC)[:, :, t0:t0 + TTW],
                 big0[:, DC * TTW:2 * DC * TTW].rearrange("p (c t) -> p c t", c=DC))
            if tp_x1 is not None:
                for c in range(DC):
                    dma("sync", "st", tp_x1[:, c * T + t0:c * T + t0 + TTW], AOT(c))

        tp_fh = tap("fh", [128, DC * T])
        for tt in range(NTT):
            t0 = tt * TTW
            wt("sync", "mm")
            wt("sync", "ac")
            sdma(big0[:, 0:DC * TTW], d_x1.ap().rearrange("p (c t) -> p c t", c=DC)[:, :, t0:t0 + TTW])
            wt("tensor", "ld")
            wt("vector", "ld")
            for c in range(DC):
                mmop(lambda t, c=c: t.matmul(ps_s[:], onesc[:, 0:1],
                                             big0[:, c * TTW:(c + 1) * TTW],
                                             start=(c == 0), stop=(c == DC - 1)))
            wt("vector", "mm")
            vop(lambda v, t0=t0: v.tensor_copy(out=R_S1[0:1, t0:t0 + TTW], in_=ps_s[0:1, :]))
            for c in range(DC):
                wt("scalar", "ve")
                wt("scalar", "mm")
                aop(lambda s, c=c: s.activation(out=sqs[:, 0:TTW],
                                                in_=big0[:, c * TTW:(c + 1) * TTW], func=AF.Square))
                wt("tensor", "ac")
                mmop(lambda t, c=c: t.matmul(ps_s[:], onesc[:, 0:1], sqs[:, 0:TTW],
                                             start=(c == 0), stop=(c == DC - 1)))
            wt("vector", "mm")
            vop(lambda v, t0=t0: v.tensor_copy(out=R_S2[0:1, t0:t0 + TTW], in_=ps_s[0:1, :]))
        ln_rows()

        for tt in range(NTT):
            t0 = tt * TTW
            bcast_row_to_tile(R_SS[0:1, t0:t0 + TTW], TTW, bc_s[:])
            bcast_row_to_tile(R_TT[0:1, t0:t0 + TTW], TTW, bc_t[:])
            wt("sync", "ve")
            sdma(big0[:, 0:DC * TTW], d_x1.ap().rearrange("p (c t) -> p c t", c=DC)[:, :, t0:t0 + TTW])
            wt("vector", "ld")
            for c in range(DC):
                FH = big1[:, c * T + t0: c * T + t0 + TTW]
                vop(lambda v, c=c, FH=FH: v.tensor_tensor(
                    out=FH, in0=big0[:, c * TTW:(c + 1) * TTW], in1=bc_s[:], op=ALU.mult))
                vop(lambda v, c=c, FH=FH: v.tensor_tensor(out=FH, in0=FH, in1=bc_t[:], op=ALU.add))
                vop(lambda v, c=c, FH=FH: v.tensor_scalar(
                    out=FH, in0=FH, scalar1=vecs[:, 3 * DC + c:3 * DC + c + 1],
                    scalar2=vecs[:, 4 * DC + c:4 * DC + c + 1], op0=ALU.mult, op1=ALU.add))
        if tp_fh is not None:
            wt("sync", "ve")
            dma("sync", "st", tp_fh[:, 0:DC * T], big1[:])

        stream_ve = [0, 0]

        def stream_piece(i2, dap):
            h = i2 % 2
            buf = wfp[:, h * 1024:(h + 1) * 1024]
            wt("sync", "ve", stream_ve[h])
            sdma(buf, dap)
            wt("vector", "ld")
            return h, buf

        def stream_done(h):
            stream_ve[h] = sems["ve"][1]

        def stream_begin():
            stream_ve[0] = stream_ve[1] = sems["ve"][1]

        def count_pass(data_slices, thr_scl_col, acc_cols, signed=False, stream=()):
            j = 0
            for ap_, w_ in data_slices:
                vop(lambda v, ap_=ap_, w_=w_, j=j: v.tensor_scalar(
                    out=big_scr(w_), in0=ap_, scalar1=scl[:, thr_scl_col:thr_scl_col + 1],
                    scalar2=None, op0=ALU.is_gt, op1=ALU.add,
                    accum_out=acc[:, acc_cols[0] + j:acc_cols[0] + j + 1]))
                j += 1
                if signed:
                    vop(lambda v, ap_=ap_, w_=w_, j=j: v.tensor_scalar(
                        out=big_scr(w_), in0=ap_, scalar1=scl[:, SCL_NEG:SCL_NEG + 1],
                        scalar2=None, op0=ALU.is_lt, op1=ALU.add,
                        accum_out=acc[:, acc_cols[0] + j:acc_cols[0] + j + 1]))
                    j += 1
            if stream:
                stream_begin()
                for i2, dap in enumerate(stream):
                    h, buf = stream_piece(i2, dap)
                    vop(lambda v, buf=buf: v.scalar_tensor_tensor(
                        out=sqs[:, 0:1024], in0=buf, scalar=-1.0, in1=buf,
                        op0=ALU.mult, op1=ALU.max))
                    vop(lambda v, j=j, tc=thr_scl_col: v.tensor_scalar(
                        out=sqs[:, 0:1024], in0=sqs[:, 0:1024],
                        scalar1=scl[:, tc:tc + 1], scalar2=None,
                        op0=ALU.is_gt, op1=ALU.add,
                        accum_out=acc[:, acc_cols[0] + j:acc_cols[0] + j + 1]))
                    stream_done(h)
                    j += 1
            vspace()
            vop(lambda v, j=j: v.tensor_reduce(
                out=acc[:, acc_cols[0] + j:acc_cols[0] + j + 1],
                in_=acc[:, acc_cols[0]:acc_cols[0] + j], axis=AX, op=ALU.add))
            vspace()
            return acc[:, acc_cols[0] + j:acc_cols[0] + j + 1]


        def big_scr(w_):
            return wfp[:, 0:w_]

        def global_count(local_col_ap, dst_row):
            wt("tensor", "ve")
            wt("tensor", "ac")
            mmop(lambda t: t.matmul(ps_c[:], onesc[:], local_col_ap, start=True, stop=True))
            wt("scalar", "mm")
            aop(lambda s: s.activation(out=nrw[:, 0:1], in_=ps_c[:], func=AF.Identity))
            apad()
            wt("sync", "ac")
            sdma(gat[0:1, 16:17], nrw[0:1, 0:1])
            allgather(gat[0:1, 16:17], 1, gat[0:1, 8:8 + n_cores])
            vop(lambda v: v.tensor_reduce(out=dst_row, in_=gat[0:1, 8:8 + n_cores], axis=AX, op=ALU.add))
            vspace()

        def quantile(data_slices, samp_slices, n_total, samp_target,
                     nt_col, ntm1_col, frac_col, amax_col, signed=False, stream=(),
                     tag=""):
            stream = list(stream)


            """Exact global quantile of the |values| in data_slices.

            nt_col/ntm1_col/frac_col: scl columns that already hold nt, nt-1
            (per-partition) and frac; result -> scl[amax_col] (and row QV_RES).
            Target-rank values must already be in rows R_NT etc (see caller).
            """
            nsl = len(data_slices)
            for si, (ap_, w_) in enumerate(data_slices):
                if signed:
                    vop(lambda v, ap_=ap_, w_=w_: v.scalar_tensor_tensor(
                        out=big_scr(w_), in0=ap_, scalar=-1.0, in1=ap_,
                        op0=ALU.mult, op1=ALU.max))
                    vop(lambda v, w_=w_, si=si: v.tensor_reduce(
                        out=acc[:, 40 + si:41 + si], in_=big_scr(w_), axis=AX, op=ALU.max))
                else:
                    vop(lambda v, ap_=ap_, si=si: v.tensor_reduce(out=acc[:, 40 + si:41 + si],
                                                                  in_=ap_, axis=AX, op=ALU.max))
            nred = nsl
            if stream:
                stream_begin()
                vop(lambda v: v.memset(bc_s[:], 0.0))
                vop(lambda v: v.memset(bc_t[:], 0.0))
                for i2, dap in enumerate(stream):
                    h, buf = stream_piece(i2, dap)
                    vop(lambda v, buf=buf: v.tensor_tensor(
                        out=bc_s[:], in0=bc_s[:], in1=buf[:, 0:512], op=ALU.max))
                    vop(lambda v, buf=buf: v.scalar_tensor_tensor(
                        out=bc_s[:], in0=buf[:, 0:512], scalar=-1.0, in1=bc_s[:],
                        op0=ALU.mult, op1=ALU.max))
                    vop(lambda v, buf=buf: v.tensor_tensor(
                        out=bc_t[:], in0=bc_t[:], in1=buf[:, 512:1024], op=ALU.max))
                    vop(lambda v, buf=buf: v.scalar_tensor_tensor(
                        out=bc_t[:], in0=buf[:, 512:1024], scalar=-1.0, in1=bc_t[:],
                        op0=ALU.mult, op1=ALU.max))
                    stream_done(h)
                vop(lambda v, nsl=nsl: v.tensor_reduce(
                    out=acc[:, 40 + nsl:41 + nsl], in_=bc_s[:], axis=AX, op=ALU.max))
                vop(lambda v, nsl=nsl: v.tensor_reduce(
                    out=acc[:, 41 + nsl:42 + nsl], in_=bc_t[:], axis=AX, op=ALU.max))
                nred = nsl + 2
            vspace()
            vop(lambda v, nred=nred: v.tensor_reduce(out=acc[:, 40 + nred:41 + nred],
                                                     in_=acc[:, 40:40 + nred], axis=AX, op=ALU.max))
            vspace()
            wt("sync", "ve")
            sdma(gat[0:1, 128:256], acc[:, 40 + nred:41 + nred])
            wt("vector", "ld")
            vop(lambda v: v.tensor_reduce(out=gat[0:1, 4:5], in_=gat[0:1, 128:256], axis=AX, op=ALU.max))
            vspace()
            allgather(gat[0:1, 4:5], 1, gat[0:1, 8:8 + n_cores])
            vop(lambda v: v.tensor_reduce(out=gat[0:1, 5:6], in_=gat[0:1, 8:8 + n_cores], axis=AX, op=ALU.max))
            vspace()
            vop(lambda v: v.tensor_scalar(out=Q(QV_HI), in0=RQ[0:1, 0:RW], scalar1=0.0,
                                          scalar2=gat[0:1, 5:6], op0=ALU.mult, op1=ALU.add))
            vop(lambda v: v.tensor_scalar(out=Q(QV_LO), in0=Q(QV_HI), scalar1=0.0,
                                          scalar2=None, op0=ALU.mult))
            tp_qg = tap("qgat" + tag, [1, 256]) if tag else None
            if tp_qg is not None:
                wt("sync", "ve")
                dma("sync", "st", tp_qg[:], gat[:])
                tp_qa = tap("qacc" + tag, [128, 12])
                dma("sync", "st", tp_qa[:], acc[:, 40:52])
                wt("vector", "st")

            for it in range(B_SAMP):
                vop(lambda v: v.tensor_tensor(out=Q(QV_MID), in0=Q(QV_LO), in1=Q(QV_HI), op=ALU.add))
                vop(lambda v: v.tensor_scalar(out=Q(QV_MID), in0=Q(QV_MID), scalar1=0.5,
                                              scalar2=None, op0=ALU.mult))
                spread_scl(Q1(QV_MID), SCL_MID)
                c_ap = count_pass(samp_slices, SCL_MID, (0,), signed=signed)
                wt("tensor", "ve")
                wt("tensor", "ac")
                mmop(lambda t, c_ap=c_ap: t.matmul(ps_c[:], onesc[:], c_ap, start=True, stop=True))
                wt("scalar", "mm")
                aop(lambda s: s.activation(out=nrw[:, 2:3], in_=ps_c[:], func=AF.Identity))
                apad()
                wt("vector", "ac")
                vop(lambda v: v.tensor_scalar(out=Q(QV_CNT), in0=RQ[0:1, 0:RW], scalar1=0.0,
                                              scalar2=nrw[0:1, 2:3], op0=ALU.mult, op1=ALU.add))
                vop(lambda v, st_=samp_target: v.tensor_scalar(out=Q(QV_GE), in0=Q(QV_CNT),
                                                               scalar1=float(st_), scalar2=None,
                                                               op0=ALU.is_ge))
                vop(lambda v: v.tensor_tensor(out=Q(QV_T1), in0=Q(QV_GE), in1=Q(QV_MID), op=ALU.mult))
                vop(lambda v: v.tensor_scalar(out=Q(QV_T2), in0=Q(QV_GE), scalar1=-1.0, scalar2=1.0,
                                              op0=ALU.mult, op1=ALU.add))
                vop(lambda v: v.tensor_tensor(out=Q(QV_LO), in0=Q(QV_LO), in1=Q(QV_T2), op=ALU.mult))
                vop(lambda v: v.tensor_tensor(out=Q(QV_LO), in0=Q(QV_LO), in1=Q(QV_T1), op=ALU.add))
                vop(lambda v: v.tensor_tensor(out=Q(QV_T1), in0=Q(QV_GE), in1=Q(QV_HI), op=ALU.mult))
                vop(lambda v: v.tensor_tensor(out=Q(QV_T2), in0=Q(QV_T2), in1=Q(QV_MID), op=ALU.mult))
                vop(lambda v: v.tensor_tensor(out=Q(QV_HI), in0=Q(QV_T1), in1=Q(QV_T2), op=ALU.add))

            wt("sync", "ve")
            sdma(gat[0:1, 17:18], Q1(QV_LO))
            sdma(gat[0:1, 18:19], Q1(QV_HI))
            allgather(gat[0:1, 17:19], 2, gat[0:1, 32:32 + 2 * n_cores])
            nvop(lambda v: v.tensor_reduce(out=gat[0:1, 48:49],
                                           in_=gat[0:1, 32:32 + 2 * n_cores].rearrange("p (a b) -> p a b", b=2)[:, :, 0],
                                           axis=AX, op=ALU.min))
            nvop(lambda v: v.tensor_reduce(out=gat[0:1, 49:50],
                                           in_=gat[0:1, 32:32 + 2 * n_cores].rearrange("p (a b) -> p a b", b=2)[:, :, 1],
                                           axis=AX, op=ALU.max))
            nvop(lambda v: v.tensor_scalar(out=gat[0:1, 48:49], in0=gat[0:1, 48:49],
                                           scalar1=0.984375, scalar2=None, op0=ALU.mult))
            nvop(lambda v: v.tensor_scalar(out=gat[0:1, 49:50], in0=gat[0:1, 49:50],
                                           scalar1=1.015625, scalar2=None, op0=ALU.mult))

            vop(lambda v: v.tensor_scalar(out=Q(QV_LO), in0=Q(QV_HI), scalar1=0.0,
                                          scalar2=None, op0=ALU.mult))
            for it in range(B_EXACT):
                if it == 0:
                    vop(lambda v: v.tensor_scalar(out=Q(QV_MID), in0=RQ[0:1, 0:RW], scalar1=0.0,
                                                  scalar2=gat[0:1, 49:50], op0=ALU.mult, op1=ALU.add))
                elif it == 1:
                    vop(lambda v: v.tensor_scalar(out=Q(QV_MID), in0=RQ[0:1, 0:RW], scalar1=0.0,
                                                  scalar2=gat[0:1, 48:49], op0=ALU.mult, op1=ALU.add))
                else:
                    vop(lambda v: v.tensor_tensor(out=Q(QV_MID), in0=Q(QV_LO), in1=Q(QV_HI), op=ALU.add))
                    vop(lambda v: v.tensor_scalar(out=Q(QV_MID), in0=Q(QV_MID), scalar1=0.5,
                                                  scalar2=None, op0=ALU.mult))
                spread_scl(Q1(QV_MID), SCL_MID)
                c_ap = count_pass(data_slices, SCL_MID, (0,), signed=signed, stream=stream)
                global_count(c_ap, gat[0:1, 60:61])
                vspace()
                vop(lambda v: v.tensor_scalar(out=Q(QV_CNT), in0=RQ[0:1, 0:RW], scalar1=0.0,
                                              scalar2=gat[0:1, 60:61], op0=ALU.mult, op1=ALU.add))
                vop(lambda v: v.tensor_scalar(out=Q(QV_GE), in0=Q(QV_CNT),
                                              scalar1=scl[0:1, nt_col:nt_col + 1], scalar2=None,
                                              op0=ALU.is_ge))
                vop(lambda v: v.tensor_tensor(out=Q(QV_T1), in0=Q(QV_GE), in1=Q(QV_MID), op=ALU.mult))
                vop(lambda v: v.tensor_scalar(out=Q(QV_T2), in0=Q(QV_GE), scalar1=-1.0, scalar2=1.0,
                                              op0=ALU.mult, op1=ALU.add))
                vop(lambda v: v.tensor_tensor(out=Q(QV_LO), in0=Q(QV_LO), in1=Q(QV_T2), op=ALU.mult))
                vop(lambda v: v.tensor_tensor(out=Q(QV_LO), in0=Q(QV_LO), in1=Q(QV_T1), op=ALU.add))
                vop(lambda v: v.tensor_tensor(out=Q(QV_T1), in0=Q(QV_GE), in1=Q(QV_HI), op=ALU.mult))
                vop(lambda v: v.tensor_tensor(out=Q(QV_T2), in0=Q(QV_T2), in1=Q(QV_MID), op=ALU.mult))
                vop(lambda v: v.tensor_tensor(out=Q(QV_HI), in0=Q(QV_T1), in1=Q(QV_T2), op=ALU.add))

            def masked_min(thr_col, dst_gat_col):
                j2 = 0
                for ap_, w_ in data_slices:
                    vop(lambda v, ap_=ap_, w_=w_: v.tensor_scalar(
                        out=big_scr(w_), in0=ap_, scalar1=scl[:, thr_col:thr_col + 1],
                        scalar2=BIG, op0=ALU.is_le, op1=ALU.mult))
                    vop(lambda v, ap_=ap_, w_=w_: v.tensor_tensor(
                        out=big_scr(w_), in0=big_scr(w_), in1=ap_, op=ALU.add))
                    vop(lambda v, w_=w_, j2=j2: v.tensor_reduce(out=acc[:, 0 + j2:1 + j2],
                                                                in_=big_scr(w_), axis=AX, op=ALU.min))
                    j2 += 1
                    if signed:
                        vop(lambda v, ap_=ap_, w_=w_: v.tensor_scalar(
                            out=big_scr(w_), in0=ap_, scalar1=scl[:, SCL_NEG:SCL_NEG + 1],
                            scalar2=BIG, op0=ALU.is_ge, op1=ALU.mult))
                        vop(lambda v, ap_=ap_, w_=w_: v.scalar_tensor_tensor(
                            out=big_scr(w_), in0=ap_, scalar=-1.0, in1=big_scr(w_),
                            op0=ALU.mult, op1=ALU.add))
                        vop(lambda v, w_=w_, j2=j2: v.tensor_reduce(out=acc[:, 0 + j2:1 + j2],
                                                                    in_=big_scr(w_), axis=AX, op=ALU.min))
                        j2 += 1
                if stream:
                    stream_begin()
                    for i2, dap in enumerate(stream):
                        h, buf = stream_piece(i2, dap)
                        vop(lambda v, buf=buf: v.scalar_tensor_tensor(
                            out=bc_s[:], in0=buf[:, 0:512], scalar=-1.0, in1=buf[:, 0:512],
                            op0=ALU.mult, op1=ALU.max))
                        vop(lambda v, tc=thr_col: v.tensor_scalar(
                            out=bc_t[:], in0=bc_s[:], scalar1=scl[:, tc:tc + 1],
                            scalar2=BIG, op0=ALU.is_le, op1=ALU.mult))
                        vop(lambda v: v.tensor_tensor(
                            out=bc_t[:], in0=bc_t[:], in1=bc_s[:], op=ALU.max))
                        vop(lambda v, buf=buf: v.scalar_tensor_tensor(
                            out=bc_s[:], in0=buf[:, 512:1024], scalar=-1.0, in1=buf[:, 512:1024],
                            op0=ALU.mult, op1=ALU.max))
                        vop(lambda v, tc=thr_col: v.tensor_scalar(
                            out=sqs[:, 0:512], in0=bc_s[:], scalar1=scl[:, tc:tc + 1],
                            scalar2=BIG, op0=ALU.is_le, op1=ALU.mult))
                        vop(lambda v: v.tensor_tensor(
                            out=sqs[:, 0:512], in0=sqs[:, 0:512], in1=bc_s[:], op=ALU.max))
                        vop(lambda v: v.tensor_tensor(
                            out=bc_t[:], in0=bc_t[:], in1=sqs[:, 0:512], op=ALU.min))
                        vop(lambda v, j2=j2: v.tensor_reduce(
                            out=acc[:, 0 + j2:1 + j2], in_=bc_t[:], axis=AX, op=ALU.min))
                        stream_done(h)
                        j2 += 1
                vspace()
                vop(lambda v, j2=j2: v.tensor_reduce(out=acc[:, 0 + j2:1 + j2],
                                                     in_=acc[:, 0:0 + j2], axis=AX, op=ALU.min))
                vspace()
                wt("sync", "ve")
                sdma(gat[0:1, 128:256], acc[:, 0 + j2:1 + j2])
                wt("vector", "ld")
                vop(lambda v: v.tensor_reduce(out=gat[0:1, 20:21], in_=gat[0:1, 128:256],
                                              axis=AX, op=ALU.min))
                vspace()
                allgather(gat[0:1, 20:21], 1, gat[0:1, 8:8 + n_cores])
                vop(lambda v, dc_=dst_gat_col: v.tensor_reduce(
                    out=gat[0:1, dc_:dc_ + 1], in_=gat[0:1, 8:8 + n_cores], axis=AX, op=ALU.min))
                vspace()

            spread_scl(Q1(QV_LO), SCL_MID)
            vspace()
            masked_min(SCL_MID, 21)
            vspace()
            spread_scl(gat[0:1, 21:22], SCL_M1)
            vspace()
            spread_scl(gat[0:1, 21:22], SCL_M1)
            c_ap = count_pass(data_slices, SCL_M1, (0,), signed=signed, stream=stream)
            global_count(c_ap, gat[0:1, 22:23])
            vspace()
            masked_min(SCL_M1, 23)
            vspace()
            nvop(lambda v: v.tensor_scalar(out=gat[0:1, 24:25], in0=gat[0:1, 22:23],
                                          scalar1=scl[0:1, ntm1_col:ntm1_col + 1], scalar2=None,
                                          op0=ALU.is_ge))
            nvop(lambda v: v.tensor_tensor(out=gat[0:1, 25:26], in0=gat[0:1, 24:25],
                                          in1=gat[0:1, 23:24], op=ALU.mult))
            nvop(lambda v: v.tensor_scalar(out=gat[0:1, 26:27], in0=gat[0:1, 24:25],
                                          scalar1=-1.0, scalar2=1.0, op0=ALU.mult, op1=ALU.add))
            nvop(lambda v: v.tensor_tensor(out=gat[0:1, 26:27], in0=gat[0:1, 26:27],
                                          in1=gat[0:1, 21:22], op=ALU.mult))
            nvop(lambda v: v.tensor_tensor(out=gat[0:1, 25:26], in0=gat[0:1, 25:26],
                                          in1=gat[0:1, 26:27], op=ALU.add))
            nvop(lambda v: v.tensor_scalar(out=gat[0:1, 27:28], in0=scl[0:1, frac_col:frac_col + 1],
                                          scalar1=-1.0, scalar2=1.0, op0=ALU.mult, op1=ALU.add))
            nvop(lambda v: v.tensor_tensor(out=gat[0:1, 27:28], in0=gat[0:1, 27:28],
                                          in1=gat[0:1, 21:22], op=ALU.mult))
            nvop(lambda v: v.tensor_scalar(out=gat[0:1, 25:26], in0=gat[0:1, 25:26],
                                          scalar1=scl[0:1, frac_col:frac_col + 1], scalar2=None,
                                          op0=ALU.mult))
            nvop(lambda v: v.tensor_tensor(out=gat[0:1, 28:29], in0=gat[0:1, 27:28],
                                          in1=gat[0:1, 25:26], op=ALU.add))
            nvop(lambda v: v.tensor_scalar(out=gat[0:1, 28:29], in0=gat[0:1, 28:29],
                                          scalar1=cin[0:1, C_EPSC:C_EPSC + 1], scalar2=None,
                                          op0=ALU.max))
            spread_scl(gat[0:1, 28:29], amax_col)

        i32w = sb("i32w", [1, 512], I32)

        def index_setup(n_elems, i_col, frac_col, nt_col, ntm1_col, nconst_col):
            RA = RQ[0:1, 0 * RW:1 * RW]
            RB = RQ[0:1, 1 * RW:2 * RW]
            RC = RQ[0:1, 2 * RW:3 * RW]
            RD2 = RQ[0:1, 3 * RW:4 * RW]
            vop(lambda v: v.tensor_scalar(out=RA, in0=RQ[0:1, 4 * RW:5 * RW], scalar1=0.0,
                                          scalar2=cin[0:1, C_200:C_200 + 1],
                                          op0=ALU.mult, op1=ALU.add))
            vop(lambda v: v.reciprocal(out=RB, in_=RA))
            vop(lambda v: v.tensor_scalar(out=RA, in0=RA, scalar1=0.0,
                                          scalar2=cin[0:1, C_ONE:C_ONE + 1],
                                          op0=ALU.mult, op1=ALU.add))
            vop(lambda v: v.tensor_tensor(out=RB, in0=RA, in1=RB, op=ALU.subtract))
            vop(lambda v, ncol=nconst_col: v.tensor_scalar(
                out=RC, in0=RA, scalar1=0.0, scalar2=cin[0:1, ncol:ncol + 1],
                op0=ALU.mult, op1=ALU.add))
            vop(lambda v: v.tensor_scalar(out=RD2, in0=RC, scalar1=-1.0, scalar2=None,
                                          op0=ALU.add))
            vop(lambda v: v.tensor_tensor(out=RB, in0=RB, in1=RD2, op=ALU.mult))
            vop(lambda v: v.tensor_copy(out=i32w[:], in_=RB))
            vop(lambda v: v.tensor_copy(out=RA, in_=i32w[:]))
            vop(lambda v: v.tensor_tensor(out=RD2, in0=RB, in1=RA, op=ALU.subtract))
            vop(lambda v: v.tensor_tensor(out=RC, in0=RC, in1=RA, op=ALU.subtract))
            spread_scl(RA[0:1, 0:1], i_col)
            spread_scl(RD2[0:1, 0:1], frac_col)
            spread_scl(RC[0:1, 0:1], nt_col)
            vspace()
            nvop(lambda v, ntc=nt_col, nmc=ntm1_col: v.tensor_scalar(
                out=scl[:, nmc:nmc + 1], in0=scl[:, ntc:ntc + 1], scalar1=-1.0, scalar2=None,
                op0=ALU.add))

        index_setup(N1, SCL_I, SCL_FRAC, SCL_NT, SCL_NTM1, C_N1)
        q1_slices = [(big1[:, c * T:(c + 1) * T], T) for c in range(DC)]
        q1_samp = [(big1[:, 0:DC * T:8], DC * T // 8)]
        samp_target1 = int(round((1.0 - 0.995) * (128 * DC * T // 8)))
        quantile(q1_slices, q1_samp, N1, samp_target1, SCL_NT, SCL_NTM1, SCL_FRAC, SCL_AMAX,
                 signed=True, tag="1")
        tp_am1 = tap("amax1", [1, 4])
        if tp_am1 is not None:
            wt("sync", "ve")
            dma("sync", "st", tp_am1[0:1, 0:1], gat[0:1, 28:29])
            dma("sync", "st", tp_am1[0:1, 1:2], scl[0:1, SCL_I:SCL_I + 1])
            dma("sync", "st", tp_am1[0:1, 2:3], scl[0:1, SCL_FRAC:SCL_FRAC + 1])
            dma("sync", "st", tp_am1[0:1, 3:4], scl[0:1, SCL_NT:SCL_NT + 1])
        tp_q1 = tap("q1dbg", [1, 8])
        if tp_q1 is not None:
            wt("sync", "ve")
            dma("sync", "st", tp_q1[0:1, 0:1], gat[0:1, 21:22])
            dma("sync", "st", tp_q1[0:1, 1:2], gat[0:1, 22:23])
            dma("sync", "st", tp_q1[0:1, 2:3], gat[0:1, 23:24])
            dma("sync", "st", tp_q1[0:1, 3:4], RQ[0:1, QV_LO * RW:QV_LO * RW + 1])
            dma("sync", "st", tp_q1[0:1, 4:5], RQ[0:1, QV_HI * RW:QV_HI * RW + 1])
            dma("sync", "st", tp_q1[0:1, 5:6], gat[0:1, 48:49])
            dma("sync", "st", tp_q1[0:1, 6:7], gat[0:1, 49:50])
            dma("sync", "st", tp_q1[0:1, 7:8], gat[0:1, 60:61])
            wt("vector", "st")

        vspace()
        nvop(lambda v: v.reciprocal(out=nrw[:, 3:4], in_=scl[:, SCL_AMAX:SCL_AMAX + 1]))
        nvop(lambda v: v.tensor_scalar(out=scl[:, SCL_S:SCL_S + 1], in0=nrw[:, 3:4],
                                       scalar1=cin[:, C_S127:C_S127 + 1], scalar2=None, op0=ALU.mult))
        nvop(lambda v: v.reciprocal(out=nrw[:, 3:4], in_=scl[:, SCL_S:SCL_S + 1]))
        nvop(lambda v: v.tensor_tensor(out=scl[:, SCL_GS:SCL_GS + 1], in0=scl[:, 12:13],
                                       in1=nrw[:, 3:4], op=ALU.mult))

        tp_fq = tap("fq", [128, DC * T])
        tp_fqd = tap("fqdbg", [128, 4 * TTW + 20])
        for c in range(DC):
            for p4 in range(4):
                if tp_fqd is not None and c == 0 and p4 == 0:
                    FH = big1[:, 0:TTW]
                    vop(lambda v, FH=FH: v.tensor_scalar(
                        out=big0[:, 0:TTW], in0=FH, scalar1=scl[:, SCL_S:SCL_S + 1],
                        scalar2=scl[:, SCL_SHIFT:SCL_SHIFT + 1], op0=ALU.mult, op1=ALU.add))
                    vop(lambda v: v.tensor_copy(out=ibuf[:, 0:TTW], in_=big0[:, 0:TTW]))
                    vop(lambda v: v.tensor_copy(out=big0[:, TTW:2 * TTW], in_=ibuf[:, 0:TTW]))
                    vop(lambda v: v.tensor_scalar(
                        out=ibuf[:, 0:TTW], in0=ibuf[:, 0:TTW], scalar1=cin[:, C_LO16:C_LO16 + 1],
                        scalar2=cin[:, C_HI16:C_HI16 + 1], op0=ALU.max, op1=ALU.min))
                    vop(lambda v: v.tensor_copy(out=big0[:, 2 * TTW:3 * TTW], in_=ibuf[:, 0:TTW]))
                    vop(lambda v: v.tensor_scalar(
                        out=big0[:, 3 * TTW:4 * TTW], in0=ibuf[:, 0:TTW],
                        scalar1=cin[:, C_16384:C_16384 + 1], scalar2=scl[:, SCL_GS:SCL_GS + 1],
                        op0=ALU.subtract, op1=ALU.mult))
                    vop(lambda v: v.tensor_copy(out=big0[:, 4 * TTW:4 * TTW + 20], in_=scl[:, 0:20]))
                    wt("sync", "ve")
                    dma("sync", "st", tp_fqd[:], big0[:, 0:4 * TTW + 20])
                    wt("vector", "st")
                FH = big1[:, c * T + p4 * TTW: c * T + (p4 + 1) * TTW]
                vop(lambda v, FH=FH: v.tensor_scalar(
                    out=big0[:, 0:TTW], in0=FH, scalar1=scl[:, SCL_S:SCL_S + 1],
                    scalar2=scl[:, SCL_SHIFT:SCL_SHIFT + 1], op0=ALU.mult, op1=ALU.add))
                vop(lambda v: v.tensor_copy(out=ibuf[:, 0:TTW], in_=big0[:, 0:TTW]))
                vop(lambda v: v.tensor_scalar(
                    out=ibuf[:, 0:TTW], in0=ibuf[:, 0:TTW], scalar1=cin[:, C_LO16:C_LO16 + 1],
                    scalar2=cin[:, C_HI16:C_HI16 + 1], op0=ALU.max, op1=ALU.min))
                vop(lambda v, c=c, p4=p4: v.tensor_scalar(
                    out=bigb[:, c * T + p4 * TTW: c * T + (p4 + 1) * TTW], in0=ibuf[:, 0:TTW],
                    scalar1=cin[:, C_16384:C_16384 + 1], scalar2=scl[:, SCL_GS:SCL_GS + 1],
                    op0=ALU.subtract, op1=ALU.mult))
        if tp_fq is not None:
            for c in range(DC):
                vop(lambda v, c=c: v.tensor_copy(out=big0[:, 0:T], in_=bigb[:, c * T:(c + 1) * T]))
                wt("sync", "ve")
                dma("sync", "st", tp_fq[:, c * T:(c + 1) * T], big0[:, 0:T])
                wt("vector", "st")

        wt("tensor", "ve")
        for tt in range(NTT):
            t0 = tt * TTW
            for mj in range(HC):
                wbuf = wsb0 if mj % 2 == 0 else wsb1
                wt("sync", "mm")
                sdma(wbuf[:, 0:DC * 128],
                     d_w1q.ap().rearrange("p (k h) -> p k h", k=DC)[:, :, mj * 128:(mj + 1) * 128])
                wt("tensor", "ld")
                wt("tensor", "ac", sems["ac"][1])
                for kc in range(DC):
                    mmop(lambda t, mj=mj, kc=kc, wbuf=wbuf, t0=t0: t.matmul(
                        ps_a[:], wbuf[:, kc * 128:(kc + 1) * 128],
                        bigb[:, kc * T + t0: kc * T + t0 + TTW],
                        start=(kc == 0), stop=(kc == DC - 1)))
                wt("scalar", "mm")
                wt("scalar", "ld")
                aop(lambda s, mj=mj: s.activation(out=sqs[:, 0:TTW], in_=ps_a[:], func=AF.Gelu))
                wt("sync", "ac")
                dma("sync", "ld", d_h2.ap()[:, mj * T + t0: mj * T + t0 + TTW], sqs[:, 0:TTW])

        index_setup(N2, SCL_I, SCL_FRAC, SCL_NT, SCL_NTM1, C_N2)
        wt("sync", "ve")
        wt("sync", "ld")
        for half, dst in ((0, big0), (1, big1)):
            for ch in range(8):
                src = d_h2.ap()[:, (half * 8 + ch) * T:(half * 8 + ch + 1) * T]
                sdma(dst[:, ch * T:(ch + 1) * T], src)
        wt("vector", "ld")
        for half, dst in ((0, big0), (1, big1)):
            for ch in range(8):
                CH = dst[:, ch * T:(ch + 1) * T]
                vop(lambda v, CH=CH: v.scalar_tensor_tensor(
                    out=CH, in0=CH, scalar=-1.0, in1=CH, op0=ALU.mult, op1=ALU.max))
        q2_slices = [(big0[:, c * T:(c + 1) * T], T) for c in range(DC)] + \
                    [(big1[:, c * T:(c + 1) * T], T) for c in range(DC)]
        q2_stream = [d_h2.ap()[:, ch * T + p * 1024: ch * T + (p + 1) * 1024]
                     for ch in range(16, 32) for p in range(2)]
        q2_samp = [(big0[:, 0:DC * T:16], DC * T // 16), (big1[:, 0:DC * T:16], DC * T // 16)]
        samp_target2 = int(round((1.0 - 0.995) * (128 * 2 * (DC * T // 16))))
        quantile(q2_slices, q2_samp, N2, samp_target2, SCL_NT, SCL_NTM1, SCL_FRAC, SCL_AMAX,
                 signed=False, stream=q2_stream)
        tp_am2 = tap("amax2", [1, 4])
        if tp_am2 is not None:
            wt("sync", "ve")
            dma("sync", "st", tp_am2[0:1, 0:1], gat[0:1, 28:29])
            dma("sync", "st", tp_am2[0:1, 1:2], scl[0:1, SCL_I:SCL_I + 1])
            dma("sync", "st", tp_am2[0:1, 2:3], scl[0:1, SCL_FRAC:SCL_FRAC + 1])
            dma("sync", "st", tp_am2[0:1, 3:4], scl[0:1, SCL_NT:SCL_NT + 1])

        vspace()
        nvop(lambda v: v.reciprocal(out=nrw[:, 3:4], in_=scl[:, SCL_AMAX:SCL_AMAX + 1]))
        nvop(lambda v: v.tensor_scalar(out=scl[:, SCL_S:SCL_S + 1], in0=nrw[:, 3:4],
                                       scalar1=cin[:, C_S127:C_S127 + 1], scalar2=None, op0=ALU.mult))
        nvop(lambda v: v.reciprocal(out=nrw[:, 3:4], in_=scl[:, SCL_S:SCL_S + 1]))
        nvop(lambda v: v.tensor_tensor(out=scl[:, SCL_GS:SCL_GS + 1], in0=scl[:, 14:15],
                                       in1=nrw[:, 3:4], op=ALU.mult))

        for tt in range(NTT):
            t0 = tt * TTW
            for hc2 in range(HC):
                wt("sync", "ve")
                sdma(wfp[:, 0:TTW], d_h2.ap()[:, hc2 * T + t0: hc2 * T + t0 + TTW])
                wt("vector", "ld")
                vop(lambda v, hc2=hc2: v.tensor_scalar(
                    out=sqs[:, 0:TTW], in0=wfp[:, 0:TTW], scalar1=scl[:, SCL_S:SCL_S + 1],
                    scalar2=scl[:, SCL_SHIFT:SCL_SHIFT + 1], op0=ALU.mult, op1=ALU.add))
                vop(lambda v, hc2=hc2: v.tensor_copy(out=ibuf[:, 0:TTW], in_=sqs[:, 0:TTW]))
                vop(lambda v, hc2=hc2: v.tensor_scalar(
                    out=ibuf[:, 0:TTW], in0=ibuf[:, 0:TTW], scalar1=cin[:, C_LO16:C_LO16 + 1],
                    scalar2=cin[:, C_HI16:C_HI16 + 1], op0=ALU.max, op1=ALU.min))
                vop(lambda v, hc2=hc2: v.tensor_scalar(
                    out=bigb[:, hc2 * TTW:(hc2 + 1) * TTW], in0=ibuf[:, 0:TTW],
                    scalar1=cin[:, C_16384:C_16384 + 1], scalar2=scl[:, SCL_GS:SCL_GS + 1],
                    op0=ALU.subtract, op1=ALU.mult))
            wt("sync", "ve")
            sdma(big0[:, 0:DC * TTW], d_x1.ap().rearrange("p (c t) -> p c t", c=DC)[:, :, t0:t0 + TTW])
            wt("tensor", "ve")
            for me in range(DC):
                for kg in range(4):
                    wbuf2 = wsb0 if (me * 4 + kg) % 2 == 0 else wsb1
                    wt("sync", "mm")
                    sdma(wbuf2[:, 0:8 * 128],
                         d_w2q.ap().rearrange("p (k e) -> p k e", k=HC)
                         [:, kg * 8:(kg + 1) * 8, me * 128:(me + 1) * 128])
                    wt("tensor", "ld")
                    wt("tensor", "ve", sems["ve"][1])
                    for kk in range(8):
                        kj = kg * 8 + kk
                        mmop(lambda t, me=me, kj=kj, kk=kk, wbuf2=wbuf2: t.matmul(
                            ps_b[:], wbuf2[:, kk * 128:(kk + 1) * 128],
                            bigb[:, kj * TTW:(kj + 1) * TTW],
                            start=(kj == 0), stop=(kj == HC - 1)))
                wt("vector", "mm")
                vop(lambda v, me=me: v.tensor_copy(
                    out=big0[:, DC * TTW + me * TTW: DC * TTW + (me + 1) * TTW], in_=ps_b[:]))
            wt("tensor", "ve")
            FOT = lambda c: big0[:, DC * TTW + c * TTW: DC * TTW + (c + 1) * TTW]
            for c in range(DC):
                mmop(lambda t, c=c: t.matmul(ps_s[:], onesc[:, 0:1], FOT(c),
                                             start=(c == 0), stop=(c == DC - 1)))
            wt("vector", "mm")
            vop(lambda v: v.tensor_copy(out=RQ[0:1, 0:TTW], in_=ps_s[0:1, :]))
            for c in range(DC):
                wt("scalar", "ve")
                wt("scalar", "mm")
                aop(lambda s, c=c: s.activation(out=sqs[:, 0:TTW],
                                                in_=FOT(c), func=AF.Square))
                wt("tensor", "ac")
                mmop(lambda t, c=c: t.matmul(ps_s[:], onesc[:, 0:1], sqs[:, 0:TTW],
                                             start=(c == 0), stop=(c == DC - 1)))
            wt("vector", "mm")
            vop(lambda v: v.tensor_copy(out=RQ[0:1, 512:512 + TTW], in_=ps_s[0:1, :]))
            S1s = RQ[0:1, 0:TTW]
            S2s = RQ[0:1, 512:512 + TTW]
            vop(lambda v: v.tensor_scalar(out=S1s, in0=S1s, scalar1=cin[:1, C_DINV:C_DINV + 1],
                                          scalar2=None, op0=ALU.mult))
            vop(lambda v: v.tensor_scalar(out=S2s, in0=S2s, scalar1=cin[:1, C_DINV:C_DINV + 1],
                                          scalar2=None, op0=ALU.mult))
            vop(lambda v: v.tensor_tensor(out=RQ[0:1, 1024:1024 + TTW], in0=S1s, in1=S1s, op=ALU.mult))
            vop(lambda v: v.tensor_tensor(out=S2s, in0=S2s, in1=RQ[0:1, 1024:1024 + TTW], op=ALU.subtract))
            vop(lambda v: v.tensor_scalar(out=RQ[0:1, 1024:1024 + TTW], in0=S2s,
                                          scalar1=EPS, scalar2=None, op0=ALU.add))
            vop(lambda v: v.reciprocal(out=S2s, in_=RQ[0:1, 1024:1024 + TTW]))
            wt("scalar", "ve")
            aop(lambda s: s.activation(out=S2s, in_=S2s, func=AF.Sqrt))
            wt("vector", "ac")
            vop(lambda v: v.tensor_tensor(out=S1s, in0=S1s, in1=S2s, op=ALU.mult))
            vop(lambda v: v.tensor_scalar(out=S1s, in0=S1s, scalar1=-1.0, scalar2=None, op0=ALU.mult))
            bcast_row_to_tile(S2s, TTW, bc_u[:])
            bcast_row_to_tile(S1s, TTW, bc_v[:])
            wt("vector", "ld")
            for c in range(DC):
                vop(lambda v, c=c: v.tensor_tensor(out=FOT(c), in0=FOT(c), in1=bc_u[:], op=ALU.mult))
                vop(lambda v, c=c: v.tensor_tensor(out=FOT(c), in0=FOT(c), in1=bc_v[:], op=ALU.add))
                vop(lambda v, c=c: v.tensor_tensor(out=FOT(c), in0=FOT(c),
                                                   in1=big0[:, c * TTW:(c + 1) * TTW], op=ALU.add))
            wt("tensor", "ve")
            tve_o = [sems["ve"][1], sems["ve"][1]]
            for sub in range(NTT):
                wt("vector", "st")
                for c in range(DC):
                    bank = ps_t[:] if c % 2 == 0 else ps_bc[:, 0:128]
                    wt("tensor", "ve", tve_o[c % 2])
                    mmop(lambda t, sub=sub, c=c, bank=bank: t.transpose(
                        bank, FOT(c)[:, sub * 128: sub * 128 + 128], ident[:]))
                    wt("vector", "mm")
                    vop(lambda v, sub=sub, c=c, bank=bank: v.tensor_copy(
                        out=wfp[:, c * 128:(c + 1) * 128], in_=bank))
                    tve_o[c % 2] = sems["ve"][1]
                wt("sync", "ve")
                dma("sync", "st", dy[t0 + sub * 128: t0 + sub * 128 + 128, :], wfp[:, 0:D])

        add("sync", lambda s: s.wait_ge(sems["st"][0], sems["st"][1]))
        add("sync", lambda s: s.wait_ge(sems["ld"][0], sems["ld"][1]))

        with nc.Block() as block:
            @block.sync
            def _(eng):
                for fn in ops["sync"]:
                    fn(eng)

            @block.vector
            def _(eng):
                for fn in ops["vector"]:
                    fn(eng)

            @block.scalar
            def _(eng):
                for fn in ops["scalar"]:
                    fn(eng)

            @block.tensor
            def _(eng):
                for fn in ops["tensor"]:
                    fn(eng)

            @block.gpsimd
            def _(eng):
                for fn in ops["gpsimd"]:
                    fn(eng)

    return nc, tap_outs


def make_consts():
    c = np.zeros(NCONST, np.float32)
    c[C_200] = 200.0
    c[C_ONE] = 1.0
    c[C_N1] = float(N1)
    c[C_N2] = float(N2)
    c[C_P23] = 8388608.0
    c[C_HALF] = 0.5
    c[C_D] = float(D)
    c[C_S127] = 127.0
    c[C_ZERO] = 0.0
    c[C_16384] = 16384.0
    c[C_LO16] = 16384.0 - 128.0
    c[C_HI16] = 16384.0 + 127.0
    c[C_RS1] = 0.0
    c[C_RS2] = 0.0
    c[C_NW1] = 1.0 / (D * H)
    c[C_NW2] = 1.0 / (D * H)
    c[C_EPSC] = EPS
    c[C_DINV] = 1.0 / D
    return np.broadcast_to(c, (128, NCONST)).copy()


def prep_inputs(L0_attn_x, L0_attn_w_xproj, L0_attn_b_xproj, L0_attn_A,
                L0_attn_D_param, L0_attn_w_out, L0_attn_b_out, L0_attn_gamma,
                L0_attn_beta, L0_ffn_ln_g, L0_ffn_ln_b, L0_ffn_w1, L0_ffn_w2):
    f = np.ascontiguousarray

    def fm(v):
        return f(np.asarray(v, np.float32).reshape(DC, 128).T)

    shared = {
        "wxT": f(np.asarray(L0_attn_w_xproj, np.float32).T),
        "woT": f(np.asarray(L0_attn_w_out, np.float32).T),
        "w1T": f(np.asarray(L0_ffn_w1, np.float32).T),
        "w2T": f(np.asarray(L0_ffn_w2, np.float32).T),
        "bxp": f(np.asarray(L0_attn_b_xproj, np.float32).reshape(EC, 128).T),
        "afm": f(np.asarray(L0_attn_A, np.float32).reshape(DC, 128, 16)
                 .transpose(1, 0, 2).reshape(128, DC * 16)),
        "vecs": f(np.concatenate([fm(L0_attn_D_param), fm(L0_attn_gamma), fm(L0_attn_beta),
                                  fm(L0_ffn_ln_g), fm(L0_ffn_ln_b), fm(L0_attn_b_out)], axis=1)),
        "ident": np.eye(128, dtype=np.float32),
        "consts": make_consts(),
    }
    xs = np.asarray(L0_attn_x, np.float32)
    return [dict(shared, x=f(xs[c])) for c in range(8)]


def kernel(**inputs):
    in_maps = prep_inputs(**inputs)
    nc, _ = build_program()
    res = run_bass_kernel_spmd(nc, in_maps, core_ids=list(range(8)))
    return np.stack([res.results[c]["y"] for c in range(8)], axis=0)


# version 03
# speedup vs baseline: 1.4061x; 1.4061x over previous
import contextlib
import math
from collections import defaultdict

import numpy as np

import concourse.bass as bass
import concourse.mybir as mybir
from concourse.bass_utils import run_bass_kernel_spmd

F32 = mybir.dt.float32
BF16 = mybir.dt.bfloat16
I16 = mybir.dt.int16
I32 = mybir.dt.int32
AF = mybir.ActivationFunctionType
ALU = mybir.AluOpType
AX = mybir.AxisListType.X

T = 2048
D = 1024
DC = D // 128
E2 = 2048
EC = E2 // 128
H = 4096
HC = H // 128
NTT = 4
TTW = T // NTT

N1 = 8 * T * D
N2 = 8 * T * H

RG = [[0, 1, 2, 3, 4, 5, 6, 7]]


def ceil_div(a, b):
    return (a + b - 1) // b
BIG = 1.0e30
EPS = 1.0e-5
KTAY = 8

B_ILL = 14

C_200, C_ONE, C_N1, C_N2, C_P23, C_HALF, C_D, C_S127, C_ZERO, C_16384, \
    C_LO16, C_HI16, C_G1HALFSC, C_RS1, C_RS2, C_NW1, C_NW2, C_EPSC, C_DINV, C_HLFD = range(20)
NCONST = 24


def build_program(taps=(), n_cores=8):
    taps = set(taps)
    nc = bass.Bass("TRN2", target_bir_lowering=False, debug=False,
                   detect_race_conditions=False, dynamic_dma_scratch_size=4096)

    dx = nc.dram_tensor("x", [T, D], F32, kind="ExternalInput").ap()
    dwxT = nc.dram_tensor("wxT", [D, E2], F32, kind="ExternalInput").ap()
    dwoT = nc.dram_tensor("woT", [D, D], F32, kind="ExternalInput").ap()
    dw1T = nc.dram_tensor("w1T", [D, H], F32, kind="ExternalInput").ap()
    dw2T = nc.dram_tensor("w2T", [H, D], F32, kind="ExternalInput").ap()
    dbxp = nc.dram_tensor("bxp", [128, EC], F32, kind="ExternalInput").ap()
    dafm = nc.dram_tensor("afm", [128, DC * 16], F32, kind="ExternalInput").ap()
    dvecs = nc.dram_tensor("vecs", [128, 6 * DC], F32, kind="ExternalInput").ap()
    dident = nc.dram_tensor("ident", [128, 128], F32, kind="ExternalInput").ap()
    dconsts = nc.dram_tensor("consts", [128, NCONST], F32, kind="ExternalInput").ap()
    dy = nc.dram_tensor("y", [T, D], F32, kind="ExternalOutput").ap()

    d_x0 = nc.dram_tensor("d_x0", [128, DC * T], F32, kind="Internal")
    d_x1 = nc.dram_tensor("d_x1", [128, DC * T], F32, kind="Internal")
    d_h2 = nc.dram_tensor("d_h2", [128, HC * T], F32, kind="Internal")
    d_wxb = nc.dram_tensor("d_wxb", [128, DC * E2], BF16, kind="Internal")
    d_wob = nc.dram_tensor("d_wob", [128, DC * D], BF16, kind="Internal")
    d_w1q = nc.dram_tensor("d_w1q", [128, DC * H], BF16, kind="Internal")
    d_w2q = nc.dram_tensor("d_w2q", [128, HC * D], BF16, kind="Internal")

    cc_n = [0]

    def new_cc(width):
        i = cc_n[0]
        cc_n[0] += 1
        ci = nc.dram_tensor(f"cci{i}", [1, width], F32, kind="Internal")
        co = nc.dram_tensor(f"cco{i}", [n_cores, width], F32, kind="Internal")
        return ci, co

    tap_outs = {}

    def tap(name, shape):
        if name in taps:
            tap_outs[name] = nc.dram_tensor("tap_" + name, shape, F32,
                                            kind="ExternalOutput").ap()
            return tap_outs[name]
        return None

    with contextlib.ExitStack() as ctx:
        sb = lambda n_, s_, dt=F32: ctx.enter_context(nc.sbuf_tensor(n_, s_, dt))
        psum = lambda n_, s_: ctx.enter_context(nc.psum_tensor(n_, s_, F32))

        big0 = sb("big0", [128, DC * T])
        big1 = sb("big1", [128, DC * T])
        bigb = sb("bigb", [128, DC * T], BF16)
        wsb0 = sb("wsb0", [128, 1024], BF16)
        wsb1 = sb("wsb1", [128, 1024], BF16)
        wfp = sb("wfp", [128, 2048])
        ibuf = sb("ibuf", [128, 512], I16)
        sqs = sb("sqs", [128, 1024])
        bc_s = sb("bc_s", [128, TTW])
        bc_t = sb("bc_t", [128, TTW])
        ones512 = sb("ones512", [128, 512])
        ident = sb("identt", [128, 128])
        onesc = sb("onesct", [128, 128])
        ones1 = sb("ones1t", [1, 128])
        cin = sb("cint", [128, NCONST])
        bxp = sb("bxpt", [128, EC])
        afm = sb("afmt", [128, DC * 16])
        vecs = sb("vecst", [128, 6 * DC])
        coef = sb("coeft", [128, KTAY * DC])
        carry = sb("carryt", [128, DC])
        acc = sb("acct", [128, 64])
        nrw = sb("nrwt", [128, 8])
        jnk = sb("jnkt", [128, 64])
        scl = sb("sclt", [128, 20])
        gat = sb("gatt", [1, 256])
        rows = sb("rowst", [1, 5632])

        sc_e = big1[:, 24 * 512:25 * 512]
        bc_u = big1[:, 25 * 512:26 * 512]
        bc_v = big1[:, 26 * 512:27 * 512]

        ps_a = psum("ps_a", [128, 512])
        ps_b = psum("ps_b", [128, 512])
        ps_t = psum("ps_t", [128, 128])
        ps_s = psum("ps_s", [1, 512])
        ps_bc = psum("ps_bc", [128, 512])
        ps_c = psum("ps_c", [128, 1])

        R_S1 = big0[0:1, 8192 + 0 * T:8192 + 1 * T]
        R_S2 = big0[0:1, 8192 + 1 * T:8192 + 2 * T]
        R_SS = big0[0:1, 8192 + 2 * T:8192 + 3 * T]
        R_TT = big0[0:1, 8192 + 3 * T:8192 + 4 * T]
        R_T1 = R_S1
        R_R1 = R_S2
        RQ = rows[0:1, 0:5632]
        RW = 512

        def Q(i):
            return RQ[0:1, i * RW:(i + 1) * RW]

        def Q1(i):
            return RQ[0:1, i * RW:i * RW + 1]

        QV_LO, QV_HI, QV_MID, QV_CNT, QV_GE, QV_T1, QV_T2, QV_RES, \
            QV_CLO, QV_CHI = range(10)
        QV_TGT = QV_RES

        SCL_MID, SCL_M1, SCL_S, SCL_GS, SCL_I, SCL_FRAC, SCL_NT, SCL_NTM1, \
            SCL_AMAX, SCL_T2, SCL_TMP, SCL_SHIFT = range(12)
        SCL_NEG = 16

        sems = {}
        for name in ["ld", "gd", "mm", "ve", "ac", "cc", "st"]:
            sems[name] = [ctx.enter_context(nc.semaphore("s_" + name)), 0]

        ops = defaultdict(list)

        def add(e, fn):
            ops[e].append(fn)

        def _mk(e, key):
            def op(fn):
                sem = sems[key][0]

                def w(eng):
                    fn(eng).then_inc(sem, 1)
                add(e, w)
                sems[key][1] += 1
            return op

        vop = _mk("vector", "ve")
        aop = _mk("scalar", "ac")
        mmop = _mk("tensor", "mm")
        gop = _mk("gpsimd", "ve")

        def wt(e, key, val=None):
            sem, v = sems[key][0], sems[key][1]
            if val is not None:
                v = val
            add(e, lambda eng: eng.wait_ge(sem, v))

        def dma(e, key, out, in_):
            sem = sems[key][0]

            def w(eng):
                eng.dma_start(out=out, in_=in_).then_inc(sem, 16)
            add(e, w)
            sems[key][1] += 16

        def sdma(out, in_):
            dma("sync", "ld", out, in_)

        def vspace():
            vop(lambda v: v.memset(sqs[:, 512:1024], 0.0))
            vop(lambda v: v.memset(sqs[:, 512:1024], 0.0))

        def nvop(fn):
            vop(fn)
            vspace()

        def apad():
            aop(lambda s: s.activation(out=jnk[:, 0:64], in_=jnk[:, 0:64], func=AF.Identity))
            aop(lambda s: s.activation(out=jnk[:, 0:64], in_=jnk[:, 0:64], func=AF.Identity))

        def gdma(out, in_):
            dma("gpsimd", "gd", out, in_)
            wt("gpsimd", "gd")

        def allgather(src_ap, width, dst_ap):
            ci, co = new_cc(width)
            wt("gpsimd", "ve")
            wt("gpsimd", "ld")
            gdma(ci.ap()[:], src_ap)
            sem = sems["cc"][0]

            def w(g):
                g.collective_compute("AllGather", ALU.bypass, replica_groups=RG,
                                     ins=[ci.ap()[:]], outs=[co.ap()[:]]).then_inc(sem, 1)
            add("gpsimd", w)
            sems["cc"][1] += 1
            wt("gpsimd", "cc")
            gdma(dst_ap, co.ap().rearrange("a b -> (a b)")[None, :])
            wt("vector", "gd")

        def bcast_row_to_tile(row_ap, width, dst):
            wt("tensor", "ve")
            wt("tensor", "ld")
            mmop(lambda t: t.matmul(ps_bc[:, 0:width], ones1[:], row_ap,
                                    start=True, stop=True))
            wt("vector", "mm")
            vop(lambda v: v.tensor_copy(out=dst, in_=ps_bc[:, 0:width]))

        def spread_scl(row11, col):
            wt("tensor", "ve")
            wt("tensor", "ac")
            mmop(lambda t: t.matmul(ps_c[:], ones1[:], row11, start=True, stop=True))
            wt("scalar", "mm")
            aop(lambda s: s.activation(out=nrw[:, 0:1], in_=ps_c[:], func=AF.Identity))
            apad()
            wt("vector", "ac")
            vop(lambda v, col=col: v.tensor_scalar(out=scl[:, col:col + 1], in0=nrw[:, 0:1],
                                                   scalar1=1.0, scalar2=None, op0=ALU.mult))
            vop(lambda v: v.tensor_scalar(out=scl[:, SCL_NEG:SCL_NEG + 1], in0=nrw[:, 0:1],
                                          scalar1=-1.0, scalar2=None, op0=ALU.mult))
            vspace()

        sdma(big1.rearrange("p (a d) -> p a d", a=16), dx.rearrange("(a p) d -> p a d", p=128))
        sdma(ident[:], dident)
        sdma(cin[:], dconsts)
        sdma(bxp[:], dbxp)
        sdma(afm[:], dafm)
        sdma(vecs[:], dvecs)
        wt("vector", "ld")

        vop(lambda v: v.memset(rows[:], 0.0))
        vop(lambda v: v.memset(scl[:], 0.0))
        vop(lambda v: v.memset(acc[:], 0.0))
        vop(lambda v: v.memset(gat[:], 0.0))
        vop(lambda v: v.memset(onesc[:], 1.0))
        vop(lambda v: v.memset(ones512[:], 1.0))
        vop(lambda v: v.memset(ones1[:], 1.0))
        vop(lambda v: v.memset(carry[:], 0.0))
        nvop(lambda v: v.tensor_scalar(out=scl[:, SCL_TMP:SCL_TMP + 1], in0=cin[:, C_P23:C_P23 + 1],
                                       scalar1=cin[:, C_HALF:C_HALF + 1], scalar2=None, op0=ALU.add))
        nvop(lambda v: v.tensor_tensor(out=scl[:, SCL_TMP:SCL_TMP + 1], in0=scl[:, SCL_TMP:SCL_TMP + 1],
                                       in1=cin[:, C_P23:C_P23 + 1], op=ALU.subtract))
        nvop(lambda v: v.tensor_scalar(out=scl[:, SCL_SHIFT:SCL_SHIFT + 1], in0=scl[:, SCL_TMP:SCL_TMP + 1],
                                       scalar1=cin[:, C_16384:C_16384 + 1], scalar2=None, op0=ALU.add))

        vop(lambda v: v.tensor_copy(out=sqs[:, 0:DC * 16], in_=afm[:]))
        for k in range(1, KTAY + 1):
            if k > 1:
                vop(lambda v: v.tensor_tensor(out=sqs[:, 0:DC * 16], in0=sqs[:, 0:DC * 16],
                                              in1=afm[:], op=ALU.mult))
            vop(lambda v, k=k: v.tensor_reduce(
                out=coef[:, (k - 1) * DC:k * DC],
                in_=sqs[:, 0:DC * 16].rearrange("p (c s) -> p c s", s=16),
                axis=AX, op=ALU.add))
        nvop(lambda v: v.tensor_scalar(out=scl[:, SCL_TMP:SCL_TMP + 1], in0=cin[:, C_ONE:C_ONE + 1],
                                       scalar1=1.0, scalar2=None, op0=ALU.mult))
        for k in range(2, KTAY + 1):
            nvop(lambda v, k=k: v.tensor_scalar(out=scl[:, SCL_TMP:SCL_TMP + 1],
                                                in0=scl[:, SCL_TMP:SCL_TMP + 1],
                                                scalar1=float(k), scalar2=None, op0=ALU.mult))
            nvop(lambda v: v.reciprocal(out=nrw[:, 1:2], in_=scl[:, SCL_TMP:SCL_TMP + 1]))
            nvop(lambda v, k=k: v.tensor_scalar(out=coef[:, (k - 1) * DC:k * DC],
                                                in0=coef[:, (k - 1) * DC:k * DC],
                                                scalar1=nrw[:, 1:2], scalar2=None, op0=ALU.mult))


        def ln_rows():
            A_, B_, C_, D_ = R_S1, R_S2, R_SS, R_TT
            vop(lambda v: v.tensor_scalar(out=A_, in0=A_, scalar1=cin[:1, C_DINV:C_DINV + 1],
                                          scalar2=None, op0=ALU.mult))
            vop(lambda v: v.tensor_scalar(out=B_, in0=B_, scalar1=cin[:1, C_DINV:C_DINV + 1],
                                          scalar2=None, op0=ALU.mult))
            vop(lambda v: v.tensor_tensor(out=D_, in0=A_, in1=A_, op=ALU.mult))
            vop(lambda v: v.tensor_tensor(out=B_, in0=B_, in1=D_, op=ALU.subtract))
            vop(lambda v: v.tensor_scalar(out=D_, in0=B_, scalar1=EPS, scalar2=None,
                                          op0=ALU.add))
            vop(lambda v: v.reciprocal(out=C_, in_=D_))
            wt("scalar", "ve")
            aop(lambda s: s.activation(out=C_, in_=C_, func=AF.Sqrt))
            wt("vector", "ac")
            vop(lambda v: v.tensor_tensor(out=D_, in0=C_, in1=C_, op=ALU.mult))
            vop(lambda v: v.tensor_tensor(out=D_, in0=D_, in1=B_, op=ALU.mult))
            vop(lambda v: v.tensor_copy(out=B_, in_=C_))
            vop(lambda v: v.tensor_scalar(out=D_, in0=D_, scalar1=EPS, scalar2=None,
                                          op0=ALU.add))
            vop(lambda v: v.reciprocal(out=C_, in_=D_))
            wt("scalar", "ve")
            aop(lambda s: s.activation(out=D_, in_=C_, func=AF.Sqrt))
            wt("vector", "ac")
            vop(lambda v: v.tensor_tensor(out=C_, in0=B_, in1=D_, op=ALU.mult))
            vop(lambda v: v.tensor_tensor(out=A_, in0=A_, in1=B_, op=ALU.mult))
            vop(lambda v: v.tensor_scalar(out=A_, in0=A_, scalar1=-1.0,
                                          scalar2=None, op0=ALU.mult))
            vop(lambda v: v.tensor_tensor(out=D_, in0=A_, in1=D_, op=ALU.mult))

        wt("tensor", "ve")
        wt("tensor", "ld")
        tve_tp = [sems["ve"][1], sems["ve"][1]]
        for idx_tp in range(16 * DC):
            tile, dc2 = divmod(idx_tp, DC)
            bank = ps_t[:] if idx_tp % 2 == 0 else ps_bc[:, 0:128]
            wt("tensor", "ve", tve_tp[idx_tp % 2])
            mmop(lambda t, tile=tile, dc2=dc2, bank=bank: t.transpose(
                bank, big1[:, tile * D + dc2 * 128: tile * D + dc2 * 128 + 128], ident[:]))
            wt("vector", "mm")
            vop(lambda v, tile=tile, dc2=dc2, bank=bank: v.tensor_copy(
                out=big0[:, dc2 * T + tile * 128: dc2 * T + tile * 128 + 128], in_=bank))
            tve_tp[idx_tp % 2] = sems["ve"][1]
        wt("sync", "ve")
        sdma(d_x0.ap()[:], big0[:])
        tp_x0 = tap("x0fm", [128, DC * T])
        if tp_x0 is not None:
            dma("sync", "st", tp_x0[:], big0[:])
        for tile in range(16):
            vop(lambda v, tile=tile: v.tensor_reduce(
                out=acc[:, 2 * tile:2 * tile + 1], in_=big1[:, tile * D:(tile + 1) * D],
                axis=AX, op=ALU.add))
            wt("scalar", "ve")
            aop(lambda s, tile=tile: s.activation(
                out=sqs[:, 0:1024], in_=big1[:, tile * D:(tile + 1) * D], func=AF.Square))
            wt("vector", "ac")
            vop(lambda v, tile=tile: v.tensor_reduce(
                out=acc[:, 2 * tile + 1:2 * tile + 2], in_=sqs[:, 0:1024], axis=AX, op=ALU.add))
            wt("sync", "ve")
            wt("sync", "ld")
            sdma(R_S1[0:1, tile * 128:(tile + 1) * 128], acc[:, 2 * tile:2 * tile + 1])
            sdma(R_S2[0:1, tile * 128:(tile + 1) * 128], acc[:, 2 * tile + 1:2 * tile + 2])
        wt("vector", "ld")
        tp_rraw = tap("rows_raw", [1, 2 * T])
        if tp_rraw is not None:
            wt("sync", "ve")
            dma("sync", "st", tp_rraw[0:1, 0:T], R_S1)
            dma("sync", "st", tp_rraw[0:1, T:2 * T], R_S2)
            wt("vector", "st")
        ln_rows()
        tp_rows = tap("rows_ln", [1, 4 * T])
        if tp_rows is not None:
            wt("sync", "ve")
            wt("sync", "ac")
            dma("sync", "st", tp_rows[0:1, 0:T], R_S1)
            dma("sync", "st", tp_rows[0:1, T:2 * T], R_S2)
            dma("sync", "st", tp_rows[0:1, 2 * T:3 * T], R_SS)
            dma("sync", "st", tp_rows[0:1, 3 * T:4 * T], R_TT)
            wt("vector", "st")

        for kc in range(DC):
            wt("sync", "ve")
            wt("sync", "ld")
            sdma(wfp[:, 0:2048], dwxT[kc * 128:(kc + 1) * 128, :])
            wt("vector", "ld")
            vop(lambda v: v.tensor_copy(out=bigb[:, 0:2048], in_=wfp[:, 0:2048]))
            wt("sync", "ve")
            sdma(d_wxb.ap()[:, kc * E2:(kc + 1) * E2], bigb[:, 0:2048])
            sdma(wfp[:, 0:1024], dwoT[kc * 128:(kc + 1) * 128, :])
            wt("vector", "ld")
            vop(lambda v: v.tensor_copy(out=bigb[:, 2048:3072], in_=wfp[:, 0:1024]))
            wt("sync", "ve")
            sdma(d_wob.ap()[:, kc * D:(kc + 1) * D], bigb[:, 2048:3072])

        def wprep(dsrc, ddst, n_chunks, width, acc_col, gcol, nwcol):
            pieces = ceil_div(width, 2048)
            pw = width // pieces
            ncols = n_chunks * pieces
            for ch in range(n_chunks):
                for pc in range(pieces):
                    j = ch * pieces + pc
                    wt("sync", "ve")
                    sdma(wfp[:, 0:pw], dsrc[ch * 128:(ch + 1) * 128, pc * pw:(pc + 1) * pw])
                    wt("vector", "ld")
                    vop(lambda v: v.scalar_tensor_tensor(
                        out=big1[:, 0:pw], in0=wfp[:, 0:pw], scalar=-1.0,
                        in1=wfp[:, 0:pw], op0=ALU.mult, op1=ALU.max))
                    vop(lambda v, j=j: v.tensor_reduce(out=acc[:, acc_col + j:acc_col + j + 1],
                                                       in_=big1[:, 0:pw], axis=AX, op=ALU.add))
            vspace()
            vop(lambda v: v.tensor_reduce(out=acc[:, acc_col + ncols:acc_col + ncols + 1],
                                          in_=acc[:, acc_col:acc_col + ncols], axis=AX, op=ALU.add))
            vspace()
            wt("tensor", "ve")
            wt("tensor", "ac")
            mmop(lambda t: t.matmul(ps_c[:], onesc[:], acc[:, acc_col + ncols:acc_col + ncols + 1],
                                    start=True, stop=True))
            wt("vector", "mm")
            nvop(lambda v: v.tensor_scalar(out=scl[:, gcol:gcol + 1], in0=ps_c[:],
                                           scalar1=cin[:, nwcol:nwcol + 1],
                                           scalar2=cin[:, C_EPSC:C_EPSC + 1],
                                           op0=ALU.mult, op1=ALU.add))
            nvop(lambda v: v.tensor_scalar(out=scl[:, gcol + 1:gcol + 2], in0=scl[:, gcol:gcol + 1],
                                           scalar1=0.5, scalar2=None, op0=ALU.mult))
            for ch in range(n_chunks):
                for pc in range(pieces):
                    wt("sync", "ve")
                    wt("sync", "ac")
                    sdma(wfp[:, 0:pw], dsrc[ch * 128:(ch + 1) * 128, pc * pw:(pc + 1) * pw])
                    wt("vector", "ld")
                    wt("scalar", "ld")
                    wt("scalar", "ve")
                    aop(lambda sx: sx.activation(out=big0[:, 0:pw], in_=wfp[:, 0:pw], func=AF.Sign))
                    vop(lambda v: v.scalar_tensor_tensor(
                        out=big1[:, 0:pw], in0=wfp[:, 0:pw], scalar=-1.0,
                        in1=wfp[:, 0:pw], op0=ALU.mult, op1=ALU.max))
                    vop(lambda v: v.tensor_scalar(out=big1[:, 0:pw], in0=big1[:, 0:pw],
                                                  scalar1=scl[:, gcol + 1:gcol + 2], scalar2=None,
                                                  op0=ALU.is_gt))
                    wt("vector", "ac")
                    vop(lambda v: v.tensor_tensor(out=bigb[:, 0:pw],
                                                  in0=big1[:, 0:pw], in1=big0[:, 0:pw], op=ALU.mult))
                    wt("sync", "ve")
                    sdma(ddst.ap()[:, ch * width + pc * pw:ch * width + (pc + 1) * pw], bigb[:, 0:pw])

        wprep(dw1T, d_w1q, DC, H, 8, 12, C_NW1)
        wprep(dw2T, d_w2q, HC, D, 25, 14, C_NW2)

        tp_xn = tap("xn", [128, DC * T])
        tp_dl = tap("delta", [128, DC * T])
        tp_gate = tap("gate", [128, DC * T])
        tp_h = tap("hcs", [128, DC * T])
        tp_ao = tap("attn_out", [128, DC * T])
        tp_x1 = tap("x1", [128, DC * T])

        X0T = lambda c: big0[:, c * TTW:(c + 1) * TTW]
        AOT = lambda c: big0[:, DC * TTW + c * TTW: DC * TTW + (c + 1) * TTW]
        XNT = lambda c: big1[:, c * TTW:(c + 1) * TTW]
        XNB = lambda c: bigb[:, c * TTW:(c + 1) * TTW]
        sc_a = big1[:, DC * TTW:2 * DC * TTW]
        sc_b = big1[:, 2 * DC * TTW:3 * DC * TTW]
        mixb = bigb[:, DC * TTW:2 * DC * TTW]

        for tt in range(NTT):
            t0 = tt * TTW
            wt("sync", "ve")
            wt("sync", "ld")
            sdma(big0[:, 0:DC * TTW], d_x0.ap().rearrange("p (c t) -> p c t", c=DC)[:, :, t0:t0 + TTW])
            bcast_row_to_tile(R_SS[0:1, t0:t0 + TTW], TTW, bc_s[:])
            bcast_row_to_tile(R_TT[0:1, t0:t0 + TTW], TTW, bc_t[:])
            bcast_row_to_tile(R_R1[0:1, t0:t0 + TTW], TTW, bc_u[:])
            bcast_row_to_tile(R_T1[0:1, t0:t0 + TTW], TTW, bc_v[:])
            wt("vector", "ld")
            for c in range(DC):
                vop(lambda v, c=c: v.tensor_tensor(out=XNT(c), in0=X0T(c), in1=bc_s[:], op=ALU.mult))
                vop(lambda v, c=c: v.tensor_tensor(out=XNT(c), in0=XNT(c), in1=bc_t[:], op=ALU.add))
                vop(lambda v, c=c: v.tensor_scalar(out=XNT(c), in0=XNT(c),
                                                   scalar1=vecs[:, DC + c:DC + c + 1],
                                                   scalar2=vecs[:, 2 * DC + c:2 * DC + c + 1],
                                                   op0=ALU.mult, op1=ALU.add))
                vop(lambda v, c=c: v.tensor_copy(out=XNB(c), in_=XNT(c)))
            if tp_xn is not None:
                wt("sync", "ve")
                for c in range(DC):
                    dma("sync", "st", tp_xn[:, c * T + t0:c * T + t0 + TTW], XNT(c))

            wt("vector", "mm")
            wt("tensor", "ve")
            for mc in range(EC):
                wbuf = wsb0 if mc % 2 == 0 else wsb1
                wt("sync", "mm")
                sdma(wbuf[:, 0:DC * 128],
                     d_wxb.ap().rearrange("p (k e) -> p k e", k=DC)[:, :, mc * 128:(mc + 1) * 128])
                wt("tensor", "ld")
                wt("tensor", "ac", sems["ac"][1])
                for kc in range(DC):
                    mmop(lambda t, mc=mc, kc=kc, wbuf=wbuf: t.matmul(
                        ps_a[:], wbuf[:, kc * 128:(kc + 1) * 128], XNB(kc),
                        start=(kc == 0), stop=(kc == DC - 1)))
                wt("scalar", "mm")
                if mc < DC:
                    aop(lambda s, mc=mc: s.activation(
                        out=sc_e[:], in_=ps_a[:], func=AF.Exp,
                        bias=bxp[:, mc:mc + 1], scale=1.0))
                    aop(lambda s, mc=mc: s.activation(
                        out=sc_a[:, mc * TTW:(mc + 1) * TTW], in_=sc_e[:],
                        func=AF.Ln, bias=1.0, scale=1.0))
                else:
                    aop(lambda s, mc=mc: s.activation(
                        out=sc_b[:, (mc - DC) * TTW:(mc - DC + 1) * TTW], in_=ps_a[:],
                        func=AF.Identity, bias=bxp[:, mc:mc + 1], scale=1.0))
            if tp_dl is not None:
                wt("sync", "ac")
                for c in range(DC):
                    dma("sync", "st", tp_dl[:, c * T + t0:c * T + t0 + TTW],
                        sc_a[:, c * TTW:(c + 1) * TTW])

            wt("vector", "ac")
            for c in range(DC):
                DL = sc_a[:, c * TTW:(c + 1) * TTW]
                BGV = sc_b[:, c * TTW:(c + 1) * TTW]
                vop(lambda v, c=c, DL=DL: v.tensor_scalar(
                    out=sc_e[:], in0=DL,
                    scalar1=coef[:, (KTAY - 1) * DC + c:(KTAY - 1) * DC + c + 1],
                    scalar2=None, op0=ALU.mult))
                for k in range(KTAY - 1, 0, -1):
                    vop(lambda v, c=c, k=k, DL=DL: v.scalar_tensor_tensor(
                        out=sc_e[:], in0=sc_e[:],
                        scalar=coef[:, (k - 1) * DC + c:(k - 1) * DC + c + 1],
                        in1=DL, op0=ALU.add, op1=ALU.mult))
                if tp_gate is not None:
                    vop(lambda v: v.tensor_scalar(out=sqs[:, 0:TTW], in0=sc_e[:],
                                                  scalar1=16.0, scalar2=None, op0=ALU.add))
                    wt("sync", "ve")
                    dma("sync", "st", tp_gate[:, c * T + t0:c * T + t0 + TTW], sqs[:, 0:TTW])
                vop(lambda v, c=c, BGV=BGV: v.tensor_tensor(out=BGV, in0=BGV, in1=XNT(c), op=ALU.mult))
                vop(lambda v, c=c, BGV=BGV: v.tensor_tensor_scan(
                    out=BGV, data0=ones512[:],
                    data1=BGV, initial=carry[:, c:c + 1], op0=ALU.mult, op1=ALU.add))
                vspace()
                vop(lambda v, c=c, BGV=BGV: v.tensor_copy(out=carry[:, c:c + 1],
                                                          in_=BGV[:, TTW - 1:TTW]))
                if tp_h is not None:
                    wt("sync", "ve")
                    dma("sync", "st", tp_h[:, c * T + t0:c * T + t0 + TTW], BGV)
                vop(lambda v, c=c, BGV=BGV: v.scalar_tensor_tensor(
                    out=sc_e[:], in0=sc_e[:], scalar=16.0, in1=BGV,
                    op0=ALU.add, op1=ALU.mult))
                vop(lambda v, c=c: v.scalar_tensor_tensor(
                    out=sc_e[:], in0=XNT(c), scalar=vecs[:, c:c + 1], in1=sc_e[:],
                    op0=ALU.mult, op1=ALU.add))
                vop(lambda v, c=c: v.tensor_copy(out=mixb[:, c * TTW:(c + 1) * TTW], in_=sc_e[:]))

            wt("tensor", "ve")
            for mc in range(DC):
                wbuf = wsb0 if mc % 2 == 0 else wsb1
                wt("sync", "mm")
                sdma(wbuf[:, 0:DC * 128],
                     d_wob.ap().rearrange("p (k e) -> p k e", k=DC)[:, :, mc * 128:(mc + 1) * 128])
                wt("tensor", "ld")
                wt("tensor", "ve", sems["ve"][1])
                for kc in range(DC):
                    mmop(lambda t, mc=mc, kc=kc, wbuf=wbuf: t.matmul(
                        ps_b[:], wbuf[:, kc * 128:(kc + 1) * 128],
                        mixb[:, kc * TTW:(kc + 1) * TTW],
                        start=(kc == 0), stop=(kc == DC - 1)))
                wt("vector", "mm")
                vop(lambda v, mc=mc: v.tensor_scalar(
                    out=AOT(mc), in0=ps_b[:],
                    scalar1=vecs[:, 5 * DC + mc:5 * DC + mc + 1], scalar2=None, op0=ALU.add))
                vop(lambda v, mc=mc: v.tensor_tensor(out=sc_e[:], in0=X0T(mc), in1=bc_u[:], op=ALU.mult))
                vop(lambda v, mc=mc: v.tensor_tensor(out=sc_e[:], in0=sc_e[:], in1=bc_v[:], op=ALU.add))
                vop(lambda v, mc=mc: v.tensor_tensor(out=AOT(mc), in0=AOT(mc), in1=sc_e[:], op=ALU.add))
            if tp_ao is not None:
                wt("sync", "ve")
                for c in range(DC):
                    dma("sync", "st", tp_ao[:, c * T + t0:c * T + t0 + TTW], AOT(c))

            wt("tensor", "ve")
            for c in range(DC):
                mmop(lambda t, c=c: t.matmul(ps_s[:], onesc[:, 0:1], AOT(c),
                                             start=(c == 0), stop=(c == DC - 1)))
            wt("vector", "mm")
            vop(lambda v: v.tensor_copy(out=RQ[0:1, 0:TTW], in_=ps_s[0:1, :]))
            for c in range(DC):
                wt("scalar", "ve")
                wt("scalar", "mm")
                aop(lambda s, c=c: s.activation(out=sqs[:, 0:TTW], in_=AOT(c), func=AF.Square))
                wt("tensor", "ac")
                mmop(lambda t, c=c: t.matmul(ps_s[:], onesc[:, 0:1], sqs[:, 0:TTW],
                                             start=(c == 0), stop=(c == DC - 1)))
            wt("vector", "mm")
            vop(lambda v: v.tensor_copy(out=RQ[0:1, 512:512 + TTW], in_=ps_s[0:1, :]))
            S1s = RQ[0:1, 0:TTW]
            S2s = RQ[0:1, 512:512 + TTW]
            vop(lambda v: v.tensor_scalar(out=S1s, in0=S1s, scalar1=cin[:1, C_DINV:C_DINV + 1],
                                          scalar2=None, op0=ALU.mult))
            vop(lambda v: v.tensor_scalar(out=S2s, in0=S2s, scalar1=cin[:1, C_DINV:C_DINV + 1],
                                          scalar2=None, op0=ALU.mult))
            vop(lambda v: v.tensor_tensor(out=RQ[0:1, 1024:1024 + TTW],
                                          in0=S1s, in1=S1s, op=ALU.mult))
            vop(lambda v: v.tensor_tensor(out=S2s, in0=S2s, in1=RQ[0:1, 1024:1024 + TTW], op=ALU.subtract))
            vop(lambda v: v.tensor_scalar(out=RQ[0:1, 1024:1024 + TTW], in0=S2s,
                                          scalar1=EPS, scalar2=None, op0=ALU.add))
            vop(lambda v: v.reciprocal(out=S2s, in_=RQ[0:1, 1024:1024 + TTW]))
            wt("scalar", "ve")
            aop(lambda s: s.activation(out=S2s, in_=S2s, func=AF.Sqrt))
            wt("vector", "ac")
            vop(lambda v: v.tensor_tensor(out=S1s, in0=S1s, in1=S2s, op=ALU.mult))
            vop(lambda v: v.tensor_scalar(out=S1s, in0=S1s, scalar1=-1.0, scalar2=None, op0=ALU.mult))
            bcast_row_to_tile(S2s, TTW, bc_u[:])
            bcast_row_to_tile(S1s, TTW, bc_v[:])
            for c in range(DC):
                vop(lambda v, c=c: v.tensor_tensor(out=AOT(c), in0=AOT(c), in1=bc_u[:], op=ALU.mult))
                vop(lambda v, c=c: v.tensor_tensor(out=AOT(c), in0=AOT(c), in1=bc_v[:], op=ALU.add))
                vop(lambda v, c=c: v.tensor_tensor(out=AOT(c), in0=AOT(c), in1=X0T(c), op=ALU.add))
            wt("sync", "ve")
            sdma(d_x1.ap().rearrange("p (c t) -> p c t", c=DC)[:, :, t0:t0 + TTW],
                 big0[:, DC * TTW:2 * DC * TTW].rearrange("p (c t) -> p c t", c=DC))
            if tp_x1 is not None:
                for c in range(DC):
                    dma("sync", "st", tp_x1[:, c * T + t0:c * T + t0 + TTW], AOT(c))

        tp_fh = tap("fh", [128, DC * T])
        for tt in range(NTT):
            t0 = tt * TTW
            wt("sync", "mm")
            wt("sync", "ac")
            sdma(big0[:, 0:DC * TTW], d_x1.ap().rearrange("p (c t) -> p c t", c=DC)[:, :, t0:t0 + TTW])
            wt("tensor", "ld")
            wt("vector", "ld")
            for c in range(DC):
                mmop(lambda t, c=c: t.matmul(ps_s[:], onesc[:, 0:1],
                                             big0[:, c * TTW:(c + 1) * TTW],
                                             start=(c == 0), stop=(c == DC - 1)))
            wt("vector", "mm")
            vop(lambda v, t0=t0: v.tensor_copy(out=R_S1[0:1, t0:t0 + TTW], in_=ps_s[0:1, :]))
            for c in range(DC):
                wt("scalar", "ve")
                wt("scalar", "mm")
                aop(lambda s, c=c: s.activation(out=sqs[:, 0:TTW],
                                                in_=big0[:, c * TTW:(c + 1) * TTW], func=AF.Square))
                wt("tensor", "ac")
                mmop(lambda t, c=c: t.matmul(ps_s[:], onesc[:, 0:1], sqs[:, 0:TTW],
                                             start=(c == 0), stop=(c == DC - 1)))
            wt("vector", "mm")
            vop(lambda v, t0=t0: v.tensor_copy(out=R_S2[0:1, t0:t0 + TTW], in_=ps_s[0:1, :]))
        ln_rows()

        for tt in range(NTT):
            t0 = tt * TTW
            bcast_row_to_tile(R_SS[0:1, t0:t0 + TTW], TTW, bc_s[:])
            bcast_row_to_tile(R_TT[0:1, t0:t0 + TTW], TTW, bc_t[:])
            wt("sync", "ve")
            sdma(big0[:, 0:DC * TTW], d_x1.ap().rearrange("p (c t) -> p c t", c=DC)[:, :, t0:t0 + TTW])
            wt("vector", "ld")
            for c in range(DC):
                FH = big1[:, c * T + t0: c * T + t0 + TTW]
                vop(lambda v, c=c, FH=FH: v.tensor_tensor(
                    out=FH, in0=big0[:, c * TTW:(c + 1) * TTW], in1=bc_s[:], op=ALU.mult))
                vop(lambda v, c=c, FH=FH: v.tensor_tensor(out=FH, in0=FH, in1=bc_t[:], op=ALU.add))
                vop(lambda v, c=c, FH=FH: v.tensor_scalar(
                    out=FH, in0=FH, scalar1=vecs[:, 3 * DC + c:3 * DC + c + 1],
                    scalar2=vecs[:, 4 * DC + c:4 * DC + c + 1], op0=ALU.mult, op1=ALU.add))
        if tp_fh is not None:
            wt("sync", "ve")
            dma("sync", "st", tp_fh[:, 0:DC * T], big1[:])

        stream_ve = [0, 0]

        def stream_piece(i2, dap):
            h = i2 % 2
            buf = wfp[:, h * 1024:(h + 1) * 1024]
            wt("sync", "ve", stream_ve[h])
            sdma(buf, dap)
            wt("vector", "ld")
            return h, buf

        def stream_done(h):
            stream_ve[h] = sems["ve"][1]

        def stream_begin():
            stream_ve[0] = stream_ve[1] = sems["ve"][1]

        def count_pass(data_slices, thr_scl_col, acc_cols, signed=False, stream=()):
            j = 0
            for ap_, w_ in data_slices:
                vop(lambda v, ap_=ap_, w_=w_, j=j: v.tensor_scalar(
                    out=big_scr(w_), in0=ap_, scalar1=scl[:, thr_scl_col:thr_scl_col + 1],
                    scalar2=None, op0=ALU.is_gt, op1=ALU.add,
                    accum_out=acc[:, acc_cols[0] + j:acc_cols[0] + j + 1]))
                j += 1
                if signed:
                    vop(lambda v, ap_=ap_, w_=w_, j=j: v.tensor_scalar(
                        out=big_scr(w_), in0=ap_, scalar1=scl[:, SCL_NEG:SCL_NEG + 1],
                        scalar2=None, op0=ALU.is_lt, op1=ALU.add,
                        accum_out=acc[:, acc_cols[0] + j:acc_cols[0] + j + 1]))
                    j += 1
            if stream:
                stream_begin()
                for i2, dap in enumerate(stream):
                    h, buf = stream_piece(i2, dap)
                    vop(lambda v, buf=buf: v.scalar_tensor_tensor(
                        out=sqs[:, 0:1024], in0=buf, scalar=-1.0, in1=buf,
                        op0=ALU.mult, op1=ALU.max))
                    vop(lambda v, j=j, tc=thr_scl_col: v.tensor_scalar(
                        out=sqs[:, 0:1024], in0=sqs[:, 0:1024],
                        scalar1=scl[:, tc:tc + 1], scalar2=None,
                        op0=ALU.is_gt, op1=ALU.add,
                        accum_out=acc[:, acc_cols[0] + j:acc_cols[0] + j + 1]))
                    stream_done(h)
                    j += 1
            vspace()
            vop(lambda v, j=j: v.tensor_reduce(
                out=acc[:, acc_cols[0] + j:acc_cols[0] + j + 1],
                in_=acc[:, acc_cols[0]:acc_cols[0] + j], axis=AX, op=ALU.add))
            vspace()
            return acc[:, acc_cols[0] + j:acc_cols[0] + j + 1]


        def big_scr(w_):
            return wfp[:, 0:w_]

        def global_count(local_col_ap, dst_row):
            wt("tensor", "ve")
            wt("tensor", "ac")
            mmop(lambda t: t.matmul(ps_c[:], onesc[:], local_col_ap, start=True, stop=True))
            wt("scalar", "mm")
            aop(lambda s: s.activation(out=nrw[:, 0:1], in_=ps_c[:], func=AF.Identity))
            apad()
            wt("sync", "ac")
            sdma(gat[0:1, 16:17], nrw[0:1, 0:1])
            allgather(gat[0:1, 16:17], 1, gat[0:1, 8:8 + n_cores])
            vop(lambda v: v.tensor_reduce(out=dst_row, in_=gat[0:1, 8:8 + n_cores], axis=AX, op=ALU.add))
            vspace()

        def quantile(data_slices, samp_slices, n_total, samp_target,
                     nt_col, ntm1_col, frac_col, amax_col, signed=False, stream=(),
                     tag=""):
            stream = list(stream)


            """Exact global quantile of the |values| in data_slices.

            nt_col/ntm1_col/frac_col: scl columns that already hold nt, nt-1
            (per-partition) and frac; result -> scl[amax_col] (and row QV_RES).
            Target-rank values must already be in rows R_NT etc (see caller).
            """
            nsl = len(data_slices)
            for si, (ap_, w_) in enumerate(data_slices):
                if signed:
                    vop(lambda v, ap_=ap_, w_=w_: v.scalar_tensor_tensor(
                        out=big_scr(w_), in0=ap_, scalar=-1.0, in1=ap_,
                        op0=ALU.mult, op1=ALU.max))
                    vop(lambda v, w_=w_, si=si: v.tensor_reduce(
                        out=acc[:, 40 + si:41 + si], in_=big_scr(w_), axis=AX, op=ALU.max))
                else:
                    vop(lambda v, ap_=ap_, si=si: v.tensor_reduce(out=acc[:, 40 + si:41 + si],
                                                                  in_=ap_, axis=AX, op=ALU.max))
            nred = nsl
            if stream:
                stream_begin()
                vop(lambda v: v.memset(bc_s[:], 0.0))
                vop(lambda v: v.memset(bc_t[:], 0.0))
                for i2, dap in enumerate(stream):
                    h, buf = stream_piece(i2, dap)
                    vop(lambda v, buf=buf: v.tensor_tensor(
                        out=bc_s[:], in0=bc_s[:], in1=buf[:, 0:512], op=ALU.max))
                    vop(lambda v, buf=buf: v.scalar_tensor_tensor(
                        out=bc_s[:], in0=buf[:, 0:512], scalar=-1.0, in1=bc_s[:],
                        op0=ALU.mult, op1=ALU.max))
                    vop(lambda v, buf=buf: v.tensor_tensor(
                        out=bc_t[:], in0=bc_t[:], in1=buf[:, 512:1024], op=ALU.max))
                    vop(lambda v, buf=buf: v.scalar_tensor_tensor(
                        out=bc_t[:], in0=buf[:, 512:1024], scalar=-1.0, in1=bc_t[:],
                        op0=ALU.mult, op1=ALU.max))
                    stream_done(h)
                vop(lambda v, nsl=nsl: v.tensor_reduce(
                    out=acc[:, 40 + nsl:41 + nsl], in_=bc_s[:], axis=AX, op=ALU.max))
                vop(lambda v, nsl=nsl: v.tensor_reduce(
                    out=acc[:, 41 + nsl:42 + nsl], in_=bc_t[:], axis=AX, op=ALU.max))
                nred = nsl + 2
            vspace()
            vop(lambda v, nred=nred: v.tensor_reduce(out=acc[:, 40 + nred:41 + nred],
                                                     in_=acc[:, 40:40 + nred], axis=AX, op=ALU.max))
            vspace()
            wt("sync", "ve")
            sdma(gat[0:1, 128:256], acc[:, 40 + nred:41 + nred])
            wt("vector", "ld")
            vop(lambda v: v.tensor_reduce(out=gat[0:1, 4:5], in_=gat[0:1, 128:256], axis=AX, op=ALU.max))
            vspace()
            allgather(gat[0:1, 4:5], 1, gat[0:1, 8:8 + n_cores])
            vop(lambda v: v.tensor_reduce(out=gat[0:1, 5:6], in_=gat[0:1, 8:8 + n_cores], axis=AX, op=ALU.max))
            vspace()
            vop(lambda v: v.tensor_scalar(out=Q(QV_HI), in0=RQ[0:1, 0:RW], scalar1=0.0,
                                          scalar2=gat[0:1, 5:6], op0=ALU.mult, op1=ALU.add))
            vop(lambda v: v.tensor_scalar(out=Q(QV_LO), in0=Q(QV_HI), scalar1=0.0,
                                          scalar2=None, op0=ALU.mult))
            tp_qg = tap("qgat" + tag, [1, 256]) if tag else None
            if tp_qg is not None:
                wt("sync", "ve")
                dma("sync", "st", tp_qg[:], gat[:])
                tp_qa = tap("qacc" + tag, [128, 12])
                dma("sync", "st", tp_qa[:], acc[:, 40:52])
                wt("vector", "st")

            LOr, HIr, MIDr, CNTr, GEr = Q(QV_LO), Q(QV_HI), Q(QV_MID), Q(QV_CNT), Q(QV_GE)
            T1r, T2r, TGTr, CLOr, CHIr = Q(QV_T1), Q(QV_T2), Q(QV_TGT), Q(QV_CLO), Q(QV_CHI)
            vop(lambda v: v.tensor_scalar(out=TGTr, in0=LOr, scalar1=0.0,
                                          scalar2=scl[0:1, nt_col:nt_col + 1],
                                          op0=ALU.mult, op1=ALU.add))
            vop(lambda v: v.tensor_scalar(out=TGTr, in0=TGTr, scalar1=-0.5, scalar2=None,
                                          op0=ALU.add))
            vop(lambda v, n_=float(n_total): v.tensor_scalar(
                out=CLOr, in0=LOr, scalar1=0.0, scalar2=n_, op0=ALU.mult, op1=ALU.add))
            vop(lambda v: v.tensor_scalar(out=CHIr, in0=LOr, scalar1=0.0, scalar2=None,
                                          op0=ALU.mult))
            for it in range(B_ILL):
                vop(lambda v: v.tensor_tensor(out=T1r, in0=CLOr, in1=CHIr, op=ALU.subtract))
                vop(lambda v: v.reciprocal(out=T2r, in_=T1r))
                vop(lambda v: v.tensor_tensor(out=T1r, in0=CLOr, in1=TGTr, op=ALU.subtract))
                vop(lambda v: v.tensor_tensor(out=T1r, in0=T1r, in1=T2r, op=ALU.mult))
                vop(lambda v: v.tensor_scalar(out=T1r, in0=T1r, scalar1=0.02, scalar2=0.98,
                                              op0=ALU.max, op1=ALU.min))
                vop(lambda v: v.tensor_tensor(out=T2r, in0=HIr, in1=LOr, op=ALU.subtract))
                vop(lambda v: v.tensor_tensor(out=T2r, in0=T2r, in1=T1r, op=ALU.mult))
                vop(lambda v: v.tensor_tensor(out=MIDr, in0=LOr, in1=T2r, op=ALU.add))
                spread_scl(Q1(QV_MID), SCL_MID)
                c_ap = count_pass(data_slices, SCL_MID, (0,), signed=signed, stream=stream)
                global_count(c_ap, gat[0:1, 60:61])
                vop(lambda v: v.tensor_scalar(out=CNTr, in0=CNTr, scalar1=0.0,
                                              scalar2=gat[0:1, 60:61], op0=ALU.mult, op1=ALU.add))
                vop(lambda v: v.tensor_scalar(out=GEr, in0=CNTr,
                                              scalar1=scl[0:1, nt_col:nt_col + 1], scalar2=None,
                                              op0=ALU.is_ge))
                vop(lambda v: v.tensor_scalar(out=T1r, in0=GEr, scalar1=-1.0, scalar2=1.0,
                                              op0=ALU.mult, op1=ALU.add))
                vop(lambda v: v.tensor_tensor(out=T2r, in0=GEr, in1=MIDr, op=ALU.mult))
                vop(lambda v: v.tensor_tensor(out=LOr, in0=LOr, in1=T1r, op=ALU.mult))
                vop(lambda v: v.tensor_tensor(out=LOr, in0=LOr, in1=T2r, op=ALU.add))
                vop(lambda v: v.tensor_tensor(out=T2r, in0=T1r, in1=MIDr, op=ALU.mult))
                vop(lambda v: v.tensor_tensor(out=HIr, in0=HIr, in1=GEr, op=ALU.mult))
                vop(lambda v: v.tensor_tensor(out=HIr, in0=HIr, in1=T2r, op=ALU.add))
                vop(lambda v: v.tensor_tensor(out=T2r, in0=CHIr, in1=TGTr, op=ALU.add))
                vop(lambda v: v.tensor_scalar(out=T2r, in0=T2r, scalar1=0.5, scalar2=None,
                                              op0=ALU.mult))
                vop(lambda v: v.tensor_tensor(out=T2r, in0=T2r, in1=GEr, op=ALU.mult))
                vop(lambda v: v.tensor_tensor(out=CHIr, in0=CNTr, in1=T1r, op=ALU.mult))
                vop(lambda v: v.tensor_tensor(out=CHIr, in0=CHIr, in1=T2r, op=ALU.add))
                vop(lambda v: v.tensor_tensor(out=T2r, in0=CLOr, in1=TGTr, op=ALU.add))
                vop(lambda v: v.tensor_scalar(out=T2r, in0=T2r, scalar1=0.5, scalar2=None,
                                              op0=ALU.mult))
                vop(lambda v: v.tensor_tensor(out=T2r, in0=T2r, in1=T1r, op=ALU.mult))
                vop(lambda v: v.tensor_tensor(out=CLOr, in0=CNTr, in1=GEr, op=ALU.mult))
                vop(lambda v: v.tensor_tensor(out=CLOr, in0=CLOr, in1=T2r, op=ALU.add))

            def masked_min(thr_col, dst_gat_col):
                j2 = 0
                for ap_, w_ in data_slices:
                    vop(lambda v, ap_=ap_, w_=w_: v.tensor_scalar(
                        out=big_scr(w_), in0=ap_, scalar1=scl[:, thr_col:thr_col + 1],
                        scalar2=BIG, op0=ALU.is_le, op1=ALU.mult))
                    vop(lambda v, ap_=ap_, w_=w_: v.tensor_tensor(
                        out=big_scr(w_), in0=big_scr(w_), in1=ap_, op=ALU.add))
                    vop(lambda v, w_=w_, j2=j2: v.tensor_reduce(out=acc[:, 0 + j2:1 + j2],
                                                                in_=big_scr(w_), axis=AX, op=ALU.min))
                    j2 += 1
                    if signed:
                        vop(lambda v, ap_=ap_, w_=w_: v.tensor_scalar(
                            out=big_scr(w_), in0=ap_, scalar1=scl[:, SCL_NEG:SCL_NEG + 1],
                            scalar2=BIG, op0=ALU.is_ge, op1=ALU.mult))
                        vop(lambda v, ap_=ap_, w_=w_: v.scalar_tensor_tensor(
                            out=big_scr(w_), in0=ap_, scalar=-1.0, in1=big_scr(w_),
                            op0=ALU.mult, op1=ALU.add))
                        vop(lambda v, w_=w_, j2=j2: v.tensor_reduce(out=acc[:, 0 + j2:1 + j2],
                                                                    in_=big_scr(w_), axis=AX, op=ALU.min))
                        j2 += 1
                if stream:
                    stream_begin()
                    for i2, dap in enumerate(stream):
                        h, buf = stream_piece(i2, dap)
                        vop(lambda v, buf=buf: v.scalar_tensor_tensor(
                            out=bc_s[:], in0=buf[:, 0:512], scalar=-1.0, in1=buf[:, 0:512],
                            op0=ALU.mult, op1=ALU.max))
                        vop(lambda v, tc=thr_col: v.tensor_scalar(
                            out=bc_t[:], in0=bc_s[:], scalar1=scl[:, tc:tc + 1],
                            scalar2=BIG, op0=ALU.is_le, op1=ALU.mult))
                        vop(lambda v: v.tensor_tensor(
                            out=bc_t[:], in0=bc_t[:], in1=bc_s[:], op=ALU.max))
                        vop(lambda v, buf=buf: v.scalar_tensor_tensor(
                            out=bc_s[:], in0=buf[:, 512:1024], scalar=-1.0, in1=buf[:, 512:1024],
                            op0=ALU.mult, op1=ALU.max))
                        vop(lambda v, tc=thr_col: v.tensor_scalar(
                            out=sqs[:, 0:512], in0=bc_s[:], scalar1=scl[:, tc:tc + 1],
                            scalar2=BIG, op0=ALU.is_le, op1=ALU.mult))
                        vop(lambda v: v.tensor_tensor(
                            out=sqs[:, 0:512], in0=sqs[:, 0:512], in1=bc_s[:], op=ALU.max))
                        vop(lambda v: v.tensor_tensor(
                            out=bc_t[:], in0=bc_t[:], in1=sqs[:, 0:512], op=ALU.min))
                        vop(lambda v, j2=j2: v.tensor_reduce(
                            out=acc[:, 0 + j2:1 + j2], in_=bc_t[:], axis=AX, op=ALU.min))
                        stream_done(h)
                        j2 += 1
                vspace()
                vop(lambda v, j2=j2: v.tensor_reduce(out=acc[:, 0 + j2:1 + j2],
                                                     in_=acc[:, 0:0 + j2], axis=AX, op=ALU.min))
                vspace()
                wt("sync", "ve")
                sdma(gat[0:1, 128:256], acc[:, 0 + j2:1 + j2])
                wt("vector", "ld")
                vop(lambda v: v.tensor_reduce(out=gat[0:1, 20:21], in_=gat[0:1, 128:256],
                                              axis=AX, op=ALU.min))
                vspace()
                allgather(gat[0:1, 20:21], 1, gat[0:1, 8:8 + n_cores])
                vop(lambda v, dc_=dst_gat_col: v.tensor_reduce(
                    out=gat[0:1, dc_:dc_ + 1], in_=gat[0:1, 8:8 + n_cores], axis=AX, op=ALU.min))
                vspace()

            spread_scl(Q1(QV_LO), SCL_MID)
            vspace()
            masked_min(SCL_MID, 21)
            vspace()
            spread_scl(gat[0:1, 21:22], SCL_M1)
            vspace()
            spread_scl(gat[0:1, 21:22], SCL_M1)
            c_ap = count_pass(data_slices, SCL_M1, (0,), signed=signed, stream=stream)
            global_count(c_ap, gat[0:1, 22:23])
            vspace()
            masked_min(SCL_M1, 23)
            vspace()
            nvop(lambda v: v.tensor_scalar(out=gat[0:1, 24:25], in0=gat[0:1, 22:23],
                                          scalar1=scl[0:1, ntm1_col:ntm1_col + 1], scalar2=None,
                                          op0=ALU.is_ge))
            nvop(lambda v: v.tensor_tensor(out=gat[0:1, 25:26], in0=gat[0:1, 24:25],
                                          in1=gat[0:1, 23:24], op=ALU.mult))
            nvop(lambda v: v.tensor_scalar(out=gat[0:1, 26:27], in0=gat[0:1, 24:25],
                                          scalar1=-1.0, scalar2=1.0, op0=ALU.mult, op1=ALU.add))
            nvop(lambda v: v.tensor_tensor(out=gat[0:1, 26:27], in0=gat[0:1, 26:27],
                                          in1=gat[0:1, 21:22], op=ALU.mult))
            nvop(lambda v: v.tensor_tensor(out=gat[0:1, 25:26], in0=gat[0:1, 25:26],
                                          in1=gat[0:1, 26:27], op=ALU.add))
            nvop(lambda v: v.tensor_scalar(out=gat[0:1, 27:28], in0=scl[0:1, frac_col:frac_col + 1],
                                          scalar1=-1.0, scalar2=1.0, op0=ALU.mult, op1=ALU.add))
            nvop(lambda v: v.tensor_tensor(out=gat[0:1, 27:28], in0=gat[0:1, 27:28],
                                          in1=gat[0:1, 21:22], op=ALU.mult))
            nvop(lambda v: v.tensor_scalar(out=gat[0:1, 25:26], in0=gat[0:1, 25:26],
                                          scalar1=scl[0:1, frac_col:frac_col + 1], scalar2=None,
                                          op0=ALU.mult))
            nvop(lambda v: v.tensor_tensor(out=gat[0:1, 28:29], in0=gat[0:1, 27:28],
                                          in1=gat[0:1, 25:26], op=ALU.add))
            nvop(lambda v: v.tensor_scalar(out=gat[0:1, 28:29], in0=gat[0:1, 28:29],
                                          scalar1=cin[0:1, C_EPSC:C_EPSC + 1], scalar2=None,
                                          op0=ALU.max))
            spread_scl(gat[0:1, 28:29], amax_col)

        i32w = sb("i32w", [1, 512], I32)

        def index_setup(n_elems, i_col, frac_col, nt_col, ntm1_col, nconst_col):
            RA = RQ[0:1, 0 * RW:1 * RW]
            RB = RQ[0:1, 1 * RW:2 * RW]
            RC = RQ[0:1, 2 * RW:3 * RW]
            RD2 = RQ[0:1, 3 * RW:4 * RW]
            vop(lambda v: v.tensor_scalar(out=RA, in0=RQ[0:1, 4 * RW:5 * RW], scalar1=0.0,
                                          scalar2=cin[0:1, C_200:C_200 + 1],
                                          op0=ALU.mult, op1=ALU.add))
            vop(lambda v: v.reciprocal(out=RB, in_=RA))
            vop(lambda v: v.tensor_scalar(out=RA, in0=RA, scalar1=0.0,
                                          scalar2=cin[0:1, C_ONE:C_ONE + 1],
                                          op0=ALU.mult, op1=ALU.add))
            vop(lambda v: v.tensor_tensor(out=RB, in0=RA, in1=RB, op=ALU.subtract))
            vop(lambda v, ncol=nconst_col: v.tensor_scalar(
                out=RC, in0=RA, scalar1=0.0, scalar2=cin[0:1, ncol:ncol + 1],
                op0=ALU.mult, op1=ALU.add))
            vop(lambda v: v.tensor_scalar(out=RD2, in0=RC, scalar1=-1.0, scalar2=None,
                                          op0=ALU.add))
            vop(lambda v: v.tensor_tensor(out=RB, in0=RB, in1=RD2, op=ALU.mult))
            vop(lambda v: v.tensor_copy(out=i32w[:], in_=RB))
            vop(lambda v: v.tensor_copy(out=RA, in_=i32w[:]))
            vop(lambda v: v.tensor_tensor(out=RD2, in0=RB, in1=RA, op=ALU.subtract))
            vop(lambda v: v.tensor_tensor(out=RC, in0=RC, in1=RA, op=ALU.subtract))
            spread_scl(RA[0:1, 0:1], i_col)
            spread_scl(RD2[0:1, 0:1], frac_col)
            spread_scl(RC[0:1, 0:1], nt_col)
            vspace()
            nvop(lambda v, ntc=nt_col, nmc=ntm1_col: v.tensor_scalar(
                out=scl[:, nmc:nmc + 1], in0=scl[:, ntc:ntc + 1], scalar1=-1.0, scalar2=None,
                op0=ALU.add))

        index_setup(N1, SCL_I, SCL_FRAC, SCL_NT, SCL_NTM1, C_N1)
        q1_slices = [(big1[:, c * T:(c + 1) * T], T) for c in range(DC)]
        q1_samp = [(big1[:, 0:DC * T:8], DC * T // 8)]
        samp_target1 = int(round((1.0 - 0.995) * (128 * DC * T // 8)))
        quantile(q1_slices, q1_samp, N1, samp_target1, SCL_NT, SCL_NTM1, SCL_FRAC, SCL_AMAX,
                 signed=True, tag="1")
        tp_am1 = tap("amax1", [1, 4])
        if tp_am1 is not None:
            wt("sync", "ve")
            dma("sync", "st", tp_am1[0:1, 0:1], gat[0:1, 28:29])
            dma("sync", "st", tp_am1[0:1, 1:2], scl[0:1, SCL_I:SCL_I + 1])
            dma("sync", "st", tp_am1[0:1, 2:3], scl[0:1, SCL_FRAC:SCL_FRAC + 1])
            dma("sync", "st", tp_am1[0:1, 3:4], scl[0:1, SCL_NT:SCL_NT + 1])
        tp_q1 = tap("q1dbg", [1, 8])
        if tp_q1 is not None:
            wt("sync", "ve")
            dma("sync", "st", tp_q1[0:1, 0:1], gat[0:1, 21:22])
            dma("sync", "st", tp_q1[0:1, 1:2], gat[0:1, 22:23])
            dma("sync", "st", tp_q1[0:1, 2:3], gat[0:1, 23:24])
            dma("sync", "st", tp_q1[0:1, 3:4], RQ[0:1, QV_LO * RW:QV_LO * RW + 1])
            dma("sync", "st", tp_q1[0:1, 4:5], RQ[0:1, QV_HI * RW:QV_HI * RW + 1])
            dma("sync", "st", tp_q1[0:1, 5:6], gat[0:1, 48:49])
            dma("sync", "st", tp_q1[0:1, 6:7], gat[0:1, 49:50])
            dma("sync", "st", tp_q1[0:1, 7:8], gat[0:1, 60:61])
            wt("vector", "st")

        vspace()
        nvop(lambda v: v.reciprocal(out=nrw[:, 3:4], in_=scl[:, SCL_AMAX:SCL_AMAX + 1]))
        nvop(lambda v: v.tensor_scalar(out=scl[:, SCL_S:SCL_S + 1], in0=nrw[:, 3:4],
                                       scalar1=cin[:, C_S127:C_S127 + 1], scalar2=None, op0=ALU.mult))
        nvop(lambda v: v.reciprocal(out=nrw[:, 3:4], in_=scl[:, SCL_S:SCL_S + 1]))
        nvop(lambda v: v.tensor_tensor(out=scl[:, SCL_GS:SCL_GS + 1], in0=scl[:, 12:13],
                                       in1=nrw[:, 3:4], op=ALU.mult))

        tp_fq = tap("fq", [128, DC * T])
        tp_fqd = tap("fqdbg", [128, 4 * TTW + 20])
        for c in range(DC):
            for p4 in range(4):
                if tp_fqd is not None and c == 0 and p4 == 0:
                    FH = big1[:, 0:TTW]
                    vop(lambda v, FH=FH: v.tensor_scalar(
                        out=big0[:, 0:TTW], in0=FH, scalar1=scl[:, SCL_S:SCL_S + 1],
                        scalar2=scl[:, SCL_SHIFT:SCL_SHIFT + 1], op0=ALU.mult, op1=ALU.add))
                    vop(lambda v: v.tensor_copy(out=ibuf[:, 0:TTW], in_=big0[:, 0:TTW]))
                    vop(lambda v: v.tensor_copy(out=big0[:, TTW:2 * TTW], in_=ibuf[:, 0:TTW]))
                    vop(lambda v: v.tensor_scalar(
                        out=ibuf[:, 0:TTW], in0=ibuf[:, 0:TTW], scalar1=cin[:, C_LO16:C_LO16 + 1],
                        scalar2=cin[:, C_HI16:C_HI16 + 1], op0=ALU.max, op1=ALU.min))
                    vop(lambda v: v.tensor_copy(out=big0[:, 2 * TTW:3 * TTW], in_=ibuf[:, 0:TTW]))
                    vop(lambda v: v.tensor_scalar(
                        out=big0[:, 3 * TTW:4 * TTW], in0=ibuf[:, 0:TTW],
                        scalar1=cin[:, C_16384:C_16384 + 1], scalar2=scl[:, SCL_GS:SCL_GS + 1],
                        op0=ALU.subtract, op1=ALU.mult))
                    vop(lambda v: v.tensor_copy(out=big0[:, 4 * TTW:4 * TTW + 20], in_=scl[:, 0:20]))
                    wt("sync", "ve")
                    dma("sync", "st", tp_fqd[:], big0[:, 0:4 * TTW + 20])
                    wt("vector", "st")
                FH = big1[:, c * T + p4 * TTW: c * T + (p4 + 1) * TTW]
                vop(lambda v, FH=FH: v.tensor_scalar(
                    out=big0[:, 0:TTW], in0=FH, scalar1=scl[:, SCL_S:SCL_S + 1],
                    scalar2=scl[:, SCL_SHIFT:SCL_SHIFT + 1], op0=ALU.mult, op1=ALU.add))
                vop(lambda v: v.tensor_copy(out=ibuf[:, 0:TTW], in_=big0[:, 0:TTW]))
                vop(lambda v: v.tensor_scalar(
                    out=ibuf[:, 0:TTW], in0=ibuf[:, 0:TTW], scalar1=cin[:, C_LO16:C_LO16 + 1],
                    scalar2=cin[:, C_HI16:C_HI16 + 1], op0=ALU.max, op1=ALU.min))
                vop(lambda v, c=c, p4=p4: v.tensor_scalar(
                    out=bigb[:, c * T + p4 * TTW: c * T + (p4 + 1) * TTW], in0=ibuf[:, 0:TTW],
                    scalar1=cin[:, C_16384:C_16384 + 1], scalar2=scl[:, SCL_GS:SCL_GS + 1],
                    op0=ALU.subtract, op1=ALU.mult))
        if tp_fq is not None:
            for c in range(DC):
                vop(lambda v, c=c: v.tensor_copy(out=big0[:, 0:T], in_=bigb[:, c * T:(c + 1) * T]))
                wt("sync", "ve")
                dma("sync", "st", tp_fq[:, c * T:(c + 1) * T], big0[:, 0:T])
                wt("vector", "st")

        wt("tensor", "ve")
        for tt in range(NTT):
            t0 = tt * TTW
            for mj in range(HC):
                wbuf = wsb0 if mj % 2 == 0 else wsb1
                wt("sync", "mm")
                sdma(wbuf[:, 0:DC * 128],
                     d_w1q.ap().rearrange("p (k h) -> p k h", k=DC)[:, :, mj * 128:(mj + 1) * 128])
                wt("tensor", "ld")
                wt("tensor", "ac", sems["ac"][1])
                for kc in range(DC):
                    mmop(lambda t, mj=mj, kc=kc, wbuf=wbuf, t0=t0: t.matmul(
                        ps_a[:], wbuf[:, kc * 128:(kc + 1) * 128],
                        bigb[:, kc * T + t0: kc * T + t0 + TTW],
                        start=(kc == 0), stop=(kc == DC - 1)))
                wt("scalar", "mm")
                wt("scalar", "ld")
                aop(lambda s, mj=mj: s.activation(out=sqs[:, 0:TTW], in_=ps_a[:], func=AF.Gelu))
                wt("sync", "ac")
                dma("sync", "ld", d_h2.ap()[:, mj * T + t0: mj * T + t0 + TTW], sqs[:, 0:TTW])

        index_setup(N2, SCL_I, SCL_FRAC, SCL_NT, SCL_NTM1, C_N2)
        wt("sync", "ve")
        wt("sync", "ld")
        for half, dst in ((0, big0), (1, big1)):
            for ch in range(8):
                src = d_h2.ap()[:, (half * 8 + ch) * T:(half * 8 + ch + 1) * T]
                sdma(dst[:, ch * T:(ch + 1) * T], src)
        wt("vector", "ld")
        for half, dst in ((0, big0), (1, big1)):
            for ch in range(8):
                CH = dst[:, ch * T:(ch + 1) * T]
                vop(lambda v, CH=CH: v.scalar_tensor_tensor(
                    out=CH, in0=CH, scalar=-1.0, in1=CH, op0=ALU.mult, op1=ALU.max))
        q2_slices = [(big0[:, c * T:(c + 1) * T], T) for c in range(DC)] + \
                    [(big1[:, c * T:(c + 1) * T], T) for c in range(DC)]
        q2_stream = [d_h2.ap()[:, ch * T + p * 1024: ch * T + (p + 1) * 1024]
                     for ch in range(16, 32) for p in range(2)]
        q2_samp = [(big0[:, 0:DC * T:16], DC * T // 16), (big1[:, 0:DC * T:16], DC * T // 16)]
        samp_target2 = int(round((1.0 - 0.995) * (128 * 2 * (DC * T // 16))))
        quantile(q2_slices, q2_samp, N2, samp_target2, SCL_NT, SCL_NTM1, SCL_FRAC, SCL_AMAX,
                 signed=False, stream=q2_stream)
        tp_am2 = tap("amax2", [1, 4])
        if tp_am2 is not None:
            wt("sync", "ve")
            dma("sync", "st", tp_am2[0:1, 0:1], gat[0:1, 28:29])
            dma("sync", "st", tp_am2[0:1, 1:2], scl[0:1, SCL_I:SCL_I + 1])
            dma("sync", "st", tp_am2[0:1, 2:3], scl[0:1, SCL_FRAC:SCL_FRAC + 1])
            dma("sync", "st", tp_am2[0:1, 3:4], scl[0:1, SCL_NT:SCL_NT + 1])

        vspace()
        nvop(lambda v: v.reciprocal(out=nrw[:, 3:4], in_=scl[:, SCL_AMAX:SCL_AMAX + 1]))
        nvop(lambda v: v.tensor_scalar(out=scl[:, SCL_S:SCL_S + 1], in0=nrw[:, 3:4],
                                       scalar1=cin[:, C_S127:C_S127 + 1], scalar2=None, op0=ALU.mult))
        nvop(lambda v: v.reciprocal(out=nrw[:, 3:4], in_=scl[:, SCL_S:SCL_S + 1]))
        nvop(lambda v: v.tensor_tensor(out=scl[:, SCL_GS:SCL_GS + 1], in0=scl[:, 14:15],
                                       in1=nrw[:, 3:4], op=ALU.mult))

        for tt in range(NTT):
            t0 = tt * TTW
            for hc2 in range(HC):
                wt("sync", "ve")
                sdma(wfp[:, 0:TTW], d_h2.ap()[:, hc2 * T + t0: hc2 * T + t0 + TTW])
                wt("vector", "ld")
                vop(lambda v, hc2=hc2: v.tensor_scalar(
                    out=sqs[:, 0:TTW], in0=wfp[:, 0:TTW], scalar1=scl[:, SCL_S:SCL_S + 1],
                    scalar2=scl[:, SCL_SHIFT:SCL_SHIFT + 1], op0=ALU.mult, op1=ALU.add))
                vop(lambda v, hc2=hc2: v.tensor_copy(out=ibuf[:, 0:TTW], in_=sqs[:, 0:TTW]))
                vop(lambda v, hc2=hc2: v.tensor_scalar(
                    out=ibuf[:, 0:TTW], in0=ibuf[:, 0:TTW], scalar1=cin[:, C_LO16:C_LO16 + 1],
                    scalar2=cin[:, C_HI16:C_HI16 + 1], op0=ALU.max, op1=ALU.min))
                vop(lambda v, hc2=hc2: v.tensor_scalar(
                    out=bigb[:, hc2 * TTW:(hc2 + 1) * TTW], in0=ibuf[:, 0:TTW],
                    scalar1=cin[:, C_16384:C_16384 + 1], scalar2=scl[:, SCL_GS:SCL_GS + 1],
                    op0=ALU.subtract, op1=ALU.mult))
            wt("sync", "ve")
            sdma(big0[:, 0:DC * TTW], d_x1.ap().rearrange("p (c t) -> p c t", c=DC)[:, :, t0:t0 + TTW])
            wt("tensor", "ve")
            for me in range(DC):
                for kg in range(4):
                    wbuf2 = wsb0 if (me * 4 + kg) % 2 == 0 else wsb1
                    wt("sync", "mm")
                    sdma(wbuf2[:, 0:8 * 128],
                         d_w2q.ap().rearrange("p (k e) -> p k e", k=HC)
                         [:, kg * 8:(kg + 1) * 8, me * 128:(me + 1) * 128])
                    wt("tensor", "ld")
                    wt("tensor", "ve", sems["ve"][1])
                    for kk in range(8):
                        kj = kg * 8 + kk
                        mmop(lambda t, me=me, kj=kj, kk=kk, wbuf2=wbuf2: t.matmul(
                            ps_b[:], wbuf2[:, kk * 128:(kk + 1) * 128],
                            bigb[:, kj * TTW:(kj + 1) * TTW],
                            start=(kj == 0), stop=(kj == HC - 1)))
                wt("vector", "mm")
                vop(lambda v, me=me: v.tensor_copy(
                    out=big0[:, DC * TTW + me * TTW: DC * TTW + (me + 1) * TTW], in_=ps_b[:]))
            wt("tensor", "ve")
            FOT = lambda c: big0[:, DC * TTW + c * TTW: DC * TTW + (c + 1) * TTW]
            for c in range(DC):
                mmop(lambda t, c=c: t.matmul(ps_s[:], onesc[:, 0:1], FOT(c),
                                             start=(c == 0), stop=(c == DC - 1)))
            wt("vector", "mm")
            vop(lambda v: v.tensor_copy(out=RQ[0:1, 0:TTW], in_=ps_s[0:1, :]))
            for c in range(DC):
                wt("scalar", "ve")
                wt("scalar", "mm")
                aop(lambda s, c=c: s.activation(out=sqs[:, 0:TTW],
                                                in_=FOT(c), func=AF.Square))
                wt("tensor", "ac")
                mmop(lambda t, c=c: t.matmul(ps_s[:], onesc[:, 0:1], sqs[:, 0:TTW],
                                             start=(c == 0), stop=(c == DC - 1)))
            wt("vector", "mm")
            vop(lambda v: v.tensor_copy(out=RQ[0:1, 512:512 + TTW], in_=ps_s[0:1, :]))
            S1s = RQ[0:1, 0:TTW]
            S2s = RQ[0:1, 512:512 + TTW]
            vop(lambda v: v.tensor_scalar(out=S1s, in0=S1s, scalar1=cin[:1, C_DINV:C_DINV + 1],
                                          scalar2=None, op0=ALU.mult))
            vop(lambda v: v.tensor_scalar(out=S2s, in0=S2s, scalar1=cin[:1, C_DINV:C_DINV + 1],
                                          scalar2=None, op0=ALU.mult))
            vop(lambda v: v.tensor_tensor(out=RQ[0:1, 1024:1024 + TTW], in0=S1s, in1=S1s, op=ALU.mult))
            vop(lambda v: v.tensor_tensor(out=S2s, in0=S2s, in1=RQ[0:1, 1024:1024 + TTW], op=ALU.subtract))
            vop(lambda v: v.tensor_scalar(out=RQ[0:1, 1024:1024 + TTW], in0=S2s,
                                          scalar1=EPS, scalar2=None, op0=ALU.add))
            vop(lambda v: v.reciprocal(out=S2s, in_=RQ[0:1, 1024:1024 + TTW]))
            wt("scalar", "ve")
            aop(lambda s: s.activation(out=S2s, in_=S2s, func=AF.Sqrt))
            wt("vector", "ac")
            vop(lambda v: v.tensor_tensor(out=S1s, in0=S1s, in1=S2s, op=ALU.mult))
            vop(lambda v: v.tensor_scalar(out=S1s, in0=S1s, scalar1=-1.0, scalar2=None, op0=ALU.mult))
            bcast_row_to_tile(S2s, TTW, bc_u[:])
            bcast_row_to_tile(S1s, TTW, bc_v[:])
            wt("vector", "ld")
            for c in range(DC):
                vop(lambda v, c=c: v.tensor_tensor(out=FOT(c), in0=FOT(c), in1=bc_u[:], op=ALU.mult))
                vop(lambda v, c=c: v.tensor_tensor(out=FOT(c), in0=FOT(c), in1=bc_v[:], op=ALU.add))
                vop(lambda v, c=c: v.tensor_tensor(out=FOT(c), in0=FOT(c),
                                                   in1=big0[:, c * TTW:(c + 1) * TTW], op=ALU.add))
            wt("tensor", "ve")
            tve_o = [sems["ve"][1], sems["ve"][1]]
            for sub in range(NTT):
                wt("vector", "st")
                for c in range(DC):
                    bank = ps_t[:] if c % 2 == 0 else ps_bc[:, 0:128]
                    wt("tensor", "ve", tve_o[c % 2])
                    mmop(lambda t, sub=sub, c=c, bank=bank: t.transpose(
                        bank, FOT(c)[:, sub * 128: sub * 128 + 128], ident[:]))
                    wt("vector", "mm")
                    vop(lambda v, sub=sub, c=c, bank=bank: v.tensor_copy(
                        out=wfp[:, c * 128:(c + 1) * 128], in_=bank))
                    tve_o[c % 2] = sems["ve"][1]
                wt("sync", "ve")
                dma("sync", "st", dy[t0 + sub * 128: t0 + sub * 128 + 128, :], wfp[:, 0:D])

        add("sync", lambda s: s.wait_ge(sems["st"][0], sems["st"][1]))
        add("sync", lambda s: s.wait_ge(sems["ld"][0], sems["ld"][1]))

        with nc.Block() as block:
            @block.sync
            def _(eng):
                for fn in ops["sync"]:
                    fn(eng)

            @block.vector
            def _(eng):
                for fn in ops["vector"]:
                    fn(eng)

            @block.scalar
            def _(eng):
                for fn in ops["scalar"]:
                    fn(eng)

            @block.tensor
            def _(eng):
                for fn in ops["tensor"]:
                    fn(eng)

            @block.gpsimd
            def _(eng):
                for fn in ops["gpsimd"]:
                    fn(eng)

    return nc, tap_outs


def make_consts():
    c = np.zeros(NCONST, np.float32)
    c[C_200] = 200.0
    c[C_ONE] = 1.0
    c[C_N1] = float(N1)
    c[C_N2] = float(N2)
    c[C_P23] = 8388608.0
    c[C_HALF] = 0.5
    c[C_D] = float(D)
    c[C_S127] = 127.0
    c[C_ZERO] = 0.0
    c[C_16384] = 16384.0
    c[C_LO16] = 16384.0 - 128.0
    c[C_HI16] = 16384.0 + 127.0
    c[C_RS1] = 0.0
    c[C_RS2] = 0.0
    c[C_NW1] = 1.0 / (D * H)
    c[C_NW2] = 1.0 / (D * H)
    c[C_EPSC] = EPS
    c[C_DINV] = 1.0 / D
    return np.broadcast_to(c, (128, NCONST)).copy()


def prep_inputs(L0_attn_x, L0_attn_w_xproj, L0_attn_b_xproj, L0_attn_A,
                L0_attn_D_param, L0_attn_w_out, L0_attn_b_out, L0_attn_gamma,
                L0_attn_beta, L0_ffn_ln_g, L0_ffn_ln_b, L0_ffn_w1, L0_ffn_w2):
    f = np.ascontiguousarray

    def fm(v):
        return f(np.asarray(v, np.float32).reshape(DC, 128).T)

    shared = {
        "wxT": f(np.asarray(L0_attn_w_xproj, np.float32).T),
        "woT": f(np.asarray(L0_attn_w_out, np.float32).T),
        "w1T": f(np.asarray(L0_ffn_w1, np.float32).T),
        "w2T": f(np.asarray(L0_ffn_w2, np.float32).T),
        "bxp": f(np.asarray(L0_attn_b_xproj, np.float32).reshape(EC, 128).T),
        "afm": f(np.asarray(L0_attn_A, np.float32).reshape(DC, 128, 16)
                 .transpose(1, 0, 2).reshape(128, DC * 16)),
        "vecs": f(np.concatenate([fm(L0_attn_D_param), fm(L0_attn_gamma), fm(L0_attn_beta),
                                  fm(L0_ffn_ln_g), fm(L0_ffn_ln_b), fm(L0_attn_b_out)], axis=1)),
        "ident": np.eye(128, dtype=np.float32),
        "consts": make_consts(),
    }
    xs = np.asarray(L0_attn_x, np.float32)
    return [dict(shared, x=f(xs[c])) for c in range(8)]


def kernel(**inputs):
    in_maps = prep_inputs(**inputs)
    nc, _ = build_program()
    res = run_bass_kernel_spmd(nc, in_maps, core_ids=list(range(8)))
    return np.stack([res.results[c]["y"] for c in range(8)], axis=0)


# version 06
# speedup vs baseline: 1.5623x; 1.1111x over previous
import contextlib
import math
from collections import defaultdict

import numpy as np

import concourse.bass as bass
import concourse.mybir as mybir
from concourse.bass_utils import run_bass_kernel_spmd

F32 = mybir.dt.float32
BF16 = mybir.dt.bfloat16
I16 = mybir.dt.int16
I32 = mybir.dt.int32
AF = mybir.ActivationFunctionType
ALU = mybir.AluOpType
AX = mybir.AxisListType.X

T = 2048
D = 1024
DC = D // 128
E2 = 2048
EC = E2 // 128
H = 4096
HC = H // 128
NTT = 4
TTW = T // NTT

N1 = 8 * T * D
N2 = 8 * T * H

RG = [[0, 1, 2, 3, 4, 5, 6, 7]]


def ceil_div(a, b):
    return (a + b - 1) // b
BIG = 1.0e30
EPS = 1.0e-5
KTAY = 8

B_ILL = 14

C_200, C_ONE, C_N1, C_N2, C_P23, C_HALF, C_D, C_S127, C_ZERO, C_16384, \
    C_LO16, C_HI16, C_G1HALFSC, C_RS1, C_RS2, C_NW1, C_NW2, C_EPSC, C_DINV, C_HLFD = range(20)
NCONST = 24


def build_program(taps=(), n_cores=8):
    taps = set(taps)
    nc = bass.Bass("TRN2", target_bir_lowering=False, debug=False,
                   detect_race_conditions=False, dynamic_dma_scratch_size=4096)

    dx = nc.dram_tensor("x", [T, D], F32, kind="ExternalInput").ap()
    dwxT = nc.dram_tensor("wxT", [D, E2], F32, kind="ExternalInput").ap()
    dwoT = nc.dram_tensor("woT", [D, D], F32, kind="ExternalInput").ap()
    dw1T = nc.dram_tensor("w1T", [D, H], F32, kind="ExternalInput").ap()
    dw2T = nc.dram_tensor("w2T", [H, D], F32, kind="ExternalInput").ap()
    dbxp = nc.dram_tensor("bxp", [128, EC], F32, kind="ExternalInput").ap()
    dafm = nc.dram_tensor("afm", [128, DC * 16], F32, kind="ExternalInput").ap()
    dvecs = nc.dram_tensor("vecs", [128, 6 * DC], F32, kind="ExternalInput").ap()
    dident = nc.dram_tensor("ident", [128, 128], F32, kind="ExternalInput").ap()
    dconsts = nc.dram_tensor("consts", [128, NCONST], F32, kind="ExternalInput").ap()
    dy = nc.dram_tensor("y", [T, D], F32, kind="ExternalOutput").ap()

    d_x0 = nc.dram_tensor("d_x0", [128, DC * T], F32, kind="Internal")
    d_x1 = nc.dram_tensor("d_x1", [128, DC * T], F32, kind="Internal")
    d_h2 = nc.dram_tensor("d_h2", [128, HC * T], F32, kind="Internal")
    d_wxb = nc.dram_tensor("d_wxb", [128, DC * E2], BF16, kind="Internal")
    d_wob = nc.dram_tensor("d_wob", [128, DC * D], BF16, kind="Internal")
    d_w1q = nc.dram_tensor("d_w1q", [128, DC * H], BF16, kind="Internal")
    d_w2q = nc.dram_tensor("d_w2q", [128, HC * D], BF16, kind="Internal")

    cc_n = [0]

    def new_cc(width):
        i = cc_n[0]
        cc_n[0] += 1
        ci = nc.dram_tensor(f"cci{i}", [1, width], F32, kind="Internal")
        co = nc.dram_tensor(f"cco{i}", [n_cores, width], F32, kind="Internal")
        return ci, co

    tap_outs = {}

    def tap(name, shape):
        if name in taps:
            tap_outs[name] = nc.dram_tensor("tap_" + name, shape, F32,
                                            kind="ExternalOutput").ap()
            return tap_outs[name]
        return None

    with contextlib.ExitStack() as ctx:
        sb = lambda n_, s_, dt=F32: ctx.enter_context(nc.sbuf_tensor(n_, s_, dt))
        psum = lambda n_, s_: ctx.enter_context(nc.psum_tensor(n_, s_, F32))

        big0 = sb("big0", [128, DC * T])
        big1 = sb("big1", [128, DC * T])
        bigb = sb("bigb", [128, DC * T], BF16)
        wsb0 = sb("wsb0", [128, 1024], BF16)
        wsb1 = sb("wsb1", [128, 1024], BF16)
        wfp = sb("wfp", [128, 2048])
        ibuf = sb("ibuf", [128, 512], I16)
        sqs = sb("sqs", [128, 1024])
        bc_s = sb("bc_s", [128, TTW])
        bc_t = sb("bc_t", [128, TTW])
        ones512 = sb("ones512", [128, 512])
        ident = sb("identt", [128, 128])
        onesc = sb("onesct", [128, 128])
        ones1 = sb("ones1t", [1, 128])
        cin = sb("cint", [128, NCONST])
        bxp = sb("bxpt", [128, EC])
        afm = sb("afmt", [128, DC * 16])
        vecs = sb("vecst", [128, 6 * DC])
        coef = sb("coeft", [128, KTAY * DC])
        carry = sb("carryt", [128, DC])
        acc = sb("acct", [128, 64])
        nrw = sb("nrwt", [128, 8])
        jnk = sb("jnkt", [128, 64])
        scl = sb("sclt", [128, 20])
        gat = sb("gatt", [1, 256])
        rows = sb("rowst", [1, 5632])

        sc_e = big1[:, 24 * 512:25 * 512]
        bc_u = big1[:, 25 * 512:26 * 512]
        bc_v = big1[:, 26 * 512:27 * 512]

        ps_a = psum("ps_a", [128, 512])
        ps_b = psum("ps_b", [128, 512])
        ps_t = psum("ps_t", [128, 128])
        ps_s = psum("ps_s", [1, 512])
        ps_bc = psum("ps_bc", [128, 512])
        ps_c = psum("ps_c", [128, 1])

        R_S1 = big0[0:1, 8192 + 0 * T:8192 + 1 * T]
        R_S2 = big0[0:1, 8192 + 1 * T:8192 + 2 * T]
        R_SS = big0[0:1, 8192 + 2 * T:8192 + 3 * T]
        R_TT = big0[0:1, 8192 + 3 * T:8192 + 4 * T]
        R_T1 = R_S1
        R_R1 = R_S2
        RQ = rows[0:1, 0:5632]
        RW = 512

        def Q(i):
            return RQ[0:1, i * RW:(i + 1) * RW]

        def Q1(i):
            return RQ[0:1, i * RW:i * RW + 1]

        QV_LO, QV_HI, QV_MID, QV_CNT, QV_GE, QV_T1, QV_T2, QV_RES, \
            QV_CLO, QV_CHI = range(10)
        QV_TGT = QV_RES

        SCL_MID, SCL_M1, SCL_S, SCL_GS, SCL_I, SCL_FRAC, SCL_NT, SCL_NTM1, \
            SCL_AMAX, SCL_T2, SCL_TMP, SCL_SHIFT = range(12)
        SCL_NEG = 16

        sems = {}
        for name in ["ld", "gd", "mm", "ve", "ac", "cc", "st"]:
            sems[name] = [ctx.enter_context(nc.semaphore("s_" + name)), 0]

        ops = defaultdict(list)

        def add(e, fn):
            ops[e].append(fn)

        def _mk(e, key):
            def op(fn):
                sem = sems[key][0]

                def w(eng):
                    fn(eng).then_inc(sem, 1)
                add(e, w)
                sems[key][1] += 1
            return op

        vop = _mk("vector", "ve")
        aop = _mk("scalar", "ac")
        mmop = _mk("tensor", "mm")
        gop = _mk("gpsimd", "ve")

        def wt(e, key, val=None):
            sem, v = sems[key][0], sems[key][1]
            if val is not None:
                v = val
            add(e, lambda eng: eng.wait_ge(sem, v))

        def dma(e, key, out, in_):
            sem = sems[key][0]

            def w(eng):
                eng.dma_start(out=out, in_=in_).then_inc(sem, 16)
            add(e, w)
            sems[key][1] += 16

        def sdma(out, in_):
            dma("sync", "ld", out, in_)

        def vspace():
            vop(lambda v: v.memset(sqs[:, 512:1024], 0.0))
            vop(lambda v: v.memset(sqs[:, 512:1024], 0.0))

        def nvop(fn):
            vop(fn)
            vspace()

        def apad():
            aop(lambda s: s.activation(out=jnk[:, 0:64], in_=jnk[:, 0:64], func=AF.Identity))
            aop(lambda s: s.activation(out=jnk[:, 0:64], in_=jnk[:, 0:64], func=AF.Identity))

        def gdma(out, in_):
            dma("gpsimd", "gd", out, in_)
            wt("gpsimd", "gd")

        def allgather(src_ap, width, dst_ap):
            ci, co = new_cc(width)
            wt("gpsimd", "ve")
            wt("gpsimd", "ld")
            gdma(ci.ap()[:], src_ap)
            sem = sems["cc"][0]

            def w(g):
                g.collective_compute("AllGather", ALU.bypass, replica_groups=RG,
                                     ins=[ci.ap()[:]], outs=[co.ap()[:]]).then_inc(sem, 1)
            add("gpsimd", w)
            sems["cc"][1] += 1
            wt("gpsimd", "cc")
            gdma(dst_ap, co.ap().rearrange("a b -> (a b)")[None, :])
            wt("vector", "gd")

        def bcast_row_to_tile(row_ap, width, dst):
            wt("tensor", "ve")
            wt("tensor", "ld")
            mmop(lambda t: t.matmul(ps_bc[:, 0:width], ones1[:], row_ap,
                                    start=True, stop=True))
            wt("vector", "mm")
            vop(lambda v: v.tensor_copy(out=dst, in_=ps_bc[:, 0:width]))

        def spread_scl(row11, col):
            wt("tensor", "ve")
            wt("tensor", "ac")
            mmop(lambda t: t.matmul(ps_c[:], ones1[:], row11, start=True, stop=True))
            wt("scalar", "mm")
            aop(lambda s: s.activation(out=nrw[:, 0:1], in_=ps_c[:], func=AF.Identity))
            apad()
            wt("vector", "ac")
            vop(lambda v, col=col: v.tensor_scalar(out=scl[:, col:col + 1], in0=nrw[:, 0:1],
                                                   scalar1=1.0, scalar2=None, op0=ALU.mult))
            vop(lambda v: v.tensor_scalar(out=scl[:, SCL_NEG:SCL_NEG + 1], in0=nrw[:, 0:1],
                                          scalar1=-1.0, scalar2=None, op0=ALU.mult))
            vspace()

        sdma(big1.rearrange("p (a d) -> p a d", a=16), dx.rearrange("(a p) d -> p a d", p=128))
        sdma(ident[:], dident)
        sdma(cin[:], dconsts)
        sdma(bxp[:], dbxp)
        sdma(afm[:], dafm)
        sdma(vecs[:], dvecs)
        wt("vector", "ld")

        vop(lambda v: v.memset(rows[:], 0.0))
        vop(lambda v: v.memset(scl[:], 0.0))
        vop(lambda v: v.memset(acc[:], 0.0))
        vop(lambda v: v.memset(gat[:], 0.0))
        vop(lambda v: v.memset(onesc[:], 1.0))
        vop(lambda v: v.memset(ones512[:], 1.0))
        vop(lambda v: v.memset(ones1[:], 1.0))
        vop(lambda v: v.memset(carry[:], 0.0))
        nvop(lambda v: v.tensor_scalar(out=scl[:, SCL_TMP:SCL_TMP + 1], in0=cin[:, C_P23:C_P23 + 1],
                                       scalar1=cin[:, C_HALF:C_HALF + 1], scalar2=None, op0=ALU.add))
        nvop(lambda v: v.tensor_tensor(out=scl[:, SCL_TMP:SCL_TMP + 1], in0=scl[:, SCL_TMP:SCL_TMP + 1],
                                       in1=cin[:, C_P23:C_P23 + 1], op=ALU.subtract))
        nvop(lambda v: v.tensor_scalar(out=scl[:, SCL_SHIFT:SCL_SHIFT + 1], in0=scl[:, SCL_TMP:SCL_TMP + 1],
                                       scalar1=cin[:, C_16384:C_16384 + 1], scalar2=None, op0=ALU.add))

        vop(lambda v: v.tensor_copy(out=sqs[:, 0:DC * 16], in_=afm[:]))
        for k in range(1, KTAY + 1):
            if k > 1:
                vop(lambda v: v.tensor_tensor(out=sqs[:, 0:DC * 16], in0=sqs[:, 0:DC * 16],
                                              in1=afm[:], op=ALU.mult))
            vop(lambda v, k=k: v.tensor_reduce(
                out=coef[:, (k - 1) * DC:k * DC],
                in_=sqs[:, 0:DC * 16].rearrange("p (c s) -> p c s", s=16),
                axis=AX, op=ALU.add))
        nvop(lambda v: v.tensor_scalar(out=scl[:, SCL_TMP:SCL_TMP + 1], in0=cin[:, C_ONE:C_ONE + 1],
                                       scalar1=1.0, scalar2=None, op0=ALU.mult))
        for k in range(2, KTAY + 1):
            nvop(lambda v, k=k: v.tensor_scalar(out=scl[:, SCL_TMP:SCL_TMP + 1],
                                                in0=scl[:, SCL_TMP:SCL_TMP + 1],
                                                scalar1=float(k), scalar2=None, op0=ALU.mult))
            nvop(lambda v: v.reciprocal(out=nrw[:, 1:2], in_=scl[:, SCL_TMP:SCL_TMP + 1]))
            nvop(lambda v, k=k: v.tensor_scalar(out=coef[:, (k - 1) * DC:k * DC],
                                                in0=coef[:, (k - 1) * DC:k * DC],
                                                scalar1=nrw[:, 1:2], scalar2=None, op0=ALU.mult))


        def ln_rows():
            A_, B_, C_, D_ = R_S1, R_S2, R_SS, R_TT
            vop(lambda v: v.tensor_scalar(out=A_, in0=A_, scalar1=cin[:1, C_DINV:C_DINV + 1],
                                          scalar2=None, op0=ALU.mult))
            vop(lambda v: v.tensor_scalar(out=B_, in0=B_, scalar1=cin[:1, C_DINV:C_DINV + 1],
                                          scalar2=None, op0=ALU.mult))
            vop(lambda v: v.tensor_tensor(out=D_, in0=A_, in1=A_, op=ALU.mult))
            vop(lambda v: v.tensor_tensor(out=B_, in0=B_, in1=D_, op=ALU.subtract))
            vop(lambda v: v.tensor_scalar(out=D_, in0=B_, scalar1=EPS, scalar2=None,
                                          op0=ALU.add))
            vop(lambda v: v.reciprocal(out=C_, in_=D_))
            wt("scalar", "ve")
            aop(lambda s: s.activation(out=C_, in_=C_, func=AF.Sqrt))
            wt("vector", "ac")
            vop(lambda v: v.tensor_tensor(out=D_, in0=C_, in1=C_, op=ALU.mult))
            vop(lambda v: v.tensor_tensor(out=D_, in0=D_, in1=B_, op=ALU.mult))
            vop(lambda v: v.tensor_copy(out=B_, in_=C_))
            vop(lambda v: v.tensor_scalar(out=D_, in0=D_, scalar1=EPS, scalar2=None,
                                          op0=ALU.add))
            vop(lambda v: v.reciprocal(out=C_, in_=D_))
            wt("scalar", "ve")
            aop(lambda s: s.activation(out=D_, in_=C_, func=AF.Sqrt))
            wt("vector", "ac")
            vop(lambda v: v.tensor_tensor(out=C_, in0=B_, in1=D_, op=ALU.mult))
            vop(lambda v: v.tensor_tensor(out=A_, in0=A_, in1=B_, op=ALU.mult))
            vop(lambda v: v.tensor_scalar(out=A_, in0=A_, scalar1=-1.0,
                                          scalar2=None, op0=ALU.mult))
            vop(lambda v: v.tensor_tensor(out=D_, in0=A_, in1=D_, op=ALU.mult))

        wt("tensor", "ve")
        wt("tensor", "ld")
        tve_tp = [sems["ve"][1], sems["ve"][1]]
        for idx_tp in range(16 * DC):
            tile, dc2 = divmod(idx_tp, DC)
            bank = ps_t[:] if idx_tp % 2 == 0 else ps_bc[:, 0:128]
            wt("tensor", "ve", tve_tp[idx_tp % 2])
            mmop(lambda t, tile=tile, dc2=dc2, bank=bank: t.transpose(
                bank, big1[:, tile * D + dc2 * 128: tile * D + dc2 * 128 + 128], ident[:]))
            wt("vector", "mm")
            vop(lambda v, tile=tile, dc2=dc2, bank=bank: v.tensor_copy(
                out=big0[:, dc2 * T + tile * 128: dc2 * T + tile * 128 + 128], in_=bank))
            tve_tp[idx_tp % 2] = sems["ve"][1]
        wt("sync", "ve")
        sdma(d_x0.ap()[:], big0[:])
        tp_x0 = tap("x0fm", [128, DC * T])
        if tp_x0 is not None:
            dma("sync", "st", tp_x0[:], big0[:])
        for tile in range(16):
            vop(lambda v, tile=tile: v.tensor_reduce(
                out=acc[:, 2 * tile:2 * tile + 1], in_=big1[:, tile * D:(tile + 1) * D],
                axis=AX, op=ALU.add))
            wt("scalar", "ve")
            aop(lambda s, tile=tile: s.activation(
                out=sqs[:, 0:1024], in_=big1[:, tile * D:(tile + 1) * D], func=AF.Square))
            wt("vector", "ac")
            vop(lambda v, tile=tile: v.tensor_reduce(
                out=acc[:, 2 * tile + 1:2 * tile + 2], in_=sqs[:, 0:1024], axis=AX, op=ALU.add))
            wt("sync", "ve")
            wt("sync", "ld")
            sdma(R_S1[0:1, tile * 128:(tile + 1) * 128], acc[:, 2 * tile:2 * tile + 1])
            sdma(R_S2[0:1, tile * 128:(tile + 1) * 128], acc[:, 2 * tile + 1:2 * tile + 2])
        wt("vector", "ld")
        tp_rraw = tap("rows_raw", [1, 2 * T])
        if tp_rraw is not None:
            wt("sync", "ve")
            dma("sync", "st", tp_rraw[0:1, 0:T], R_S1)
            dma("sync", "st", tp_rraw[0:1, T:2 * T], R_S2)
            wt("vector", "st")
        ln_rows()
        tp_rows = tap("rows_ln", [1, 4 * T])
        if tp_rows is not None:
            wt("sync", "ve")
            wt("sync", "ac")
            dma("sync", "st", tp_rows[0:1, 0:T], R_S1)
            dma("sync", "st", tp_rows[0:1, T:2 * T], R_S2)
            dma("sync", "st", tp_rows[0:1, 2 * T:3 * T], R_SS)
            dma("sync", "st", tp_rows[0:1, 3 * T:4 * T], R_TT)
            wt("vector", "st")

        for kc in range(DC):
            wt("sync", "ve")
            wt("sync", "ld")
            sdma(wfp[:, 0:2048], dwxT[kc * 128:(kc + 1) * 128, :])
            wt("vector", "ld")
            vop(lambda v: v.tensor_copy(out=bigb[:, 0:2048], in_=wfp[:, 0:2048]))
            wt("sync", "ve")
            sdma(d_wxb.ap()[:, kc * E2:(kc + 1) * E2], bigb[:, 0:2048])
            sdma(wfp[:, 0:1024], dwoT[kc * 128:(kc + 1) * 128, :])
            wt("vector", "ld")
            vop(lambda v: v.tensor_copy(out=bigb[:, 2048:3072], in_=wfp[:, 0:1024]))
            wt("sync", "ve")
            sdma(d_wob.ap()[:, kc * D:(kc + 1) * D], bigb[:, 2048:3072])

        def wprep(dsrc, ddst, n_chunks, width, acc_col, gcol, nwcol):
            pieces = ceil_div(width, 2048)
            pw = width // pieces
            ncols = n_chunks * pieces
            for ch in range(n_chunks):
                for pc in range(pieces):
                    j = ch * pieces + pc
                    wt("sync", "ve")
                    sdma(wfp[:, 0:pw], dsrc[ch * 128:(ch + 1) * 128, pc * pw:(pc + 1) * pw])
                    wt("vector", "ld")
                    vop(lambda v: v.scalar_tensor_tensor(
                        out=big1[:, 0:pw], in0=wfp[:, 0:pw], scalar=-1.0,
                        in1=wfp[:, 0:pw], op0=ALU.mult, op1=ALU.max))
                    vop(lambda v, j=j: v.tensor_reduce(out=acc[:, acc_col + j:acc_col + j + 1],
                                                       in_=big1[:, 0:pw], axis=AX, op=ALU.add))
            vspace()
            vop(lambda v: v.tensor_reduce(out=acc[:, acc_col + ncols:acc_col + ncols + 1],
                                          in_=acc[:, acc_col:acc_col + ncols], axis=AX, op=ALU.add))
            vspace()
            wt("tensor", "ve")
            wt("tensor", "ac")
            mmop(lambda t: t.matmul(ps_c[:], onesc[:], acc[:, acc_col + ncols:acc_col + ncols + 1],
                                    start=True, stop=True))
            wt("vector", "mm")
            nvop(lambda v: v.tensor_scalar(out=scl[:, gcol:gcol + 1], in0=ps_c[:],
                                           scalar1=cin[:, nwcol:nwcol + 1],
                                           scalar2=cin[:, C_EPSC:C_EPSC + 1],
                                           op0=ALU.mult, op1=ALU.add))
            nvop(lambda v: v.tensor_scalar(out=scl[:, gcol + 1:gcol + 2], in0=scl[:, gcol:gcol + 1],
                                           scalar1=0.5, scalar2=None, op0=ALU.mult))
            for ch in range(n_chunks):
                for pc in range(pieces):
                    wt("sync", "ve")
                    wt("sync", "ac")
                    sdma(wfp[:, 0:pw], dsrc[ch * 128:(ch + 1) * 128, pc * pw:(pc + 1) * pw])
                    wt("vector", "ld")
                    wt("scalar", "ld")
                    wt("scalar", "ve")
                    aop(lambda sx: sx.activation(out=big0[:, 0:pw], in_=wfp[:, 0:pw], func=AF.Sign))
                    vop(lambda v: v.scalar_tensor_tensor(
                        out=big1[:, 0:pw], in0=wfp[:, 0:pw], scalar=-1.0,
                        in1=wfp[:, 0:pw], op0=ALU.mult, op1=ALU.max))
                    vop(lambda v: v.tensor_scalar(out=big1[:, 0:pw], in0=big1[:, 0:pw],
                                                  scalar1=scl[:, gcol + 1:gcol + 2], scalar2=None,
                                                  op0=ALU.is_gt))
                    wt("vector", "ac")
                    vop(lambda v: v.tensor_tensor(out=bigb[:, 0:pw],
                                                  in0=big1[:, 0:pw], in1=big0[:, 0:pw], op=ALU.mult))
                    wt("sync", "ve")
                    sdma(ddst.ap()[:, ch * width + pc * pw:ch * width + (pc + 1) * pw], bigb[:, 0:pw])

        wprep(dw1T, d_w1q, DC, H, 8, 12, C_NW1)
        wprep(dw2T, d_w2q, HC, D, 25, 14, C_NW2)

        tp_xn = tap("xn", [128, DC * T])
        tp_dl = tap("delta", [128, DC * T])
        tp_gate = tap("gate", [128, DC * T])
        tp_h = tap("hcs", [128, DC * T])
        tp_ao = tap("attn_out", [128, DC * T])
        tp_x1 = tap("x1", [128, DC * T])

        X0T = lambda c: big0[:, c * TTW:(c + 1) * TTW]
        AOT = lambda c: big0[:, DC * TTW + c * TTW: DC * TTW + (c + 1) * TTW]
        XNT = lambda c: big1[:, c * TTW:(c + 1) * TTW]
        XNB = lambda c: bigb[:, c * TTW:(c + 1) * TTW]
        sc_a = big1[:, DC * TTW:2 * DC * TTW]
        sc_b = big1[:, 2 * DC * TTW:3 * DC * TTW]
        mixb = bigb[:, DC * TTW:2 * DC * TTW]

        for tt in range(NTT):
            t0 = tt * TTW
            wt("sync", "ve")
            wt("sync", "ld")
            sdma(big0[:, 0:DC * TTW], d_x0.ap().rearrange("p (c t) -> p c t", c=DC)[:, :, t0:t0 + TTW])
            bcast_row_to_tile(R_SS[0:1, t0:t0 + TTW], TTW, bc_s[:])
            bcast_row_to_tile(R_TT[0:1, t0:t0 + TTW], TTW, bc_t[:])
            bcast_row_to_tile(R_R1[0:1, t0:t0 + TTW], TTW, bc_u[:])
            bcast_row_to_tile(R_T1[0:1, t0:t0 + TTW], TTW, bc_v[:])
            wt("vector", "ld")
            for c in range(DC):
                vop(lambda v, c=c: v.tensor_tensor(out=XNT(c), in0=X0T(c), in1=bc_s[:], op=ALU.mult))
                vop(lambda v, c=c: v.tensor_tensor(out=XNT(c), in0=XNT(c), in1=bc_t[:], op=ALU.add))
                vop(lambda v, c=c: v.tensor_scalar(out=XNT(c), in0=XNT(c),
                                                   scalar1=vecs[:, DC + c:DC + c + 1],
                                                   scalar2=vecs[:, 2 * DC + c:2 * DC + c + 1],
                                                   op0=ALU.mult, op1=ALU.add))
                vop(lambda v, c=c: v.tensor_copy(out=XNB(c), in_=XNT(c)))
            if tp_xn is not None:
                wt("sync", "ve")
                for c in range(DC):
                    dma("sync", "st", tp_xn[:, c * T + t0:c * T + t0 + TTW], XNT(c))

            wt("vector", "mm")
            wt("tensor", "ve")
            for mc in range(EC):
                wbuf = wsb0 if mc % 2 == 0 else wsb1
                wt("sync", "mm")
                sdma(wbuf[:, 0:DC * 128],
                     d_wxb.ap().rearrange("p (k e) -> p k e", k=DC)[:, :, mc * 128:(mc + 1) * 128])
                wt("tensor", "ld")
                wt("tensor", "ac", sems["ac"][1])
                for kc in range(DC):
                    mmop(lambda t, mc=mc, kc=kc, wbuf=wbuf: t.matmul(
                        ps_a[:], wbuf[:, kc * 128:(kc + 1) * 128], XNB(kc),
                        start=(kc == 0), stop=(kc == DC - 1)))
                wt("scalar", "mm")
                if mc < DC:
                    aop(lambda s, mc=mc: s.activation(
                        out=sc_e[:], in_=ps_a[:], func=AF.Exp,
                        bias=bxp[:, mc:mc + 1], scale=1.0))
                    aop(lambda s, mc=mc: s.activation(
                        out=sc_a[:, mc * TTW:(mc + 1) * TTW], in_=sc_e[:],
                        func=AF.Ln, bias=1.0, scale=1.0))
                else:
                    aop(lambda s, mc=mc: s.activation(
                        out=sc_b[:, (mc - DC) * TTW:(mc - DC + 1) * TTW], in_=ps_a[:],
                        func=AF.Identity, bias=bxp[:, mc:mc + 1], scale=1.0))
            if tp_dl is not None:
                wt("sync", "ac")
                for c in range(DC):
                    dma("sync", "st", tp_dl[:, c * T + t0:c * T + t0 + TTW],
                        sc_a[:, c * TTW:(c + 1) * TTW])

            wt("vector", "ac")
            for c in range(DC):
                DL = sc_a[:, c * TTW:(c + 1) * TTW]
                BGV = sc_b[:, c * TTW:(c + 1) * TTW]
                vop(lambda v, c=c, DL=DL: v.tensor_scalar(
                    out=sc_e[:], in0=DL,
                    scalar1=coef[:, (KTAY - 1) * DC + c:(KTAY - 1) * DC + c + 1],
                    scalar2=None, op0=ALU.mult))
                for k in range(KTAY - 1, 0, -1):
                    vop(lambda v, c=c, k=k, DL=DL: v.scalar_tensor_tensor(
                        out=sc_e[:], in0=sc_e[:],
                        scalar=coef[:, (k - 1) * DC + c:(k - 1) * DC + c + 1],
                        in1=DL, op0=ALU.add, op1=ALU.mult))
                if tp_gate is not None:
                    vop(lambda v: v.tensor_scalar(out=sqs[:, 0:TTW], in0=sc_e[:],
                                                  scalar1=16.0, scalar2=None, op0=ALU.add))
                    wt("sync", "ve")
                    dma("sync", "st", tp_gate[:, c * T + t0:c * T + t0 + TTW], sqs[:, 0:TTW])
                vop(lambda v, c=c, BGV=BGV: v.tensor_tensor(out=BGV, in0=BGV, in1=XNT(c), op=ALU.mult))
                vop(lambda v, c=c, BGV=BGV: v.tensor_tensor_scan(
                    out=BGV, data0=ones512[:],
                    data1=BGV, initial=carry[:, c:c + 1], op0=ALU.mult, op1=ALU.add))
                vspace()
                vop(lambda v, c=c, BGV=BGV: v.tensor_copy(out=carry[:, c:c + 1],
                                                          in_=BGV[:, TTW - 1:TTW]))
                if tp_h is not None:
                    wt("sync", "ve")
                    dma("sync", "st", tp_h[:, c * T + t0:c * T + t0 + TTW], BGV)
                vop(lambda v, c=c, BGV=BGV: v.scalar_tensor_tensor(
                    out=sc_e[:], in0=sc_e[:], scalar=16.0, in1=BGV,
                    op0=ALU.add, op1=ALU.mult))
                vop(lambda v, c=c: v.scalar_tensor_tensor(
                    out=sc_e[:], in0=XNT(c), scalar=vecs[:, c:c + 1], in1=sc_e[:],
                    op0=ALU.mult, op1=ALU.add))
                vop(lambda v, c=c: v.tensor_copy(out=mixb[:, c * TTW:(c + 1) * TTW], in_=sc_e[:]))

            wt("tensor", "ve")
            for mc in range(DC):
                wbuf = wsb0 if mc % 2 == 0 else wsb1
                wt("sync", "mm")
                sdma(wbuf[:, 0:DC * 128],
                     d_wob.ap().rearrange("p (k e) -> p k e", k=DC)[:, :, mc * 128:(mc + 1) * 128])
                wt("tensor", "ld")
                wt("tensor", "ve", sems["ve"][1])
                for kc in range(DC):
                    mmop(lambda t, mc=mc, kc=kc, wbuf=wbuf: t.matmul(
                        ps_b[:], wbuf[:, kc * 128:(kc + 1) * 128],
                        mixb[:, kc * TTW:(kc + 1) * TTW],
                        start=(kc == 0), stop=(kc == DC - 1)))
                wt("vector", "mm")
                vop(lambda v, mc=mc: v.tensor_scalar(
                    out=AOT(mc), in0=ps_b[:],
                    scalar1=vecs[:, 5 * DC + mc:5 * DC + mc + 1], scalar2=None, op0=ALU.add))
                vop(lambda v, mc=mc: v.tensor_tensor(out=sc_e[:], in0=X0T(mc), in1=bc_u[:], op=ALU.mult))
                vop(lambda v, mc=mc: v.tensor_tensor(out=sc_e[:], in0=sc_e[:], in1=bc_v[:], op=ALU.add))
                vop(lambda v, mc=mc: v.tensor_tensor(out=AOT(mc), in0=AOT(mc), in1=sc_e[:], op=ALU.add))
            if tp_ao is not None:
                wt("sync", "ve")
                for c in range(DC):
                    dma("sync", "st", tp_ao[:, c * T + t0:c * T + t0 + TTW], AOT(c))

            wt("tensor", "ve")
            for c in range(DC):
                mmop(lambda t, c=c: t.matmul(ps_s[:], onesc[:, 0:1], AOT(c),
                                             start=(c == 0), stop=(c == DC - 1)))
            wt("vector", "mm")
            vop(lambda v: v.tensor_copy(out=RQ[0:1, 0:TTW], in_=ps_s[0:1, :]))
            for c in range(DC):
                wt("scalar", "ve")
                wt("scalar", "mm")
                aop(lambda s, c=c: s.activation(out=sqs[:, 0:TTW], in_=AOT(c), func=AF.Square))
                wt("tensor", "ac")
                mmop(lambda t, c=c: t.matmul(ps_s[:], onesc[:, 0:1], sqs[:, 0:TTW],
                                             start=(c == 0), stop=(c == DC - 1)))
            wt("vector", "mm")
            vop(lambda v: v.tensor_copy(out=RQ[0:1, 512:512 + TTW], in_=ps_s[0:1, :]))
            S1s = RQ[0:1, 0:TTW]
            S2s = RQ[0:1, 512:512 + TTW]
            vop(lambda v: v.tensor_scalar(out=S1s, in0=S1s, scalar1=cin[:1, C_DINV:C_DINV + 1],
                                          scalar2=None, op0=ALU.mult))
            vop(lambda v: v.tensor_scalar(out=S2s, in0=S2s, scalar1=cin[:1, C_DINV:C_DINV + 1],
                                          scalar2=None, op0=ALU.mult))
            vop(lambda v: v.tensor_tensor(out=RQ[0:1, 1024:1024 + TTW],
                                          in0=S1s, in1=S1s, op=ALU.mult))
            vop(lambda v: v.tensor_tensor(out=S2s, in0=S2s, in1=RQ[0:1, 1024:1024 + TTW], op=ALU.subtract))
            vop(lambda v: v.tensor_scalar(out=RQ[0:1, 1024:1024 + TTW], in0=S2s,
                                          scalar1=EPS, scalar2=None, op0=ALU.add))
            vop(lambda v: v.reciprocal(out=S2s, in_=RQ[0:1, 1024:1024 + TTW]))
            wt("scalar", "ve")
            aop(lambda s: s.activation(out=S2s, in_=S2s, func=AF.Sqrt))
            wt("vector", "ac")
            vop(lambda v: v.tensor_tensor(out=S1s, in0=S1s, in1=S2s, op=ALU.mult))
            vop(lambda v: v.tensor_scalar(out=S1s, in0=S1s, scalar1=-1.0, scalar2=None, op0=ALU.mult))
            bcast_row_to_tile(S2s, TTW, bc_u[:])
            bcast_row_to_tile(S1s, TTW, bc_v[:])
            for c in range(DC):
                vop(lambda v, c=c: v.tensor_tensor(out=AOT(c), in0=AOT(c), in1=bc_u[:], op=ALU.mult))
                vop(lambda v, c=c: v.tensor_tensor(out=AOT(c), in0=AOT(c), in1=bc_v[:], op=ALU.add))
                vop(lambda v, c=c: v.tensor_tensor(out=AOT(c), in0=AOT(c), in1=X0T(c), op=ALU.add))
            wt("sync", "ve")
            sdma(d_x1.ap().rearrange("p (c t) -> p c t", c=DC)[:, :, t0:t0 + TTW],
                 big0[:, DC * TTW:2 * DC * TTW].rearrange("p (c t) -> p c t", c=DC))
            if tp_x1 is not None:
                for c in range(DC):
                    dma("sync", "st", tp_x1[:, c * T + t0:c * T + t0 + TTW], AOT(c))

        tp_fh = tap("fh", [128, DC * T])
        for tt in range(NTT):
            t0 = tt * TTW
            wt("sync", "mm")
            wt("sync", "ac")
            sdma(big0[:, 0:DC * TTW], d_x1.ap().rearrange("p (c t) -> p c t", c=DC)[:, :, t0:t0 + TTW])
            wt("tensor", "ld")
            wt("vector", "ld")
            for c in range(DC):
                mmop(lambda t, c=c: t.matmul(ps_s[:], onesc[:, 0:1],
                                             big0[:, c * TTW:(c + 1) * TTW],
                                             start=(c == 0), stop=(c == DC - 1)))
            wt("vector", "mm")
            vop(lambda v, t0=t0: v.tensor_copy(out=R_S1[0:1, t0:t0 + TTW], in_=ps_s[0:1, :]))
            for c in range(DC):
                wt("scalar", "ve")
                wt("scalar", "mm")
                aop(lambda s, c=c: s.activation(out=sqs[:, 0:TTW],
                                                in_=big0[:, c * TTW:(c + 1) * TTW], func=AF.Square))
                wt("tensor", "ac")
                mmop(lambda t, c=c: t.matmul(ps_s[:], onesc[:, 0:1], sqs[:, 0:TTW],
                                             start=(c == 0), stop=(c == DC - 1)))
            wt("vector", "mm")
            vop(lambda v, t0=t0: v.tensor_copy(out=R_S2[0:1, t0:t0 + TTW], in_=ps_s[0:1, :]))
        ln_rows()

        for tt in range(NTT):
            t0 = tt * TTW
            bcast_row_to_tile(R_SS[0:1, t0:t0 + TTW], TTW, bc_s[:])
            bcast_row_to_tile(R_TT[0:1, t0:t0 + TTW], TTW, bc_t[:])
            wt("sync", "ve")
            sdma(big0[:, 0:DC * TTW], d_x1.ap().rearrange("p (c t) -> p c t", c=DC)[:, :, t0:t0 + TTW])
            wt("vector", "ld")
            for c in range(DC):
                FH = big1[:, c * T + t0: c * T + t0 + TTW]
                vop(lambda v, c=c, FH=FH: v.tensor_tensor(
                    out=FH, in0=big0[:, c * TTW:(c + 1) * TTW], in1=bc_s[:], op=ALU.mult))
                vop(lambda v, c=c, FH=FH: v.tensor_tensor(out=FH, in0=FH, in1=bc_t[:], op=ALU.add))
                vop(lambda v, c=c, FH=FH: v.tensor_scalar(
                    out=FH, in0=FH, scalar1=vecs[:, 3 * DC + c:3 * DC + c + 1],
                    scalar2=vecs[:, 4 * DC + c:4 * DC + c + 1], op0=ALU.mult, op1=ALU.add))
        if tp_fh is not None:
            wt("sync", "ve")
            dma("sync", "st", tp_fh[:, 0:DC * T], big1[:])

        stream_ve = [0, 0]

        def stream_piece(i2, dap):
            h = i2 % 2
            buf = wfp[:, h * 1024:(h + 1) * 1024]
            wt("sync", "ve", stream_ve[h])
            sdma(buf, dap)
            wt("vector", "ld")
            return h, buf

        def stream_done(h):
            stream_ve[h] = sems["ve"][1]

        def stream_begin():
            stream_ve[0] = stream_ve[1] = sems["ve"][1]

        def count_pass(data_slices, thr_scl_col, acc_cols, signed=False, stream=()):
            j = 0
            for ap_, w_ in data_slices:
                vop(lambda v, ap_=ap_, w_=w_, j=j: v.tensor_scalar(
                    out=big_scr(w_), in0=ap_, scalar1=scl[:, thr_scl_col:thr_scl_col + 1],
                    scalar2=None, op0=ALU.is_gt, op1=ALU.add,
                    accum_out=acc[:, acc_cols[0] + j:acc_cols[0] + j + 1]))
                j += 1
                if signed:
                    vop(lambda v, ap_=ap_, w_=w_, j=j: v.tensor_scalar(
                        out=big_scr(w_), in0=ap_, scalar1=scl[:, SCL_NEG:SCL_NEG + 1],
                        scalar2=None, op0=ALU.is_lt, op1=ALU.add,
                        accum_out=acc[:, acc_cols[0] + j:acc_cols[0] + j + 1]))
                    j += 1
            if stream:
                stream_begin()
                for i2, dap in enumerate(stream):
                    h, buf = stream_piece(i2, dap)
                    vop(lambda v, buf=buf: v.scalar_tensor_tensor(
                        out=sqs[:, 0:1024], in0=buf, scalar=-1.0, in1=buf,
                        op0=ALU.mult, op1=ALU.max))
                    vop(lambda v, j=j, tc=thr_scl_col: v.tensor_scalar(
                        out=sqs[:, 0:1024], in0=sqs[:, 0:1024],
                        scalar1=scl[:, tc:tc + 1], scalar2=None,
                        op0=ALU.is_gt, op1=ALU.add,
                        accum_out=acc[:, acc_cols[0] + j:acc_cols[0] + j + 1]))
                    stream_done(h)
                    j += 1
            vspace()
            vop(lambda v, j=j: v.tensor_reduce(
                out=acc[:, acc_cols[0] + j:acc_cols[0] + j + 1],
                in_=acc[:, acc_cols[0]:acc_cols[0] + j], axis=AX, op=ALU.add))
            vspace()
            return acc[:, acc_cols[0] + j:acc_cols[0] + j + 1]


        def big_scr(w_):
            return wfp[:, 0:w_]

        def global_count(local_col_ap, dst_row):
            wt("tensor", "ve")
            wt("tensor", "ac")
            mmop(lambda t: t.matmul(ps_c[:], onesc[:], local_col_ap, start=True, stop=True))
            wt("scalar", "mm")
            aop(lambda s: s.activation(out=nrw[:, 0:1], in_=ps_c[:], func=AF.Identity))
            apad()
            wt("sync", "ac")
            sdma(gat[0:1, 16:17], nrw[0:1, 0:1])
            allgather(gat[0:1, 16:17], 1, gat[0:1, 8:8 + n_cores])
            vop(lambda v: v.tensor_reduce(out=dst_row, in_=gat[0:1, 8:8 + n_cores], axis=AX, op=ALU.add))
            vspace()

        def quantile(data_slices, samp_slices, n_total, samp_target,
                     nt_col, ntm1_col, frac_col, amax_col, signed=False, stream=(),
                     tag="", n_it=B_ILL):
            stream = list(stream)


            """Exact global quantile of the |values| in data_slices.

            nt_col/ntm1_col/frac_col: scl columns that already hold nt, nt-1
            (per-partition) and frac; result -> scl[amax_col] (and row QV_RES).
            Target-rank values must already be in rows R_NT etc (see caller).
            """
            nsl = len(data_slices)
            for si, (ap_, w_) in enumerate(data_slices):
                if signed:
                    vop(lambda v, ap_=ap_, w_=w_: v.scalar_tensor_tensor(
                        out=big_scr(w_), in0=ap_, scalar=-1.0, in1=ap_,
                        op0=ALU.mult, op1=ALU.max))
                    vop(lambda v, w_=w_, si=si: v.tensor_reduce(
                        out=acc[:, 40 + si:41 + si], in_=big_scr(w_), axis=AX, op=ALU.max))
                else:
                    vop(lambda v, ap_=ap_, si=si: v.tensor_reduce(out=acc[:, 40 + si:41 + si],
                                                                  in_=ap_, axis=AX, op=ALU.max))
            nred = nsl
            if stream:
                stream_begin()
                vop(lambda v: v.memset(bc_s[:], 0.0))
                vop(lambda v: v.memset(bc_t[:], 0.0))
                for i2, dap in enumerate(stream):
                    h, buf = stream_piece(i2, dap)
                    vop(lambda v, buf=buf: v.tensor_tensor(
                        out=bc_s[:], in0=bc_s[:], in1=buf[:, 0:512], op=ALU.max))
                    vop(lambda v, buf=buf: v.scalar_tensor_tensor(
                        out=bc_s[:], in0=buf[:, 0:512], scalar=-1.0, in1=bc_s[:],
                        op0=ALU.mult, op1=ALU.max))
                    vop(lambda v, buf=buf: v.tensor_tensor(
                        out=bc_t[:], in0=bc_t[:], in1=buf[:, 512:1024], op=ALU.max))
                    vop(lambda v, buf=buf: v.scalar_tensor_tensor(
                        out=bc_t[:], in0=buf[:, 512:1024], scalar=-1.0, in1=bc_t[:],
                        op0=ALU.mult, op1=ALU.max))
                    stream_done(h)
                vop(lambda v, nsl=nsl: v.tensor_reduce(
                    out=acc[:, 40 + nsl:41 + nsl], in_=bc_s[:], axis=AX, op=ALU.max))
                vop(lambda v, nsl=nsl: v.tensor_reduce(
                    out=acc[:, 41 + nsl:42 + nsl], in_=bc_t[:], axis=AX, op=ALU.max))
                nred = nsl + 2
            vspace()
            vop(lambda v, nred=nred: v.tensor_reduce(out=acc[:, 40 + nred:41 + nred],
                                                     in_=acc[:, 40:40 + nred], axis=AX, op=ALU.max))
            vspace()
            wt("sync", "ve")
            sdma(gat[0:1, 128:256], acc[:, 40 + nred:41 + nred])
            wt("vector", "ld")
            vop(lambda v: v.tensor_reduce(out=gat[0:1, 4:5], in_=gat[0:1, 128:256], axis=AX, op=ALU.max))
            vspace()
            allgather(gat[0:1, 4:5], 1, gat[0:1, 8:8 + n_cores])
            vop(lambda v: v.tensor_reduce(out=gat[0:1, 5:6], in_=gat[0:1, 8:8 + n_cores], axis=AX, op=ALU.max))
            vspace()
            vop(lambda v: v.tensor_scalar(out=Q(QV_HI), in0=RQ[0:1, 0:RW], scalar1=0.0,
                                          scalar2=gat[0:1, 5:6], op0=ALU.mult, op1=ALU.add))
            vop(lambda v: v.tensor_scalar(out=Q(QV_LO), in0=Q(QV_HI), scalar1=0.0,
                                          scalar2=None, op0=ALU.mult))
            tp_qg = tap("qgat" + tag, [1, 256]) if tag else None
            if tp_qg is not None:
                wt("sync", "ve")
                dma("sync", "st", tp_qg[:], gat[:])
                tp_qa = tap("qacc" + tag, [128, 12])
                dma("sync", "st", tp_qa[:], acc[:, 40:52])
                wt("vector", "st")

            nt_py = n_total - int(math.floor(0.995 * (n_total - 1)))
            ntf = float(nt_py)
            ltgt = float(math.log(nt_py - 0.5))
            LOr, HIr, MIDr, CNTr, GEr = Q(QV_LO), Q(QV_HI), Q(QV_MID), Q(QV_CNT), Q(QV_GE)
            T1r, T2r, LCLOr, LCHIr = Q(QV_T1), Q(QV_T2), Q(QV_CLO), Q(QV_CHI)
            vop(lambda v, c_=float(math.log(n_total)): v.tensor_scalar(
                out=LCLOr, in0=LOr, scalar1=0.0, scalar2=c_, op0=ALU.mult, op1=ALU.add))
            vop(lambda v, c_=float(math.log(0.5)): v.tensor_scalar(
                out=LCHIr, in0=LOr, scalar1=0.0, scalar2=c_, op0=ALU.mult, op1=ALU.add))
            for it in range(n_it):
                if it % 2 == 0:
                    vop(lambda v: v.tensor_tensor(out=T1r, in0=LCLOr, in1=LCHIr, op=ALU.subtract))
                    vop(lambda v: v.reciprocal(out=T2r, in_=T1r))
                    vop(lambda v, c_=-ltgt: v.tensor_scalar(out=T1r, in0=LCLOr, scalar1=c_,
                                                            scalar2=None, op0=ALU.add))
                    vop(lambda v: v.tensor_tensor(out=T1r, in0=T1r, in1=T2r, op=ALU.mult))
                    vop(lambda v: v.tensor_scalar(out=T1r, in0=T1r, scalar1=0.02, scalar2=0.98,
                                                  op0=ALU.max, op1=ALU.min))
                    vop(lambda v: v.tensor_tensor(out=T2r, in0=HIr, in1=LOr, op=ALU.subtract))
                    vop(lambda v: v.tensor_tensor(out=T2r, in0=T2r, in1=T1r, op=ALU.mult))
                    vop(lambda v: v.tensor_tensor(out=MIDr, in0=LOr, in1=T2r, op=ALU.add))
                else:
                    vop(lambda v: v.tensor_tensor(out=MIDr, in0=LOr, in1=HIr, op=ALU.add))
                    vop(lambda v: v.tensor_scalar(out=MIDr, in0=MIDr, scalar1=0.5,
                                                  scalar2=None, op0=ALU.mult))
                spread_scl(Q1(QV_MID), SCL_MID)
                c_ap = count_pass(data_slices, SCL_MID, (0,), signed=signed, stream=stream)
                global_count(c_ap, gat[0:1, 60:61])
                vop(lambda v: v.tensor_scalar(out=CNTr, in0=CNTr, scalar1=0.0,
                                              scalar2=gat[0:1, 60:61], op0=ALU.mult, op1=ALU.add))
                vop(lambda v, c_=ntf: v.tensor_scalar(out=GEr, in0=CNTr, scalar1=c_,
                                                      scalar2=None, op0=ALU.is_ge))
                vop(lambda v: v.tensor_scalar(out=T1r, in0=GEr, scalar1=-1.0, scalar2=1.0,
                                              op0=ALU.mult, op1=ALU.add))
                vop(lambda v: v.tensor_tensor(out=T2r, in0=GEr, in1=MIDr, op=ALU.mult))
                vop(lambda v: v.tensor_tensor(out=LOr, in0=LOr, in1=T1r, op=ALU.mult))
                vop(lambda v: v.tensor_tensor(out=LOr, in0=LOr, in1=T2r, op=ALU.add))
                vop(lambda v: v.tensor_tensor(out=T2r, in0=T1r, in1=MIDr, op=ALU.mult))
                vop(lambda v: v.tensor_tensor(out=HIr, in0=HIr, in1=GEr, op=ALU.mult))
                vop(lambda v: v.tensor_tensor(out=HIr, in0=HIr, in1=T2r, op=ALU.add))
                vop(lambda v: v.tensor_scalar(out=T2r, in0=CNTr, scalar1=0.5, scalar2=None,
                                              op0=ALU.max))
                wt("scalar", "ve")
                aop(lambda s: s.activation(out=T2r, in_=T2r, func=AF.Ln))
                wt("vector", "ac")
                vop(lambda v, c_=ltgt / 2.0: v.tensor_scalar(out=MIDr, in0=LCLOr, scalar1=0.5,
                                                             scalar2=c_, op0=ALU.mult, op1=ALU.add))
                vop(lambda v: v.tensor_tensor(out=MIDr, in0=MIDr, in1=T1r, op=ALU.mult))
                vop(lambda v: v.tensor_tensor(out=LCLOr, in0=T2r, in1=GEr, op=ALU.mult))
                vop(lambda v: v.tensor_tensor(out=LCLOr, in0=LCLOr, in1=MIDr, op=ALU.add))
                vop(lambda v, c_=ltgt / 2.0: v.tensor_scalar(out=MIDr, in0=LCHIr, scalar1=0.5,
                                                             scalar2=c_, op0=ALU.mult, op1=ALU.add))
                vop(lambda v: v.tensor_tensor(out=MIDr, in0=MIDr, in1=GEr, op=ALU.mult))
                vop(lambda v: v.tensor_tensor(out=LCHIr, in0=T2r, in1=T1r, op=ALU.mult))
                vop(lambda v: v.tensor_tensor(out=LCHIr, in0=LCHIr, in1=MIDr, op=ALU.add))

            def masked_min(thr_col, dst_gat_col):
                j2 = 0
                for ap_, w_ in data_slices:
                    vop(lambda v, ap_=ap_, w_=w_: v.tensor_scalar(
                        out=big_scr(w_), in0=ap_, scalar1=scl[:, thr_col:thr_col + 1],
                        scalar2=BIG, op0=ALU.is_le, op1=ALU.mult))
                    vop(lambda v, ap_=ap_, w_=w_: v.tensor_tensor(
                        out=big_scr(w_), in0=big_scr(w_), in1=ap_, op=ALU.add))
                    vop(lambda v, w_=w_, j2=j2: v.tensor_reduce(out=acc[:, 0 + j2:1 + j2],
                                                                in_=big_scr(w_), axis=AX, op=ALU.min))
                    j2 += 1
                    if signed:
                        vop(lambda v, ap_=ap_, w_=w_: v.tensor_scalar(
                            out=big_scr(w_), in0=ap_, scalar1=scl[:, SCL_NEG:SCL_NEG + 1],
                            scalar2=BIG, op0=ALU.is_ge, op1=ALU.mult))
                        vop(lambda v, ap_=ap_, w_=w_: v.scalar_tensor_tensor(
                            out=big_scr(w_), in0=ap_, scalar=-1.0, in1=big_scr(w_),
                            op0=ALU.mult, op1=ALU.add))
                        vop(lambda v, w_=w_, j2=j2: v.tensor_reduce(out=acc[:, 0 + j2:1 + j2],
                                                                    in_=big_scr(w_), axis=AX, op=ALU.min))
                        j2 += 1
                if stream:
                    stream_begin()
                    for i2, dap in enumerate(stream):
                        h, buf = stream_piece(i2, dap)
                        vop(lambda v, buf=buf: v.scalar_tensor_tensor(
                            out=bc_s[:], in0=buf[:, 0:512], scalar=-1.0, in1=buf[:, 0:512],
                            op0=ALU.mult, op1=ALU.max))
                        vop(lambda v, tc=thr_col: v.tensor_scalar(
                            out=bc_t[:], in0=bc_s[:], scalar1=scl[:, tc:tc + 1],
                            scalar2=BIG, op0=ALU.is_le, op1=ALU.mult))
                        vop(lambda v: v.tensor_tensor(
                            out=bc_t[:], in0=bc_t[:], in1=bc_s[:], op=ALU.max))
                        vop(lambda v, buf=buf: v.scalar_tensor_tensor(
                            out=bc_s[:], in0=buf[:, 512:1024], scalar=-1.0, in1=buf[:, 512:1024],
                            op0=ALU.mult, op1=ALU.max))
                        vop(lambda v, tc=thr_col: v.tensor_scalar(
                            out=sqs[:, 0:512], in0=bc_s[:], scalar1=scl[:, tc:tc + 1],
                            scalar2=BIG, op0=ALU.is_le, op1=ALU.mult))
                        vop(lambda v: v.tensor_tensor(
                            out=sqs[:, 0:512], in0=sqs[:, 0:512], in1=bc_s[:], op=ALU.max))
                        vop(lambda v: v.tensor_tensor(
                            out=bc_t[:], in0=bc_t[:], in1=sqs[:, 0:512], op=ALU.min))
                        vop(lambda v, j2=j2: v.tensor_reduce(
                            out=acc[:, 0 + j2:1 + j2], in_=bc_t[:], axis=AX, op=ALU.min))
                        stream_done(h)
                        j2 += 1
                vspace()
                vop(lambda v, j2=j2: v.tensor_reduce(out=acc[:, 0 + j2:1 + j2],
                                                     in_=acc[:, 0:0 + j2], axis=AX, op=ALU.min))
                vspace()
                wt("sync", "ve")
                sdma(gat[0:1, 128:256], acc[:, 0 + j2:1 + j2])
                wt("vector", "ld")
                vop(lambda v: v.tensor_reduce(out=gat[0:1, 20:21], in_=gat[0:1, 128:256],
                                              axis=AX, op=ALU.min))
                vspace()
                allgather(gat[0:1, 20:21], 1, gat[0:1, 8:8 + n_cores])
                vop(lambda v, dc_=dst_gat_col: v.tensor_reduce(
                    out=gat[0:1, dc_:dc_ + 1], in_=gat[0:1, 8:8 + n_cores], axis=AX, op=ALU.min))
                vspace()

            spread_scl(Q1(QV_LO), SCL_MID)
            vspace()
            masked_min(SCL_MID, 21)
            nvop(lambda v: v.tensor_scalar(out=gat[0:1, 28:29], in0=gat[0:1, 21:22],
                                          scalar1=cin[0:1, C_EPSC:C_EPSC + 1], scalar2=None,
                                          op0=ALU.max))
            spread_scl(gat[0:1, 28:29], amax_col)

        i32w = sb("i32w", [1, 512], I32)

        def index_setup(n_elems, i_col, frac_col, nt_col, ntm1_col, nconst_col):
            RA = RQ[0:1, 0 * RW:1 * RW]
            RB = RQ[0:1, 1 * RW:2 * RW]
            RC = RQ[0:1, 2 * RW:3 * RW]
            RD2 = RQ[0:1, 3 * RW:4 * RW]
            vop(lambda v: v.tensor_scalar(out=RA, in0=RQ[0:1, 4 * RW:5 * RW], scalar1=0.0,
                                          scalar2=cin[0:1, C_200:C_200 + 1],
                                          op0=ALU.mult, op1=ALU.add))
            vop(lambda v: v.reciprocal(out=RB, in_=RA))
            vop(lambda v: v.tensor_scalar(out=RA, in0=RA, scalar1=0.0,
                                          scalar2=cin[0:1, C_ONE:C_ONE + 1],
                                          op0=ALU.mult, op1=ALU.add))
            vop(lambda v: v.tensor_tensor(out=RB, in0=RA, in1=RB, op=ALU.subtract))
            vop(lambda v, ncol=nconst_col: v.tensor_scalar(
                out=RC, in0=RA, scalar1=0.0, scalar2=cin[0:1, ncol:ncol + 1],
                op0=ALU.mult, op1=ALU.add))
            vop(lambda v: v.tensor_scalar(out=RD2, in0=RC, scalar1=-1.0, scalar2=None,
                                          op0=ALU.add))
            vop(lambda v: v.tensor_tensor(out=RB, in0=RB, in1=RD2, op=ALU.mult))
            vop(lambda v: v.tensor_copy(out=i32w[:], in_=RB))
            vop(lambda v: v.tensor_copy(out=RA, in_=i32w[:]))
            vop(lambda v: v.tensor_tensor(out=RD2, in0=RB, in1=RA, op=ALU.subtract))
            vop(lambda v: v.tensor_tensor(out=RC, in0=RC, in1=RA, op=ALU.subtract))
            spread_scl(RA[0:1, 0:1], i_col)
            spread_scl(RD2[0:1, 0:1], frac_col)
            spread_scl(RC[0:1, 0:1], nt_col)
            vspace()
            nvop(lambda v, ntc=nt_col, nmc=ntm1_col: v.tensor_scalar(
                out=scl[:, nmc:nmc + 1], in0=scl[:, ntc:ntc + 1], scalar1=-1.0, scalar2=None,
                op0=ALU.add))

        q1_slices = [(big1[:, c * T:(c + 1) * T], T) for c in range(DC)]
        q1_samp = [(big1[:, 0:DC * T:8], DC * T // 8)]
        samp_target1 = int(round((1.0 - 0.995) * (128 * DC * T // 8)))
        quantile(q1_slices, q1_samp, N1, samp_target1, SCL_NT, SCL_NTM1, SCL_FRAC, SCL_AMAX,
                 signed=True, tag="1")
        tp_am1 = tap("amax1", [1, 4])
        if tp_am1 is not None:
            wt("sync", "ve")
            dma("sync", "st", tp_am1[0:1, 0:1], gat[0:1, 28:29])
            dma("sync", "st", tp_am1[0:1, 1:2], scl[0:1, SCL_I:SCL_I + 1])
            dma("sync", "st", tp_am1[0:1, 2:3], scl[0:1, SCL_FRAC:SCL_FRAC + 1])
            dma("sync", "st", tp_am1[0:1, 3:4], scl[0:1, SCL_NT:SCL_NT + 1])
        tp_q1 = tap("q1dbg", [1, 8])
        if tp_q1 is not None:
            wt("sync", "ve")
            dma("sync", "st", tp_q1[0:1, 0:1], gat[0:1, 21:22])
            dma("sync", "st", tp_q1[0:1, 1:2], gat[0:1, 22:23])
            dma("sync", "st", tp_q1[0:1, 2:3], gat[0:1, 23:24])
            dma("sync", "st", tp_q1[0:1, 3:4], RQ[0:1, QV_LO * RW:QV_LO * RW + 1])
            dma("sync", "st", tp_q1[0:1, 4:5], RQ[0:1, QV_HI * RW:QV_HI * RW + 1])
            dma("sync", "st", tp_q1[0:1, 5:6], gat[0:1, 48:49])
            dma("sync", "st", tp_q1[0:1, 6:7], gat[0:1, 49:50])
            dma("sync", "st", tp_q1[0:1, 7:8], gat[0:1, 60:61])
            wt("vector", "st")

        vspace()
        nvop(lambda v: v.reciprocal(out=nrw[:, 3:4], in_=scl[:, SCL_AMAX:SCL_AMAX + 1]))
        nvop(lambda v: v.tensor_scalar(out=scl[:, SCL_S:SCL_S + 1], in0=nrw[:, 3:4],
                                       scalar1=cin[:, C_S127:C_S127 + 1], scalar2=None, op0=ALU.mult))
        nvop(lambda v: v.reciprocal(out=nrw[:, 3:4], in_=scl[:, SCL_S:SCL_S + 1]))
        nvop(lambda v: v.tensor_tensor(out=scl[:, SCL_GS:SCL_GS + 1], in0=scl[:, 12:13],
                                       in1=nrw[:, 3:4], op=ALU.mult))

        tp_fq = tap("fq", [128, DC * T])
        tp_fqd = tap("fqdbg", [128, 4 * TTW + 20])
        for c in range(DC):
            for p4 in range(4):
                if tp_fqd is not None and c == 0 and p4 == 0:
                    FH = big1[:, 0:TTW]
                    vop(lambda v, FH=FH: v.tensor_scalar(
                        out=big0[:, 0:TTW], in0=FH, scalar1=scl[:, SCL_S:SCL_S + 1],
                        scalar2=scl[:, SCL_SHIFT:SCL_SHIFT + 1], op0=ALU.mult, op1=ALU.add))
                    vop(lambda v: v.tensor_copy(out=ibuf[:, 0:TTW], in_=big0[:, 0:TTW]))
                    vop(lambda v: v.tensor_copy(out=big0[:, TTW:2 * TTW], in_=ibuf[:, 0:TTW]))
                    vop(lambda v: v.tensor_scalar(
                        out=ibuf[:, 0:TTW], in0=ibuf[:, 0:TTW], scalar1=cin[:, C_LO16:C_LO16 + 1],
                        scalar2=cin[:, C_HI16:C_HI16 + 1], op0=ALU.max, op1=ALU.min))
                    vop(lambda v: v.tensor_copy(out=big0[:, 2 * TTW:3 * TTW], in_=ibuf[:, 0:TTW]))
                    vop(lambda v: v.tensor_scalar(
                        out=big0[:, 3 * TTW:4 * TTW], in0=ibuf[:, 0:TTW],
                        scalar1=cin[:, C_16384:C_16384 + 1], scalar2=scl[:, SCL_GS:SCL_GS + 1],
                        op0=ALU.subtract, op1=ALU.mult))
                    vop(lambda v: v.tensor_copy(out=big0[:, 4 * TTW:4 * TTW + 20], in_=scl[:, 0:20]))
                    wt("sync", "ve")
                    dma("sync", "st", tp_fqd[:], big0[:, 0:4 * TTW + 20])
                    wt("vector", "st")
                FH = big1[:, c * T + p4 * TTW: c * T + (p4 + 1) * TTW]
                vop(lambda v, FH=FH: v.tensor_scalar(
                    out=big0[:, 0:TTW], in0=FH, scalar1=scl[:, SCL_S:SCL_S + 1],
                    scalar2=scl[:, SCL_SHIFT:SCL_SHIFT + 1], op0=ALU.mult, op1=ALU.add))
                vop(lambda v: v.tensor_copy(out=ibuf[:, 0:TTW], in_=big0[:, 0:TTW]))
                vop(lambda v: v.tensor_scalar(
                    out=ibuf[:, 0:TTW], in0=ibuf[:, 0:TTW], scalar1=cin[:, C_LO16:C_LO16 + 1],
                    scalar2=cin[:, C_HI16:C_HI16 + 1], op0=ALU.max, op1=ALU.min))
                vop(lambda v, c=c, p4=p4: v.tensor_scalar(
                    out=bigb[:, c * T + p4 * TTW: c * T + (p4 + 1) * TTW], in0=ibuf[:, 0:TTW],
                    scalar1=cin[:, C_16384:C_16384 + 1], scalar2=scl[:, SCL_GS:SCL_GS + 1],
                    op0=ALU.subtract, op1=ALU.mult))
        if tp_fq is not None:
            for c in range(DC):
                vop(lambda v, c=c: v.tensor_copy(out=big0[:, 0:T], in_=bigb[:, c * T:(c + 1) * T]))
                wt("sync", "ve")
                dma("sync", "st", tp_fq[:, c * T:(c + 1) * T], big0[:, 0:T])
                wt("vector", "st")

        wt("tensor", "ve")
        for tt in range(NTT):
            t0 = tt * TTW
            for mj in range(HC):
                wbuf = wsb0 if mj % 2 == 0 else wsb1
                wt("sync", "mm")
                sdma(wbuf[:, 0:DC * 128],
                     d_w1q.ap().rearrange("p (k h) -> p k h", k=DC)[:, :, mj * 128:(mj + 1) * 128])
                wt("tensor", "ld")
                wt("tensor", "ac", sems["ac"][1])
                for kc in range(DC):
                    mmop(lambda t, mj=mj, kc=kc, wbuf=wbuf, t0=t0: t.matmul(
                        ps_a[:], wbuf[:, kc * 128:(kc + 1) * 128],
                        bigb[:, kc * T + t0: kc * T + t0 + TTW],
                        start=(kc == 0), stop=(kc == DC - 1)))
                wt("scalar", "mm")
                wt("scalar", "ld")
                aop(lambda s, mj=mj: s.activation(out=sqs[:, 0:TTW], in_=ps_a[:], func=AF.Gelu))
                wt("sync", "ac")
                dma("sync", "ld", d_h2.ap()[:, mj * T + t0: mj * T + t0 + TTW], sqs[:, 0:TTW])

        wt("sync", "ve")
        wt("sync", "ld")
        for half, dst in ((0, big0), (1, big1)):
            for ch in range(8):
                src = d_h2.ap()[:, (half * 8 + ch) * T:(half * 8 + ch + 1) * T]
                sdma(dst[:, ch * T:(ch + 1) * T], src)
        wt("vector", "ld")
        for half, dst in ((0, big0), (1, big1)):
            for ch in range(8):
                CH = dst[:, ch * T:(ch + 1) * T]
                vop(lambda v, CH=CH: v.scalar_tensor_tensor(
                    out=CH, in0=CH, scalar=-1.0, in1=CH, op0=ALU.mult, op1=ALU.max))
        q2_slices = [(big0[:, c * T:(c + 1) * T], T) for c in range(DC)] + \
                    [(big1[:, c * T:(c + 1) * T], T) for c in range(DC)]
        q2_stream = [d_h2.ap()[:, ch * T + p * 1024: ch * T + (p + 1) * 1024]
                     for ch in range(16, 32) for p in range(2)]
        q2_samp = [(big0[:, 0:DC * T:16], DC * T // 16), (big1[:, 0:DC * T:16], DC * T // 16)]
        samp_target2 = int(round((1.0 - 0.995) * (128 * 2 * (DC * T // 16))))
        quantile(q2_slices, q2_samp, N2, samp_target2, SCL_NT, SCL_NTM1, SCL_FRAC, SCL_AMAX,
                 signed=False, stream=q2_stream, n_it=12)
        tp_am2 = tap("amax2", [1, 4])
        if tp_am2 is not None:
            wt("sync", "ve")
            dma("sync", "st", tp_am2[0:1, 0:1], gat[0:1, 28:29])
            dma("sync", "st", tp_am2[0:1, 1:2], scl[0:1, SCL_I:SCL_I + 1])
            dma("sync", "st", tp_am2[0:1, 2:3], scl[0:1, SCL_FRAC:SCL_FRAC + 1])
            dma("sync", "st", tp_am2[0:1, 3:4], scl[0:1, SCL_NT:SCL_NT + 1])

        vspace()
        nvop(lambda v: v.reciprocal(out=nrw[:, 3:4], in_=scl[:, SCL_AMAX:SCL_AMAX + 1]))
        nvop(lambda v: v.tensor_scalar(out=scl[:, SCL_S:SCL_S + 1], in0=nrw[:, 3:4],
                                       scalar1=cin[:, C_S127:C_S127 + 1], scalar2=None, op0=ALU.mult))
        nvop(lambda v: v.reciprocal(out=nrw[:, 3:4], in_=scl[:, SCL_S:SCL_S + 1]))
        nvop(lambda v: v.tensor_tensor(out=scl[:, SCL_GS:SCL_GS + 1], in0=scl[:, 14:15],
                                       in1=nrw[:, 3:4], op=ALU.mult))

        for tt in range(NTT):
            t0 = tt * TTW
            for hc2 in range(HC):
                wt("sync", "ve")
                sdma(wfp[:, 0:TTW], d_h2.ap()[:, hc2 * T + t0: hc2 * T + t0 + TTW])
                wt("vector", "ld")
                vop(lambda v, hc2=hc2: v.tensor_scalar(
                    out=sqs[:, 0:TTW], in0=wfp[:, 0:TTW], scalar1=scl[:, SCL_S:SCL_S + 1],
                    scalar2=scl[:, SCL_SHIFT:SCL_SHIFT + 1], op0=ALU.mult, op1=ALU.add))
                vop(lambda v, hc2=hc2: v.tensor_copy(out=ibuf[:, 0:TTW], in_=sqs[:, 0:TTW]))
                vop(lambda v, hc2=hc2: v.tensor_scalar(
                    out=ibuf[:, 0:TTW], in0=ibuf[:, 0:TTW], scalar1=cin[:, C_LO16:C_LO16 + 1],
                    scalar2=cin[:, C_HI16:C_HI16 + 1], op0=ALU.max, op1=ALU.min))
                vop(lambda v, hc2=hc2: v.tensor_scalar(
                    out=bigb[:, hc2 * TTW:(hc2 + 1) * TTW], in0=ibuf[:, 0:TTW],
                    scalar1=cin[:, C_16384:C_16384 + 1], scalar2=scl[:, SCL_GS:SCL_GS + 1],
                    op0=ALU.subtract, op1=ALU.mult))
            wt("sync", "ve")
            sdma(big0[:, 0:DC * TTW], d_x1.ap().rearrange("p (c t) -> p c t", c=DC)[:, :, t0:t0 + TTW])
            wt("tensor", "ve")
            for me in range(DC):
                for kg in range(4):
                    wbuf2 = wsb0 if (me * 4 + kg) % 2 == 0 else wsb1
                    wt("sync", "mm")
                    sdma(wbuf2[:, 0:8 * 128],
                         d_w2q.ap().rearrange("p (k e) -> p k e", k=HC)
                         [:, kg * 8:(kg + 1) * 8, me * 128:(me + 1) * 128])
                    wt("tensor", "ld")
                    wt("tensor", "ve", sems["ve"][1])
                    for kk in range(8):
                        kj = kg * 8 + kk
                        mmop(lambda t, me=me, kj=kj, kk=kk, wbuf2=wbuf2: t.matmul(
                            ps_b[:], wbuf2[:, kk * 128:(kk + 1) * 128],
                            bigb[:, kj * TTW:(kj + 1) * TTW],
                            start=(kj == 0), stop=(kj == HC - 1)))
                wt("vector", "mm")
                vop(lambda v, me=me: v.tensor_copy(
                    out=big0[:, DC * TTW + me * TTW: DC * TTW + (me + 1) * TTW], in_=ps_b[:]))
            wt("tensor", "ve")
            FOT = lambda c: big0[:, DC * TTW + c * TTW: DC * TTW + (c + 1) * TTW]
            for c in range(DC):
                mmop(lambda t, c=c: t.matmul(ps_s[:], onesc[:, 0:1], FOT(c),
                                             start=(c == 0), stop=(c == DC - 1)))
            wt("vector", "mm")
            vop(lambda v: v.tensor_copy(out=RQ[0:1, 0:TTW], in_=ps_s[0:1, :]))
            for c in range(DC):
                wt("scalar", "ve")
                wt("scalar", "mm")
                aop(lambda s, c=c: s.activation(out=sqs[:, 0:TTW],
                                                in_=FOT(c), func=AF.Square))
                wt("tensor", "ac")
                mmop(lambda t, c=c: t.matmul(ps_s[:], onesc[:, 0:1], sqs[:, 0:TTW],
                                             start=(c == 0), stop=(c == DC - 1)))
            wt("vector", "mm")
            vop(lambda v: v.tensor_copy(out=RQ[0:1, 512:512 + TTW], in_=ps_s[0:1, :]))
            S1s = RQ[0:1, 0:TTW]
            S2s = RQ[0:1, 512:512 + TTW]
            vop(lambda v: v.tensor_scalar(out=S1s, in0=S1s, scalar1=cin[:1, C_DINV:C_DINV + 1],
                                          scalar2=None, op0=ALU.mult))
            vop(lambda v: v.tensor_scalar(out=S2s, in0=S2s, scalar1=cin[:1, C_DINV:C_DINV + 1],
                                          scalar2=None, op0=ALU.mult))
            vop(lambda v: v.tensor_tensor(out=RQ[0:1, 1024:1024 + TTW], in0=S1s, in1=S1s, op=ALU.mult))
            vop(lambda v: v.tensor_tensor(out=S2s, in0=S2s, in1=RQ[0:1, 1024:1024 + TTW], op=ALU.subtract))
            vop(lambda v: v.tensor_scalar(out=RQ[0:1, 1024:1024 + TTW], in0=S2s,
                                          scalar1=EPS, scalar2=None, op0=ALU.add))
            vop(lambda v: v.reciprocal(out=S2s, in_=RQ[0:1, 1024:1024 + TTW]))
            wt("scalar", "ve")
            aop(lambda s: s.activation(out=S2s, in_=S2s, func=AF.Sqrt))
            wt("vector", "ac")
            vop(lambda v: v.tensor_tensor(out=S1s, in0=S1s, in1=S2s, op=ALU.mult))
            vop(lambda v: v.tensor_scalar(out=S1s, in0=S1s, scalar1=-1.0, scalar2=None, op0=ALU.mult))
            bcast_row_to_tile(S2s, TTW, bc_u[:])
            bcast_row_to_tile(S1s, TTW, bc_v[:])
            wt("vector", "ld")
            for c in range(DC):
                vop(lambda v, c=c: v.tensor_tensor(out=FOT(c), in0=FOT(c), in1=bc_u[:], op=ALU.mult))
                vop(lambda v, c=c: v.tensor_tensor(out=FOT(c), in0=FOT(c), in1=bc_v[:], op=ALU.add))
                vop(lambda v, c=c: v.tensor_tensor(out=FOT(c), in0=FOT(c),
                                                   in1=big0[:, c * TTW:(c + 1) * TTW], op=ALU.add))
            wt("tensor", "ve")
            tve_o = [sems["ve"][1], sems["ve"][1]]
            for sub in range(NTT):
                wt("vector", "st")
                for c in range(DC):
                    bank = ps_t[:] if c % 2 == 0 else ps_bc[:, 0:128]
                    wt("tensor", "ve", tve_o[c % 2])
                    mmop(lambda t, sub=sub, c=c, bank=bank: t.transpose(
                        bank, FOT(c)[:, sub * 128: sub * 128 + 128], ident[:]))
                    wt("vector", "mm")
                    vop(lambda v, sub=sub, c=c, bank=bank: v.tensor_copy(
                        out=wfp[:, c * 128:(c + 1) * 128], in_=bank))
                    tve_o[c % 2] = sems["ve"][1]
                wt("sync", "ve")
                dma("sync", "st", dy[t0 + sub * 128: t0 + sub * 128 + 128, :], wfp[:, 0:D])

        add("sync", lambda s: s.wait_ge(sems["st"][0], sems["st"][1]))
        add("sync", lambda s: s.wait_ge(sems["ld"][0], sems["ld"][1]))

        with nc.Block() as block:
            @block.sync
            def _(eng):
                for fn in ops["sync"]:
                    fn(eng)

            @block.vector
            def _(eng):
                for fn in ops["vector"]:
                    fn(eng)

            @block.scalar
            def _(eng):
                for fn in ops["scalar"]:
                    fn(eng)

            @block.tensor
            def _(eng):
                for fn in ops["tensor"]:
                    fn(eng)

            @block.gpsimd
            def _(eng):
                for fn in ops["gpsimd"]:
                    fn(eng)

    return nc, tap_outs


def make_consts():
    c = np.zeros(NCONST, np.float32)
    c[C_200] = 200.0
    c[C_ONE] = 1.0
    c[C_N1] = float(N1)
    c[C_N2] = float(N2)
    c[C_P23] = 8388608.0
    c[C_HALF] = 0.5
    c[C_D] = float(D)
    c[C_S127] = 127.0
    c[C_ZERO] = 0.0
    c[C_16384] = 16384.0
    c[C_LO16] = 16384.0 - 128.0
    c[C_HI16] = 16384.0 + 127.0
    c[C_RS1] = 0.0
    c[C_RS2] = 0.0
    c[C_NW1] = 1.0 / (D * H)
    c[C_NW2] = 1.0 / (D * H)
    c[C_EPSC] = EPS
    c[C_DINV] = 1.0 / D
    return np.broadcast_to(c, (128, NCONST)).copy()


def prep_inputs(L0_attn_x, L0_attn_w_xproj, L0_attn_b_xproj, L0_attn_A,
                L0_attn_D_param, L0_attn_w_out, L0_attn_b_out, L0_attn_gamma,
                L0_attn_beta, L0_ffn_ln_g, L0_ffn_ln_b, L0_ffn_w1, L0_ffn_w2):
    f = np.ascontiguousarray

    def fm(v):
        return f(np.asarray(v, np.float32).reshape(DC, 128).T)

    shared = {
        "wxT": f(np.asarray(L0_attn_w_xproj, np.float32).T),
        "woT": f(np.asarray(L0_attn_w_out, np.float32).T),
        "w1T": f(np.asarray(L0_ffn_w1, np.float32).T),
        "w2T": f(np.asarray(L0_ffn_w2, np.float32).T),
        "bxp": f(np.asarray(L0_attn_b_xproj, np.float32).reshape(EC, 128).T),
        "afm": f(np.asarray(L0_attn_A, np.float32).reshape(DC, 128, 16)
                 .transpose(1, 0, 2).reshape(128, DC * 16)),
        "vecs": f(np.concatenate([fm(L0_attn_D_param), fm(L0_attn_gamma), fm(L0_attn_beta),
                                  fm(L0_ffn_ln_g), fm(L0_ffn_ln_b), fm(L0_attn_b_out)], axis=1)),
        "ident": np.eye(128, dtype=np.float32),
        "consts": make_consts(),
    }
    xs = np.asarray(L0_attn_x, np.float32)
    return [dict(shared, x=f(xs[c])) for c in range(8)]


def kernel(**inputs):
    in_maps = prep_inputs(**inputs)
    nc, _ = build_program()
    res = run_bass_kernel_spmd(nc, in_maps, core_ids=list(range(8)))
    return np.stack([res.results[c]["y"] for c in range(8)], axis=0)


# version 07
# speedup vs baseline: 1.6404x; 1.0500x over previous
import contextlib
import math
from collections import defaultdict

import numpy as np

import concourse.bass as bass
import concourse.mybir as mybir
from concourse.bass_utils import run_bass_kernel_spmd

F32 = mybir.dt.float32
BF16 = mybir.dt.bfloat16
I16 = mybir.dt.int16
I32 = mybir.dt.int32
AF = mybir.ActivationFunctionType
ALU = mybir.AluOpType
AX = mybir.AxisListType.X

T = 2048
D = 1024
DC = D // 128
E2 = 2048
EC = E2 // 128
H = 4096
HC = H // 128
NTT = 4
TTW = T // NTT

N1 = 8 * T * D
N2 = 8 * T * H

RG = [[0, 1, 2, 3, 4, 5, 6, 7]]


def ceil_div(a, b):
    return (a + b - 1) // b
BIG = 1.0e30
EPS = 1.0e-5
KTAY = 8

B_ILL = 12

C_200, C_ONE, C_N1, C_N2, C_P23, C_HALF, C_D, C_S127, C_ZERO, C_16384, \
    C_LO16, C_HI16, C_G1HALFSC, C_RS1, C_RS2, C_NW1, C_NW2, C_EPSC, C_DINV, C_HLFD = range(20)
NCONST = 24


def build_program(taps=(), n_cores=8):
    taps = set(taps)
    nc = bass.Bass("TRN2", target_bir_lowering=False, debug=False,
                   detect_race_conditions=False, dynamic_dma_scratch_size=4096)

    dx = nc.dram_tensor("x", [T, D], F32, kind="ExternalInput").ap()
    dwxT = nc.dram_tensor("wxT", [D, E2], F32, kind="ExternalInput").ap()
    dwoT = nc.dram_tensor("woT", [D, D], F32, kind="ExternalInput").ap()
    dw1T = nc.dram_tensor("w1T", [D, H], F32, kind="ExternalInput").ap()
    dw2T = nc.dram_tensor("w2T", [H, D], F32, kind="ExternalInput").ap()
    dbxp = nc.dram_tensor("bxp", [128, EC], F32, kind="ExternalInput").ap()
    dafm = nc.dram_tensor("afm", [128, DC * 16], F32, kind="ExternalInput").ap()
    dvecs = nc.dram_tensor("vecs", [128, 6 * DC], F32, kind="ExternalInput").ap()
    dident = nc.dram_tensor("ident", [128, 128], F32, kind="ExternalInput").ap()
    dconsts = nc.dram_tensor("consts", [128, NCONST], F32, kind="ExternalInput").ap()
    dy = nc.dram_tensor("y", [T, D], F32, kind="ExternalOutput").ap()

    d_x0 = nc.dram_tensor("d_x0", [128, DC * T], F32, kind="Internal")
    d_x1 = nc.dram_tensor("d_x1", [128, DC * T], F32, kind="Internal")
    d_h2 = nc.dram_tensor("d_h2", [128, HC * T], F32, kind="Internal")
    d_wxb = nc.dram_tensor("d_wxb", [128, DC * E2], BF16, kind="Internal")
    d_wob = nc.dram_tensor("d_wob", [128, DC * D], BF16, kind="Internal")
    d_w1q = nc.dram_tensor("d_w1q", [128, DC * H], BF16, kind="Internal")
    d_w2q = nc.dram_tensor("d_w2q", [128, HC * D], BF16, kind="Internal")

    cc_n = [0]

    def new_cc(width):
        i = cc_n[0]
        cc_n[0] += 1
        ci = nc.dram_tensor(f"cci{i}", [1, width], F32, kind="Internal")
        co = nc.dram_tensor(f"cco{i}", [n_cores, width], F32, kind="Internal")
        return ci, co

    tap_outs = {}

    def tap(name, shape):
        if name in taps:
            tap_outs[name] = nc.dram_tensor("tap_" + name, shape, F32,
                                            kind="ExternalOutput").ap()
            return tap_outs[name]
        return None

    with contextlib.ExitStack() as ctx:
        sb = lambda n_, s_, dt=F32: ctx.enter_context(nc.sbuf_tensor(n_, s_, dt))
        psum = lambda n_, s_: ctx.enter_context(nc.psum_tensor(n_, s_, F32))

        big0 = sb("big0", [128, DC * T])
        big1 = sb("big1", [128, DC * T])
        bigb = sb("bigb", [128, DC * T], BF16)
        wsb0 = sb("wsb0", [128, 1024], BF16)
        wsb1 = sb("wsb1", [128, 1024], BF16)
        wfp = sb("wfp", [128, 2048])
        ibuf = sb("ibuf", [128, 512], I16)
        sqs = sb("sqs", [128, 1024])
        bc_s = sb("bc_s", [128, TTW])
        bc_t = sb("bc_t", [128, TTW])
        ones512 = sb("ones512", [128, 512])
        ident = sb("identt", [128, 128])
        onesc = sb("onesct", [128, 128])
        ones1 = sb("ones1t", [1, 128])
        cin = sb("cint", [128, NCONST])
        bxp = sb("bxpt", [128, EC])
        afm = sb("afmt", [128, DC * 16])
        vecs = sb("vecst", [128, 6 * DC])
        coef = sb("coeft", [128, KTAY * DC])
        carry = sb("carryt", [128, DC])
        acc = sb("acct", [128, 64])
        nrw = sb("nrwt", [128, 8])
        jnk = sb("jnkt", [128, 64])
        scl = sb("sclt", [128, 20])
        gat = sb("gatt", [1, 256])
        rows = sb("rowst", [1, 5632])

        sc_e = big1[:, 24 * 512:25 * 512]
        bc_u = big1[:, 25 * 512:26 * 512]
        bc_v = big1[:, 26 * 512:27 * 512]

        ps_a = psum("ps_a", [128, 512])
        ps_b = psum("ps_b", [128, 512])
        ps_t = psum("ps_t", [128, 128])
        ps_s = psum("ps_s", [1, 512])
        ps_bc = psum("ps_bc", [128, 512])
        ps_c = psum("ps_c", [128, 1])

        R_S1 = big0[0:1, 8192 + 0 * T:8192 + 1 * T]
        R_S2 = big0[0:1, 8192 + 1 * T:8192 + 2 * T]
        R_SS = big0[0:1, 8192 + 2 * T:8192 + 3 * T]
        R_TT = big0[0:1, 8192 + 3 * T:8192 + 4 * T]
        R_T1 = R_S1
        R_R1 = R_S2
        RQ = rows[0:1, 0:5632]
        RW = 512

        def Q(i):
            return RQ[0:1, i * RW:(i + 1) * RW]

        def Q1(i):
            return RQ[0:1, i * RW:i * RW + 1]

        QV_LO, QV_HI, QV_MID, QV_CNT, QV_GE, QV_T1, QV_T2, QV_RES, \
            QV_CLO, QV_CHI = range(10)
        QV_TGT = QV_RES

        SCL_MID, SCL_M1, SCL_S, SCL_GS, SCL_I, SCL_FRAC, SCL_NT, SCL_NTM1, \
            SCL_AMAX, SCL_T2, SCL_TMP, SCL_SHIFT = range(12)
        SCL_NEG = 16

        sems = {}
        for name in ["ld", "gd", "mm", "ve", "ac", "cc", "st"]:
            sems[name] = [ctx.enter_context(nc.semaphore("s_" + name)), 0]

        ops = defaultdict(list)

        def add(e, fn):
            ops[e].append(fn)

        def _mk(e, key):
            def op(fn):
                sem = sems[key][0]

                def w(eng):
                    fn(eng).then_inc(sem, 1)
                add(e, w)
                sems[key][1] += 1
            return op

        vop = _mk("vector", "ve")
        aop = _mk("scalar", "ac")
        mmop = _mk("tensor", "mm")
        gop = _mk("gpsimd", "ve")

        def wt(e, key, val=None):
            sem, v = sems[key][0], sems[key][1]
            if val is not None:
                v = val
            add(e, lambda eng: eng.wait_ge(sem, v))

        def dma(e, key, out, in_):
            sem = sems[key][0]

            def w(eng):
                eng.dma_start(out=out, in_=in_).then_inc(sem, 16)
            add(e, w)
            sems[key][1] += 16

        def sdma(out, in_):
            dma("sync", "ld", out, in_)

        def vspace():
            vop(lambda v: v.memset(sqs[:, 512:1024], 0.0))
            vop(lambda v: v.memset(sqs[:, 512:1024], 0.0))

        def nvop(fn):
            vop(fn)
            vspace()

        def apad():
            aop(lambda s: s.activation(out=jnk[:, 0:64], in_=jnk[:, 0:64], func=AF.Identity))
            aop(lambda s: s.activation(out=jnk[:, 0:64], in_=jnk[:, 0:64], func=AF.Identity))

        def gdma(out, in_):
            dma("gpsimd", "gd", out, in_)
            wt("gpsimd", "gd")

        def allgather(src_ap, width, dst_ap):
            ci, co = new_cc(width)
            wt("gpsimd", "ve")
            wt("gpsimd", "ld")
            gdma(ci.ap()[:], src_ap)
            sem = sems["cc"][0]

            def w(g):
                g.collective_compute("AllGather", ALU.bypass, replica_groups=RG,
                                     ins=[ci.ap()[:]], outs=[co.ap()[:]]).then_inc(sem, 1)
            add("gpsimd", w)
            sems["cc"][1] += 1
            wt("gpsimd", "cc")
            gdma(dst_ap, co.ap().rearrange("a b -> (a b)")[None, :])
            wt("vector", "gd")

        def bcast_row_to_tile(row_ap, width, dst):
            wt("tensor", "ve")
            wt("tensor", "ld")
            mmop(lambda t: t.matmul(ps_bc[:, 0:width], ones1[:], row_ap,
                                    start=True, stop=True))
            wt("vector", "mm")
            vop(lambda v: v.tensor_copy(out=dst, in_=ps_bc[:, 0:width]))

        def spread_scl(row11, col):
            wt("tensor", "ve")
            wt("tensor", "ac")
            mmop(lambda t: t.matmul(ps_c[:], ones1[:], row11, start=True, stop=True))
            wt("scalar", "mm")
            aop(lambda s: s.activation(out=nrw[:, 0:1], in_=ps_c[:], func=AF.Identity))
            apad()
            wt("vector", "ac")
            vop(lambda v, col=col: v.tensor_scalar(out=scl[:, col:col + 1], in0=nrw[:, 0:1],
                                                   scalar1=1.0, scalar2=None, op0=ALU.mult))
            vop(lambda v: v.tensor_scalar(out=scl[:, SCL_NEG:SCL_NEG + 1], in0=nrw[:, 0:1],
                                          scalar1=-1.0, scalar2=None, op0=ALU.mult))
            vspace()

        sdma(big1.rearrange("p (a d) -> p a d", a=16), dx.rearrange("(a p) d -> p a d", p=128))
        sdma(ident[:], dident)
        sdma(cin[:], dconsts)
        sdma(bxp[:], dbxp)
        sdma(afm[:], dafm)
        sdma(vecs[:], dvecs)
        wt("vector", "ld")

        vop(lambda v: v.memset(rows[:], 0.0))
        vop(lambda v: v.memset(scl[:], 0.0))
        vop(lambda v: v.memset(acc[:], 0.0))
        vop(lambda v: v.memset(gat[:], 0.0))
        vop(lambda v: v.memset(onesc[:], 1.0))
        vop(lambda v: v.memset(ones512[:], 1.0))
        vop(lambda v: v.memset(ones1[:], 1.0))
        vop(lambda v: v.memset(carry[:], 0.0))
        nvop(lambda v: v.tensor_scalar(out=scl[:, SCL_TMP:SCL_TMP + 1], in0=cin[:, C_P23:C_P23 + 1],
                                       scalar1=cin[:, C_HALF:C_HALF + 1], scalar2=None, op0=ALU.add))
        nvop(lambda v: v.tensor_tensor(out=scl[:, SCL_TMP:SCL_TMP + 1], in0=scl[:, SCL_TMP:SCL_TMP + 1],
                                       in1=cin[:, C_P23:C_P23 + 1], op=ALU.subtract))
        nvop(lambda v: v.tensor_scalar(out=scl[:, SCL_SHIFT:SCL_SHIFT + 1], in0=scl[:, SCL_TMP:SCL_TMP + 1],
                                       scalar1=cin[:, C_16384:C_16384 + 1], scalar2=None, op0=ALU.add))

        vop(lambda v: v.tensor_copy(out=sqs[:, 0:DC * 16], in_=afm[:]))
        for k in range(1, KTAY + 1):
            if k > 1:
                vop(lambda v: v.tensor_tensor(out=sqs[:, 0:DC * 16], in0=sqs[:, 0:DC * 16],
                                              in1=afm[:], op=ALU.mult))
            vop(lambda v, k=k: v.tensor_reduce(
                out=coef[:, (k - 1) * DC:k * DC],
                in_=sqs[:, 0:DC * 16].rearrange("p (c s) -> p c s", s=16),
                axis=AX, op=ALU.add))
        nvop(lambda v: v.tensor_scalar(out=scl[:, SCL_TMP:SCL_TMP + 1], in0=cin[:, C_ONE:C_ONE + 1],
                                       scalar1=1.0, scalar2=None, op0=ALU.mult))
        for k in range(2, KTAY + 1):
            nvop(lambda v, k=k: v.tensor_scalar(out=scl[:, SCL_TMP:SCL_TMP + 1],
                                                in0=scl[:, SCL_TMP:SCL_TMP + 1],
                                                scalar1=float(k), scalar2=None, op0=ALU.mult))
            nvop(lambda v: v.reciprocal(out=nrw[:, 1:2], in_=scl[:, SCL_TMP:SCL_TMP + 1]))
            nvop(lambda v, k=k: v.tensor_scalar(out=coef[:, (k - 1) * DC:k * DC],
                                                in0=coef[:, (k - 1) * DC:k * DC],
                                                scalar1=nrw[:, 1:2], scalar2=None, op0=ALU.mult))


        def ln_rows():
            A_, B_, C_, D_ = R_S1, R_S2, R_SS, R_TT
            vop(lambda v: v.tensor_scalar(out=A_, in0=A_, scalar1=cin[:1, C_DINV:C_DINV + 1],
                                          scalar2=None, op0=ALU.mult))
            vop(lambda v: v.tensor_scalar(out=B_, in0=B_, scalar1=cin[:1, C_DINV:C_DINV + 1],
                                          scalar2=None, op0=ALU.mult))
            vop(lambda v: v.tensor_tensor(out=D_, in0=A_, in1=A_, op=ALU.mult))
            vop(lambda v: v.tensor_tensor(out=B_, in0=B_, in1=D_, op=ALU.subtract))
            vop(lambda v: v.tensor_scalar(out=D_, in0=B_, scalar1=EPS, scalar2=None,
                                          op0=ALU.add))
            vop(lambda v: v.reciprocal(out=C_, in_=D_))
            wt("scalar", "ve")
            aop(lambda s: s.activation(out=C_, in_=C_, func=AF.Sqrt))
            wt("vector", "ac")
            vop(lambda v: v.tensor_tensor(out=D_, in0=C_, in1=C_, op=ALU.mult))
            vop(lambda v: v.tensor_tensor(out=D_, in0=D_, in1=B_, op=ALU.mult))
            vop(lambda v: v.tensor_copy(out=B_, in_=C_))
            vop(lambda v: v.tensor_scalar(out=D_, in0=D_, scalar1=EPS, scalar2=None,
                                          op0=ALU.add))
            vop(lambda v: v.reciprocal(out=C_, in_=D_))
            wt("scalar", "ve")
            aop(lambda s: s.activation(out=D_, in_=C_, func=AF.Sqrt))
            wt("vector", "ac")
            vop(lambda v: v.tensor_tensor(out=C_, in0=B_, in1=D_, op=ALU.mult))
            vop(lambda v: v.tensor_tensor(out=A_, in0=A_, in1=B_, op=ALU.mult))
            vop(lambda v: v.tensor_scalar(out=A_, in0=A_, scalar1=-1.0,
                                          scalar2=None, op0=ALU.mult))
            vop(lambda v: v.tensor_tensor(out=D_, in0=A_, in1=D_, op=ALU.mult))

        wt("tensor", "ve")
        wt("tensor", "ld")
        tve_tp = [sems["ve"][1], sems["ve"][1]]
        for idx_tp in range(16 * DC):
            tile, dc2 = divmod(idx_tp, DC)
            bank = ps_t[:] if idx_tp % 2 == 0 else ps_bc[:, 0:128]
            wt("tensor", "ve", tve_tp[idx_tp % 2])
            mmop(lambda t, tile=tile, dc2=dc2, bank=bank: t.transpose(
                bank, big1[:, tile * D + dc2 * 128: tile * D + dc2 * 128 + 128], ident[:]))
            wt("vector", "mm")
            vop(lambda v, tile=tile, dc2=dc2, bank=bank: v.tensor_copy(
                out=big0[:, dc2 * T + tile * 128: dc2 * T + tile * 128 + 128], in_=bank))
            tve_tp[idx_tp % 2] = sems["ve"][1]
        wt("sync", "ve")
        sdma(d_x0.ap()[:], big0[:])
        tp_x0 = tap("x0fm", [128, DC * T])
        if tp_x0 is not None:
            dma("sync", "st", tp_x0[:], big0[:])
        for tile in range(16):
            vop(lambda v, tile=tile: v.tensor_reduce(
                out=acc[:, 2 * tile:2 * tile + 1], in_=big1[:, tile * D:(tile + 1) * D],
                axis=AX, op=ALU.add))
            wt("scalar", "ve")
            aop(lambda s, tile=tile: s.activation(
                out=sqs[:, 0:1024], in_=big1[:, tile * D:(tile + 1) * D], func=AF.Square))
            wt("vector", "ac")
            vop(lambda v, tile=tile: v.tensor_reduce(
                out=acc[:, 2 * tile + 1:2 * tile + 2], in_=sqs[:, 0:1024], axis=AX, op=ALU.add))
            wt("sync", "ve")
            wt("sync", "ld")
            sdma(R_S1[0:1, tile * 128:(tile + 1) * 128], acc[:, 2 * tile:2 * tile + 1])
            sdma(R_S2[0:1, tile * 128:(tile + 1) * 128], acc[:, 2 * tile + 1:2 * tile + 2])
        wt("vector", "ld")
        tp_rraw = tap("rows_raw", [1, 2 * T])
        if tp_rraw is not None:
            wt("sync", "ve")
            dma("sync", "st", tp_rraw[0:1, 0:T], R_S1)
            dma("sync", "st", tp_rraw[0:1, T:2 * T], R_S2)
            wt("vector", "st")
        ln_rows()
        tp_rows = tap("rows_ln", [1, 4 * T])
        if tp_rows is not None:
            wt("sync", "ve")
            wt("sync", "ac")
            dma("sync", "st", tp_rows[0:1, 0:T], R_S1)
            dma("sync", "st", tp_rows[0:1, T:2 * T], R_S2)
            dma("sync", "st", tp_rows[0:1, 2 * T:3 * T], R_SS)
            dma("sync", "st", tp_rows[0:1, 3 * T:4 * T], R_TT)
            wt("vector", "st")

        for kc in range(DC):
            wt("sync", "ve")
            wt("sync", "ld")
            sdma(wfp[:, 0:2048], dwxT[kc * 128:(kc + 1) * 128, :])
            wt("vector", "ld")
            vop(lambda v: v.tensor_copy(out=bigb[:, 0:2048], in_=wfp[:, 0:2048]))
            wt("sync", "ve")
            sdma(d_wxb.ap()[:, kc * E2:(kc + 1) * E2], bigb[:, 0:2048])
            sdma(wfp[:, 0:1024], dwoT[kc * 128:(kc + 1) * 128, :])
            wt("vector", "ld")
            vop(lambda v: v.tensor_copy(out=bigb[:, 2048:3072], in_=wfp[:, 0:1024]))
            wt("sync", "ve")
            sdma(d_wob.ap()[:, kc * D:(kc + 1) * D], bigb[:, 2048:3072])

        def wprep(dsrc, ddst, n_chunks, width, acc_col, gcol, nwcol):
            pieces = ceil_div(width, 2048)
            pw = width // pieces
            ncols = n_chunks * pieces
            for ch in range(n_chunks):
                for pc in range(pieces):
                    j = ch * pieces + pc
                    wt("sync", "ve")
                    sdma(wfp[:, 0:pw], dsrc[ch * 128:(ch + 1) * 128, pc * pw:(pc + 1) * pw])
                    wt("vector", "ld")
                    vop(lambda v: v.scalar_tensor_tensor(
                        out=big1[:, 0:pw], in0=wfp[:, 0:pw], scalar=-1.0,
                        in1=wfp[:, 0:pw], op0=ALU.mult, op1=ALU.max))
                    vop(lambda v, j=j: v.tensor_reduce(out=acc[:, acc_col + j:acc_col + j + 1],
                                                       in_=big1[:, 0:pw], axis=AX, op=ALU.add))
            vspace()
            vop(lambda v: v.tensor_reduce(out=acc[:, acc_col + ncols:acc_col + ncols + 1],
                                          in_=acc[:, acc_col:acc_col + ncols], axis=AX, op=ALU.add))
            vspace()
            wt("tensor", "ve")
            wt("tensor", "ac")
            mmop(lambda t: t.matmul(ps_c[:], onesc[:], acc[:, acc_col + ncols:acc_col + ncols + 1],
                                    start=True, stop=True))
            wt("vector", "mm")
            nvop(lambda v: v.tensor_scalar(out=scl[:, gcol:gcol + 1], in0=ps_c[:],
                                           scalar1=cin[:, nwcol:nwcol + 1],
                                           scalar2=cin[:, C_EPSC:C_EPSC + 1],
                                           op0=ALU.mult, op1=ALU.add))
            nvop(lambda v: v.tensor_scalar(out=scl[:, gcol + 1:gcol + 2], in0=scl[:, gcol:gcol + 1],
                                           scalar1=0.5, scalar2=None, op0=ALU.mult))
            for ch in range(n_chunks):
                for pc in range(pieces):
                    wt("sync", "ve")
                    wt("sync", "ac")
                    sdma(wfp[:, 0:pw], dsrc[ch * 128:(ch + 1) * 128, pc * pw:(pc + 1) * pw])
                    wt("vector", "ld")
                    wt("scalar", "ld")
                    wt("scalar", "ve")
                    aop(lambda sx: sx.activation(out=big0[:, 0:pw], in_=wfp[:, 0:pw], func=AF.Sign))
                    vop(lambda v: v.scalar_tensor_tensor(
                        out=big1[:, 0:pw], in0=wfp[:, 0:pw], scalar=-1.0,
                        in1=wfp[:, 0:pw], op0=ALU.mult, op1=ALU.max))
                    vop(lambda v: v.tensor_scalar(out=big1[:, 0:pw], in0=big1[:, 0:pw],
                                                  scalar1=scl[:, gcol + 1:gcol + 2], scalar2=None,
                                                  op0=ALU.is_gt))
                    wt("vector", "ac")
                    vop(lambda v: v.tensor_tensor(out=bigb[:, 0:pw],
                                                  in0=big1[:, 0:pw], in1=big0[:, 0:pw], op=ALU.mult))
                    wt("sync", "ve")
                    sdma(ddst.ap()[:, ch * width + pc * pw:ch * width + (pc + 1) * pw], bigb[:, 0:pw])

        wprep(dw1T, d_w1q, DC, H, 8, 12, C_NW1)
        wprep(dw2T, d_w2q, HC, D, 25, 14, C_NW2)

        tp_xn = tap("xn", [128, DC * T])
        tp_dl = tap("delta", [128, DC * T])
        tp_gate = tap("gate", [128, DC * T])
        tp_h = tap("hcs", [128, DC * T])
        tp_ao = tap("attn_out", [128, DC * T])
        tp_x1 = tap("x1", [128, DC * T])

        X0T = lambda c: big0[:, c * TTW:(c + 1) * TTW]
        AOT = lambda c: big0[:, DC * TTW + c * TTW: DC * TTW + (c + 1) * TTW]
        XNT = lambda c: big1[:, c * TTW:(c + 1) * TTW]
        XNB = lambda c: bigb[:, c * TTW:(c + 1) * TTW]
        sc_a = big1[:, DC * TTW:2 * DC * TTW]
        sc_b = big1[:, 2 * DC * TTW:3 * DC * TTW]
        mixb = bigb[:, DC * TTW:2 * DC * TTW]

        for tt in range(NTT):
            t0 = tt * TTW
            wt("sync", "ve")
            wt("sync", "ld")
            sdma(big0[:, 0:DC * TTW], d_x0.ap().rearrange("p (c t) -> p c t", c=DC)[:, :, t0:t0 + TTW])
            bcast_row_to_tile(R_SS[0:1, t0:t0 + TTW], TTW, bc_s[:])
            bcast_row_to_tile(R_TT[0:1, t0:t0 + TTW], TTW, bc_t[:])
            bcast_row_to_tile(R_R1[0:1, t0:t0 + TTW], TTW, bc_u[:])
            bcast_row_to_tile(R_T1[0:1, t0:t0 + TTW], TTW, bc_v[:])
            wt("vector", "ld")
            for c in range(DC):
                vop(lambda v, c=c: v.tensor_tensor(out=XNT(c), in0=X0T(c), in1=bc_s[:], op=ALU.mult))
                vop(lambda v, c=c: v.tensor_tensor(out=XNT(c), in0=XNT(c), in1=bc_t[:], op=ALU.add))
                vop(lambda v, c=c: v.tensor_scalar(out=XNT(c), in0=XNT(c),
                                                   scalar1=vecs[:, DC + c:DC + c + 1],
                                                   scalar2=vecs[:, 2 * DC + c:2 * DC + c + 1],
                                                   op0=ALU.mult, op1=ALU.add))
                vop(lambda v, c=c: v.tensor_copy(out=XNB(c), in_=XNT(c)))
            if tp_xn is not None:
                wt("sync", "ve")
                for c in range(DC):
                    dma("sync", "st", tp_xn[:, c * T + t0:c * T + t0 + TTW], XNT(c))

            wt("vector", "mm")
            wt("tensor", "ve")
            for mc in range(EC):
                wbuf = wsb0 if mc % 2 == 0 else wsb1
                wt("sync", "mm")
                sdma(wbuf[:, 0:DC * 128],
                     d_wxb.ap().rearrange("p (k e) -> p k e", k=DC)[:, :, mc * 128:(mc + 1) * 128])
                wt("tensor", "ld")
                wt("tensor", "ac", sems["ac"][1])
                for kc in range(DC):
                    mmop(lambda t, mc=mc, kc=kc, wbuf=wbuf: t.matmul(
                        ps_a[:], wbuf[:, kc * 128:(kc + 1) * 128], XNB(kc),
                        start=(kc == 0), stop=(kc == DC - 1)))
                wt("scalar", "mm")
                if mc < DC:
                    aop(lambda s, mc=mc: s.activation(
                        out=sc_e[:], in_=ps_a[:], func=AF.Exp,
                        bias=bxp[:, mc:mc + 1], scale=1.0))
                    aop(lambda s, mc=mc: s.activation(
                        out=sc_a[:, mc * TTW:(mc + 1) * TTW], in_=sc_e[:],
                        func=AF.Ln, bias=1.0, scale=1.0))
                else:
                    aop(lambda s, mc=mc: s.activation(
                        out=sc_b[:, (mc - DC) * TTW:(mc - DC + 1) * TTW], in_=ps_a[:],
                        func=AF.Identity, bias=bxp[:, mc:mc + 1], scale=1.0))
            if tp_dl is not None:
                wt("sync", "ac")
                for c in range(DC):
                    dma("sync", "st", tp_dl[:, c * T + t0:c * T + t0 + TTW],
                        sc_a[:, c * TTW:(c + 1) * TTW])

            wt("vector", "ac")
            for c in range(DC):
                DL = sc_a[:, c * TTW:(c + 1) * TTW]
                BGV = sc_b[:, c * TTW:(c + 1) * TTW]
                vop(lambda v, c=c, DL=DL: v.tensor_scalar(
                    out=sc_e[:], in0=DL,
                    scalar1=coef[:, (KTAY - 1) * DC + c:(KTAY - 1) * DC + c + 1],
                    scalar2=None, op0=ALU.mult))
                for k in range(KTAY - 1, 0, -1):
                    vop(lambda v, c=c, k=k, DL=DL: v.scalar_tensor_tensor(
                        out=sc_e[:], in0=sc_e[:],
                        scalar=coef[:, (k - 1) * DC + c:(k - 1) * DC + c + 1],
                        in1=DL, op0=ALU.add, op1=ALU.mult))
                if tp_gate is not None:
                    vop(lambda v: v.tensor_scalar(out=sqs[:, 0:TTW], in0=sc_e[:],
                                                  scalar1=16.0, scalar2=None, op0=ALU.add))
                    wt("sync", "ve")
                    dma("sync", "st", tp_gate[:, c * T + t0:c * T + t0 + TTW], sqs[:, 0:TTW])
                vop(lambda v, c=c, BGV=BGV: v.tensor_tensor(out=BGV, in0=BGV, in1=XNT(c), op=ALU.mult))
                vop(lambda v, c=c, BGV=BGV: v.tensor_tensor_scan(
                    out=BGV, data0=ones512[:],
                    data1=BGV, initial=carry[:, c:c + 1], op0=ALU.mult, op1=ALU.add))
                vspace()
                vop(lambda v, c=c, BGV=BGV: v.tensor_copy(out=carry[:, c:c + 1],
                                                          in_=BGV[:, TTW - 1:TTW]))
                if tp_h is not None:
                    wt("sync", "ve")
                    dma("sync", "st", tp_h[:, c * T + t0:c * T + t0 + TTW], BGV)
                vop(lambda v, c=c, BGV=BGV: v.scalar_tensor_tensor(
                    out=sc_e[:], in0=sc_e[:], scalar=16.0, in1=BGV,
                    op0=ALU.add, op1=ALU.mult))
                vop(lambda v, c=c: v.scalar_tensor_tensor(
                    out=sc_e[:], in0=XNT(c), scalar=vecs[:, c:c + 1], in1=sc_e[:],
                    op0=ALU.mult, op1=ALU.add))
                vop(lambda v, c=c: v.tensor_copy(out=mixb[:, c * TTW:(c + 1) * TTW], in_=sc_e[:]))

            wt("tensor", "ve")
            for mc in range(DC):
                wbuf = wsb0 if mc % 2 == 0 else wsb1
                wt("sync", "mm")
                sdma(wbuf[:, 0:DC * 128],
                     d_wob.ap().rearrange("p (k e) -> p k e", k=DC)[:, :, mc * 128:(mc + 1) * 128])
                wt("tensor", "ld")
                wt("tensor", "ve", sems["ve"][1])
                for kc in range(DC):
                    mmop(lambda t, mc=mc, kc=kc, wbuf=wbuf: t.matmul(
                        ps_b[:], wbuf[:, kc * 128:(kc + 1) * 128],
                        mixb[:, kc * TTW:(kc + 1) * TTW],
                        start=(kc == 0), stop=(kc == DC - 1)))
                wt("vector", "mm")
                vop(lambda v, mc=mc: v.tensor_scalar(
                    out=AOT(mc), in0=ps_b[:],
                    scalar1=vecs[:, 5 * DC + mc:5 * DC + mc + 1], scalar2=None, op0=ALU.add))
                vop(lambda v, mc=mc: v.tensor_tensor(out=sc_e[:], in0=X0T(mc), in1=bc_u[:], op=ALU.mult))
                vop(lambda v, mc=mc: v.tensor_tensor(out=sc_e[:], in0=sc_e[:], in1=bc_v[:], op=ALU.add))
                vop(lambda v, mc=mc: v.tensor_tensor(out=AOT(mc), in0=AOT(mc), in1=sc_e[:], op=ALU.add))
            if tp_ao is not None:
                wt("sync", "ve")
                for c in range(DC):
                    dma("sync", "st", tp_ao[:, c * T + t0:c * T + t0 + TTW], AOT(c))

            wt("tensor", "ve")
            for c in range(DC):
                mmop(lambda t, c=c: t.matmul(ps_s[:], onesc[:, 0:1], AOT(c),
                                             start=(c == 0), stop=(c == DC - 1)))
            wt("vector", "mm")
            vop(lambda v: v.tensor_copy(out=RQ[0:1, 0:TTW], in_=ps_s[0:1, :]))
            for c in range(DC):
                wt("scalar", "ve")
                wt("scalar", "mm")
                aop(lambda s, c=c: s.activation(out=sqs[:, 0:TTW], in_=AOT(c), func=AF.Square))
                wt("tensor", "ac")
                mmop(lambda t, c=c: t.matmul(ps_s[:], onesc[:, 0:1], sqs[:, 0:TTW],
                                             start=(c == 0), stop=(c == DC - 1)))
            wt("vector", "mm")
            vop(lambda v: v.tensor_copy(out=RQ[0:1, 512:512 + TTW], in_=ps_s[0:1, :]))
            S1s = RQ[0:1, 0:TTW]
            S2s = RQ[0:1, 512:512 + TTW]
            vop(lambda v: v.tensor_scalar(out=S1s, in0=S1s, scalar1=cin[:1, C_DINV:C_DINV + 1],
                                          scalar2=None, op0=ALU.mult))
            vop(lambda v: v.tensor_scalar(out=S2s, in0=S2s, scalar1=cin[:1, C_DINV:C_DINV + 1],
                                          scalar2=None, op0=ALU.mult))
            vop(lambda v: v.tensor_tensor(out=RQ[0:1, 1024:1024 + TTW],
                                          in0=S1s, in1=S1s, op=ALU.mult))
            vop(lambda v: v.tensor_tensor(out=S2s, in0=S2s, in1=RQ[0:1, 1024:1024 + TTW], op=ALU.subtract))
            vop(lambda v: v.tensor_scalar(out=RQ[0:1, 1024:1024 + TTW], in0=S2s,
                                          scalar1=EPS, scalar2=None, op0=ALU.add))
            vop(lambda v: v.reciprocal(out=S2s, in_=RQ[0:1, 1024:1024 + TTW]))
            wt("scalar", "ve")
            aop(lambda s: s.activation(out=S2s, in_=S2s, func=AF.Sqrt))
            wt("vector", "ac")
            vop(lambda v: v.tensor_tensor(out=S1s, in0=S1s, in1=S2s, op=ALU.mult))
            vop(lambda v: v.tensor_scalar(out=S1s, in0=S1s, scalar1=-1.0, scalar2=None, op0=ALU.mult))
            bcast_row_to_tile(S2s, TTW, bc_u[:])
            bcast_row_to_tile(S1s, TTW, bc_v[:])
            for c in range(DC):
                vop(lambda v, c=c: v.tensor_tensor(out=AOT(c), in0=AOT(c), in1=bc_u[:], op=ALU.mult))
                vop(lambda v, c=c: v.tensor_tensor(out=AOT(c), in0=AOT(c), in1=bc_v[:], op=ALU.add))
                vop(lambda v, c=c: v.tensor_tensor(out=AOT(c), in0=AOT(c), in1=X0T(c), op=ALU.add))
            wt("sync", "ve")
            sdma(d_x1.ap().rearrange("p (c t) -> p c t", c=DC)[:, :, t0:t0 + TTW],
                 big0[:, DC * TTW:2 * DC * TTW].rearrange("p (c t) -> p c t", c=DC))
            if tp_x1 is not None:
                for c in range(DC):
                    dma("sync", "st", tp_x1[:, c * T + t0:c * T + t0 + TTW], AOT(c))

        tp_fh = tap("fh", [128, DC * T])
        for tt in range(NTT):
            t0 = tt * TTW
            wt("sync", "mm")
            wt("sync", "ac")
            sdma(big0[:, 0:DC * TTW], d_x1.ap().rearrange("p (c t) -> p c t", c=DC)[:, :, t0:t0 + TTW])
            wt("tensor", "ld")
            wt("vector", "ld")
            for c in range(DC):
                mmop(lambda t, c=c: t.matmul(ps_s[:], onesc[:, 0:1],
                                             big0[:, c * TTW:(c + 1) * TTW],
                                             start=(c == 0), stop=(c == DC - 1)))
            wt("vector", "mm")
            vop(lambda v, t0=t0: v.tensor_copy(out=R_S1[0:1, t0:t0 + TTW], in_=ps_s[0:1, :]))
            for c in range(DC):
                wt("scalar", "ve")
                wt("scalar", "mm")
                aop(lambda s, c=c: s.activation(out=sqs[:, 0:TTW],
                                                in_=big0[:, c * TTW:(c + 1) * TTW], func=AF.Square))
                wt("tensor", "ac")
                mmop(lambda t, c=c: t.matmul(ps_s[:], onesc[:, 0:1], sqs[:, 0:TTW],
                                             start=(c == 0), stop=(c == DC - 1)))
            wt("vector", "mm")
            vop(lambda v, t0=t0: v.tensor_copy(out=R_S2[0:1, t0:t0 + TTW], in_=ps_s[0:1, :]))
        ln_rows()

        for tt in range(NTT):
            t0 = tt * TTW
            bcast_row_to_tile(R_SS[0:1, t0:t0 + TTW], TTW, bc_s[:])
            bcast_row_to_tile(R_TT[0:1, t0:t0 + TTW], TTW, bc_t[:])
            wt("sync", "ve")
            sdma(big0[:, 0:DC * TTW], d_x1.ap().rearrange("p (c t) -> p c t", c=DC)[:, :, t0:t0 + TTW])
            wt("vector", "ld")
            for c in range(DC):
                FH = big1[:, c * T + t0: c * T + t0 + TTW]
                vop(lambda v, c=c, FH=FH: v.tensor_tensor(
                    out=FH, in0=big0[:, c * TTW:(c + 1) * TTW], in1=bc_s[:], op=ALU.mult))
                vop(lambda v, c=c, FH=FH: v.tensor_tensor(out=FH, in0=FH, in1=bc_t[:], op=ALU.add))
                vop(lambda v, c=c, FH=FH: v.tensor_scalar(
                    out=FH, in0=FH, scalar1=vecs[:, 3 * DC + c:3 * DC + c + 1],
                    scalar2=vecs[:, 4 * DC + c:4 * DC + c + 1], op0=ALU.mult, op1=ALU.add))
        if tp_fh is not None:
            wt("sync", "ve")
            dma("sync", "st", tp_fh[:, 0:DC * T], big1[:])

        stream_ve = [0, 0]

        def stream_piece(i2, dap):
            h = i2 % 2
            buf = wfp[:, h * 1024:(h + 1) * 1024]
            wt("sync", "ve", stream_ve[h])
            sdma(buf, dap)
            wt("vector", "ld")
            return h, buf

        def stream_done(h):
            stream_ve[h] = sems["ve"][1]

        def stream_begin():
            stream_ve[0] = stream_ve[1] = sems["ve"][1]

        def count_pass(data_slices, thr_scl_col, acc_cols, signed=False, stream=()):
            j = 0
            for ap_, w_ in data_slices:
                vop(lambda v, ap_=ap_, w_=w_, j=j: v.tensor_scalar(
                    out=big_scr(w_), in0=ap_, scalar1=scl[:, thr_scl_col:thr_scl_col + 1],
                    scalar2=None, op0=ALU.is_gt, op1=ALU.add,
                    accum_out=acc[:, acc_cols[0] + j:acc_cols[0] + j + 1]))
                j += 1
                if signed:
                    vop(lambda v, ap_=ap_, w_=w_, j=j: v.tensor_scalar(
                        out=big_scr(w_), in0=ap_, scalar1=scl[:, SCL_NEG:SCL_NEG + 1],
                        scalar2=None, op0=ALU.is_lt, op1=ALU.add,
                        accum_out=acc[:, acc_cols[0] + j:acc_cols[0] + j + 1]))
                    j += 1
            if stream:
                stream_begin()
                for i2, dap in enumerate(stream):
                    h, buf = stream_piece(i2, dap)
                    vop(lambda v, buf=buf: v.scalar_tensor_tensor(
                        out=sqs[:, 0:1024], in0=buf, scalar=-1.0, in1=buf,
                        op0=ALU.mult, op1=ALU.max))
                    vop(lambda v, j=j, tc=thr_scl_col: v.tensor_scalar(
                        out=sqs[:, 0:1024], in0=sqs[:, 0:1024],
                        scalar1=scl[:, tc:tc + 1], scalar2=None,
                        op0=ALU.is_gt, op1=ALU.add,
                        accum_out=acc[:, acc_cols[0] + j:acc_cols[0] + j + 1]))
                    stream_done(h)
                    j += 1
            vspace()
            vop(lambda v, j=j: v.tensor_reduce(
                out=acc[:, acc_cols[0] + j:acc_cols[0] + j + 1],
                in_=acc[:, acc_cols[0]:acc_cols[0] + j], axis=AX, op=ALU.add))
            vspace()
            return acc[:, acc_cols[0] + j:acc_cols[0] + j + 1]


        def big_scr(w_):
            return wfp[:, 0:w_]

        def global_count(local_col_ap, dst_row):
            wt("tensor", "ve")
            wt("tensor", "ac")
            mmop(lambda t: t.matmul(ps_c[:], onesc[:], local_col_ap, start=True, stop=True))
            wt("scalar", "mm")
            aop(lambda s: s.activation(out=nrw[:, 0:1], in_=ps_c[:], func=AF.Identity))
            apad()
            wt("sync", "ac")
            sdma(gat[0:1, 16:17], nrw[0:1, 0:1])
            allgather(gat[0:1, 16:17], 1, gat[0:1, 8:8 + n_cores])
            vop(lambda v: v.tensor_reduce(out=dst_row, in_=gat[0:1, 8:8 + n_cores], axis=AX, op=ALU.add))
            vspace()

        def quantile(data_slices, samp_slices, n_total, samp_target,
                     nt_col, ntm1_col, frac_col, amax_col, signed=False, stream=(),
                     tag="", n_it=B_ILL):
            stream = list(stream)


            """Exact global quantile of the |values| in data_slices.

            nt_col/ntm1_col/frac_col: scl columns that already hold nt, nt-1
            (per-partition) and frac; result -> scl[amax_col] (and row QV_RES).
            Target-rank values must already be in rows R_NT etc (see caller).
            """
            nsl = len(data_slices)
            for si, (ap_, w_) in enumerate(data_slices):
                if signed:
                    vop(lambda v, ap_=ap_, w_=w_: v.scalar_tensor_tensor(
                        out=big_scr(w_), in0=ap_, scalar=-1.0, in1=ap_,
                        op0=ALU.mult, op1=ALU.max))
                    vop(lambda v, w_=w_, si=si: v.tensor_reduce(
                        out=acc[:, 40 + si:41 + si], in_=big_scr(w_), axis=AX, op=ALU.max))
                else:
                    vop(lambda v, ap_=ap_, si=si: v.tensor_reduce(out=acc[:, 40 + si:41 + si],
                                                                  in_=ap_, axis=AX, op=ALU.max))
            nred = nsl
            if stream:
                stream_begin()
                vop(lambda v: v.memset(bc_s[:], 0.0))
                vop(lambda v: v.memset(bc_t[:], 0.0))
                for i2, dap in enumerate(stream):
                    h, buf = stream_piece(i2, dap)
                    vop(lambda v, buf=buf: v.tensor_tensor(
                        out=bc_s[:], in0=bc_s[:], in1=buf[:, 0:512], op=ALU.max))
                    vop(lambda v, buf=buf: v.scalar_tensor_tensor(
                        out=bc_s[:], in0=buf[:, 0:512], scalar=-1.0, in1=bc_s[:],
                        op0=ALU.mult, op1=ALU.max))
                    vop(lambda v, buf=buf: v.tensor_tensor(
                        out=bc_t[:], in0=bc_t[:], in1=buf[:, 512:1024], op=ALU.max))
                    vop(lambda v, buf=buf: v.scalar_tensor_tensor(
                        out=bc_t[:], in0=buf[:, 512:1024], scalar=-1.0, in1=bc_t[:],
                        op0=ALU.mult, op1=ALU.max))
                    stream_done(h)
                vop(lambda v, nsl=nsl: v.tensor_reduce(
                    out=acc[:, 40 + nsl:41 + nsl], in_=bc_s[:], axis=AX, op=ALU.max))
                vop(lambda v, nsl=nsl: v.tensor_reduce(
                    out=acc[:, 41 + nsl:42 + nsl], in_=bc_t[:], axis=AX, op=ALU.max))
                nred = nsl + 2
            vspace()
            vop(lambda v, nred=nred: v.tensor_reduce(out=acc[:, 40 + nred:41 + nred],
                                                     in_=acc[:, 40:40 + nred], axis=AX, op=ALU.max))
            vspace()
            wt("sync", "ve")
            sdma(gat[0:1, 128:256], acc[:, 40 + nred:41 + nred])
            wt("vector", "ld")
            vop(lambda v: v.tensor_reduce(out=gat[0:1, 4:5], in_=gat[0:1, 128:256], axis=AX, op=ALU.max))
            vspace()
            allgather(gat[0:1, 4:5], 1, gat[0:1, 8:8 + n_cores])
            vop(lambda v: v.tensor_reduce(out=gat[0:1, 5:6], in_=gat[0:1, 8:8 + n_cores], axis=AX, op=ALU.max))
            vspace()
            vop(lambda v: v.tensor_scalar(out=Q(QV_HI), in0=RQ[0:1, 0:RW], scalar1=0.0,
                                          scalar2=gat[0:1, 5:6], op0=ALU.mult, op1=ALU.add))
            vop(lambda v: v.tensor_scalar(out=Q(QV_LO), in0=Q(QV_HI), scalar1=0.0,
                                          scalar2=None, op0=ALU.mult))
            tp_qg = tap("qgat" + tag, [1, 256]) if tag else None
            if tp_qg is not None:
                wt("sync", "ve")
                dma("sync", "st", tp_qg[:], gat[:])
                tp_qa = tap("qacc" + tag, [128, 12])
                dma("sync", "st", tp_qa[:], acc[:, 40:52])
                wt("vector", "st")

            nt_py = n_total - int(math.floor(0.995 * (n_total - 1)))
            ntf = float(nt_py)
            ltgt = float(math.log(nt_py - 0.5))
            LOr, HIr, MIDr, CNTr, GEr = Q(QV_LO), Q(QV_HI), Q(QV_MID), Q(QV_CNT), Q(QV_GE)
            T1r, T2r, LCLOr, LCHIr = Q(QV_T1), Q(QV_T2), Q(QV_CLO), Q(QV_CHI)
            vop(lambda v, c_=float(math.log(n_total)): v.tensor_scalar(
                out=LCLOr, in0=LOr, scalar1=0.0, scalar2=c_, op0=ALU.mult, op1=ALU.add))
            vop(lambda v, c_=float(math.log(0.5)): v.tensor_scalar(
                out=LCHIr, in0=LOr, scalar1=0.0, scalar2=c_, op0=ALU.mult, op1=ALU.add))
            for it in range(n_it):
                if it % 2 == 0:
                    vop(lambda v: v.tensor_tensor(out=T1r, in0=LCLOr, in1=LCHIr, op=ALU.subtract))
                    vop(lambda v: v.reciprocal(out=T2r, in_=T1r))
                    vop(lambda v, c_=-ltgt: v.tensor_scalar(out=T1r, in0=LCLOr, scalar1=c_,
                                                            scalar2=None, op0=ALU.add))
                    vop(lambda v: v.tensor_tensor(out=T1r, in0=T1r, in1=T2r, op=ALU.mult))
                    vop(lambda v: v.tensor_scalar(out=T1r, in0=T1r, scalar1=0.02, scalar2=0.98,
                                                  op0=ALU.max, op1=ALU.min))
                    vop(lambda v: v.tensor_tensor(out=T2r, in0=HIr, in1=LOr, op=ALU.subtract))
                    vop(lambda v: v.tensor_tensor(out=T2r, in0=T2r, in1=T1r, op=ALU.mult))
                    vop(lambda v: v.tensor_tensor(out=MIDr, in0=LOr, in1=T2r, op=ALU.add))
                else:
                    vop(lambda v: v.tensor_tensor(out=MIDr, in0=LOr, in1=HIr, op=ALU.add))
                    vop(lambda v: v.tensor_scalar(out=MIDr, in0=MIDr, scalar1=0.5,
                                                  scalar2=None, op0=ALU.mult))
                spread_scl(Q1(QV_MID), SCL_MID)
                c_ap = count_pass(data_slices, SCL_MID, (0,), signed=signed, stream=stream)
                global_count(c_ap, gat[0:1, 60:61])
                vop(lambda v: v.tensor_scalar(out=CNTr, in0=CNTr, scalar1=0.0,
                                              scalar2=gat[0:1, 60:61], op0=ALU.mult, op1=ALU.add))
                vop(lambda v, c_=ntf: v.tensor_scalar(out=GEr, in0=CNTr, scalar1=c_,
                                                      scalar2=None, op0=ALU.is_ge))
                vop(lambda v: v.tensor_scalar(out=T1r, in0=GEr, scalar1=-1.0, scalar2=1.0,
                                              op0=ALU.mult, op1=ALU.add))
                vop(lambda v: v.tensor_tensor(out=T2r, in0=GEr, in1=MIDr, op=ALU.mult))
                vop(lambda v: v.tensor_tensor(out=LOr, in0=LOr, in1=T1r, op=ALU.mult))
                vop(lambda v: v.tensor_tensor(out=LOr, in0=LOr, in1=T2r, op=ALU.add))
                vop(lambda v: v.tensor_tensor(out=T2r, in0=T1r, in1=MIDr, op=ALU.mult))
                vop(lambda v: v.tensor_tensor(out=HIr, in0=HIr, in1=GEr, op=ALU.mult))
                vop(lambda v: v.tensor_tensor(out=HIr, in0=HIr, in1=T2r, op=ALU.add))
                vop(lambda v: v.tensor_scalar(out=T2r, in0=CNTr, scalar1=0.5, scalar2=None,
                                              op0=ALU.max))
                wt("scalar", "ve")
                aop(lambda s: s.activation(out=T2r, in_=T2r, func=AF.Ln))
                wt("vector", "ac")
                vop(lambda v, c_=ltgt / 2.0: v.tensor_scalar(out=MIDr, in0=LCLOr, scalar1=0.5,
                                                             scalar2=c_, op0=ALU.mult, op1=ALU.add))
                vop(lambda v: v.tensor_tensor(out=MIDr, in0=MIDr, in1=T1r, op=ALU.mult))
                vop(lambda v: v.tensor_tensor(out=LCLOr, in0=T2r, in1=GEr, op=ALU.mult))
                vop(lambda v: v.tensor_tensor(out=LCLOr, in0=LCLOr, in1=MIDr, op=ALU.add))
                vop(lambda v, c_=ltgt / 2.0: v.tensor_scalar(out=MIDr, in0=LCHIr, scalar1=0.5,
                                                             scalar2=c_, op0=ALU.mult, op1=ALU.add))
                vop(lambda v: v.tensor_tensor(out=MIDr, in0=MIDr, in1=GEr, op=ALU.mult))
                vop(lambda v: v.tensor_tensor(out=LCHIr, in0=T2r, in1=T1r, op=ALU.mult))
                vop(lambda v: v.tensor_tensor(out=LCHIr, in0=LCHIr, in1=MIDr, op=ALU.add))

            def masked_min(thr_col, dst_gat_col):
                j2 = 0
                for ap_, w_ in data_slices:
                    vop(lambda v, ap_=ap_, w_=w_: v.tensor_scalar(
                        out=big_scr(w_), in0=ap_, scalar1=scl[:, thr_col:thr_col + 1],
                        scalar2=BIG, op0=ALU.is_le, op1=ALU.mult))
                    vop(lambda v, ap_=ap_, w_=w_: v.tensor_tensor(
                        out=big_scr(w_), in0=big_scr(w_), in1=ap_, op=ALU.add))
                    vop(lambda v, w_=w_, j2=j2: v.tensor_reduce(out=acc[:, 0 + j2:1 + j2],
                                                                in_=big_scr(w_), axis=AX, op=ALU.min))
                    j2 += 1
                    if signed:
                        vop(lambda v, ap_=ap_, w_=w_: v.tensor_scalar(
                            out=big_scr(w_), in0=ap_, scalar1=scl[:, SCL_NEG:SCL_NEG + 1],
                            scalar2=BIG, op0=ALU.is_ge, op1=ALU.mult))
                        vop(lambda v, ap_=ap_, w_=w_: v.scalar_tensor_tensor(
                            out=big_scr(w_), in0=ap_, scalar=-1.0, in1=big_scr(w_),
                            op0=ALU.mult, op1=ALU.add))
                        vop(lambda v, w_=w_, j2=j2: v.tensor_reduce(out=acc[:, 0 + j2:1 + j2],
                                                                    in_=big_scr(w_), axis=AX, op=ALU.min))
                        j2 += 1
                if stream:
                    stream_begin()
                    for i2, dap in enumerate(stream):
                        h, buf = stream_piece(i2, dap)
                        vop(lambda v, buf=buf: v.scalar_tensor_tensor(
                            out=bc_s[:], in0=buf[:, 0:512], scalar=-1.0, in1=buf[:, 0:512],
                            op0=ALU.mult, op1=ALU.max))
                        vop(lambda v, tc=thr_col: v.tensor_scalar(
                            out=bc_t[:], in0=bc_s[:], scalar1=scl[:, tc:tc + 1],
                            scalar2=BIG, op0=ALU.is_le, op1=ALU.mult))
                        vop(lambda v: v.tensor_tensor(
                            out=bc_t[:], in0=bc_t[:], in1=bc_s[:], op=ALU.max))
                        vop(lambda v, buf=buf: v.scalar_tensor_tensor(
                            out=bc_s[:], in0=buf[:, 512:1024], scalar=-1.0, in1=buf[:, 512:1024],
                            op0=ALU.mult, op1=ALU.max))
                        vop(lambda v, tc=thr_col: v.tensor_scalar(
                            out=sqs[:, 0:512], in0=bc_s[:], scalar1=scl[:, tc:tc + 1],
                            scalar2=BIG, op0=ALU.is_le, op1=ALU.mult))
                        vop(lambda v: v.tensor_tensor(
                            out=sqs[:, 0:512], in0=sqs[:, 0:512], in1=bc_s[:], op=ALU.max))
                        vop(lambda v: v.tensor_tensor(
                            out=bc_t[:], in0=bc_t[:], in1=sqs[:, 0:512], op=ALU.min))
                        vop(lambda v, j2=j2: v.tensor_reduce(
                            out=acc[:, 0 + j2:1 + j2], in_=bc_t[:], axis=AX, op=ALU.min))
                        stream_done(h)
                        j2 += 1
                vspace()
                vop(lambda v, j2=j2: v.tensor_reduce(out=acc[:, 0 + j2:1 + j2],
                                                     in_=acc[:, 0:0 + j2], axis=AX, op=ALU.min))
                vspace()
                wt("sync", "ve")
                sdma(gat[0:1, 128:256], acc[:, 0 + j2:1 + j2])
                wt("vector", "ld")
                vop(lambda v: v.tensor_reduce(out=gat[0:1, 20:21], in_=gat[0:1, 128:256],
                                              axis=AX, op=ALU.min))
                vspace()
                allgather(gat[0:1, 20:21], 1, gat[0:1, 8:8 + n_cores])
                vop(lambda v, dc_=dst_gat_col: v.tensor_reduce(
                    out=gat[0:1, dc_:dc_ + 1], in_=gat[0:1, 8:8 + n_cores], axis=AX, op=ALU.min))
                vspace()

            spread_scl(Q1(QV_LO), SCL_MID)
            vspace()
            masked_min(SCL_MID, 21)
            nvop(lambda v: v.tensor_scalar(out=gat[0:1, 28:29], in0=gat[0:1, 21:22],
                                          scalar1=cin[0:1, C_EPSC:C_EPSC + 1], scalar2=None,
                                          op0=ALU.max))
            spread_scl(gat[0:1, 28:29], amax_col)

        i32w = sb("i32w", [1, 512], I32)

        def index_setup(n_elems, i_col, frac_col, nt_col, ntm1_col, nconst_col):
            RA = RQ[0:1, 0 * RW:1 * RW]
            RB = RQ[0:1, 1 * RW:2 * RW]
            RC = RQ[0:1, 2 * RW:3 * RW]
            RD2 = RQ[0:1, 3 * RW:4 * RW]
            vop(lambda v: v.tensor_scalar(out=RA, in0=RQ[0:1, 4 * RW:5 * RW], scalar1=0.0,
                                          scalar2=cin[0:1, C_200:C_200 + 1],
                                          op0=ALU.mult, op1=ALU.add))
            vop(lambda v: v.reciprocal(out=RB, in_=RA))
            vop(lambda v: v.tensor_scalar(out=RA, in0=RA, scalar1=0.0,
                                          scalar2=cin[0:1, C_ONE:C_ONE + 1],
                                          op0=ALU.mult, op1=ALU.add))
            vop(lambda v: v.tensor_tensor(out=RB, in0=RA, in1=RB, op=ALU.subtract))
            vop(lambda v, ncol=nconst_col: v.tensor_scalar(
                out=RC, in0=RA, scalar1=0.0, scalar2=cin[0:1, ncol:ncol + 1],
                op0=ALU.mult, op1=ALU.add))
            vop(lambda v: v.tensor_scalar(out=RD2, in0=RC, scalar1=-1.0, scalar2=None,
                                          op0=ALU.add))
            vop(lambda v: v.tensor_tensor(out=RB, in0=RB, in1=RD2, op=ALU.mult))
            vop(lambda v: v.tensor_copy(out=i32w[:], in_=RB))
            vop(lambda v: v.tensor_copy(out=RA, in_=i32w[:]))
            vop(lambda v: v.tensor_tensor(out=RD2, in0=RB, in1=RA, op=ALU.subtract))
            vop(lambda v: v.tensor_tensor(out=RC, in0=RC, in1=RA, op=ALU.subtract))
            spread_scl(RA[0:1, 0:1], i_col)
            spread_scl(RD2[0:1, 0:1], frac_col)
            spread_scl(RC[0:1, 0:1], nt_col)
            vspace()
            nvop(lambda v, ntc=nt_col, nmc=ntm1_col: v.tensor_scalar(
                out=scl[:, nmc:nmc + 1], in0=scl[:, ntc:ntc + 1], scalar1=-1.0, scalar2=None,
                op0=ALU.add))

        q1_slices = [(big1[:, c * T:(c + 1) * T], T) for c in range(DC)]
        q1_samp = [(big1[:, 0:DC * T:8], DC * T // 8)]
        samp_target1 = int(round((1.0 - 0.995) * (128 * DC * T // 8)))
        quantile(q1_slices, q1_samp, N1, samp_target1, SCL_NT, SCL_NTM1, SCL_FRAC, SCL_AMAX,
                 signed=True, tag="1")
        tp_am1 = tap("amax1", [1, 4])
        if tp_am1 is not None:
            wt("sync", "ve")
            dma("sync", "st", tp_am1[0:1, 0:1], gat[0:1, 28:29])
            dma("sync", "st", tp_am1[0:1, 1:2], scl[0:1, SCL_I:SCL_I + 1])
            dma("sync", "st", tp_am1[0:1, 2:3], scl[0:1, SCL_FRAC:SCL_FRAC + 1])
            dma("sync", "st", tp_am1[0:1, 3:4], scl[0:1, SCL_NT:SCL_NT + 1])
        tp_q1 = tap("q1dbg", [1, 8])
        if tp_q1 is not None:
            wt("sync", "ve")
            dma("sync", "st", tp_q1[0:1, 0:1], gat[0:1, 21:22])
            dma("sync", "st", tp_q1[0:1, 1:2], gat[0:1, 22:23])
            dma("sync", "st", tp_q1[0:1, 2:3], gat[0:1, 23:24])
            dma("sync", "st", tp_q1[0:1, 3:4], RQ[0:1, QV_LO * RW:QV_LO * RW + 1])
            dma("sync", "st", tp_q1[0:1, 4:5], RQ[0:1, QV_HI * RW:QV_HI * RW + 1])
            dma("sync", "st", tp_q1[0:1, 5:6], gat[0:1, 48:49])
            dma("sync", "st", tp_q1[0:1, 6:7], gat[0:1, 49:50])
            dma("sync", "st", tp_q1[0:1, 7:8], gat[0:1, 60:61])
            wt("vector", "st")

        vspace()
        nvop(lambda v: v.reciprocal(out=nrw[:, 3:4], in_=scl[:, SCL_AMAX:SCL_AMAX + 1]))
        nvop(lambda v: v.tensor_scalar(out=scl[:, SCL_S:SCL_S + 1], in0=nrw[:, 3:4],
                                       scalar1=cin[:, C_S127:C_S127 + 1], scalar2=None, op0=ALU.mult))
        nvop(lambda v: v.reciprocal(out=nrw[:, 3:4], in_=scl[:, SCL_S:SCL_S + 1]))
        nvop(lambda v: v.tensor_tensor(out=scl[:, SCL_GS:SCL_GS + 1], in0=scl[:, 12:13],
                                       in1=nrw[:, 3:4], op=ALU.mult))

        tp_fq = tap("fq", [128, DC * T])
        tp_fqd = tap("fqdbg", [128, 4 * TTW + 20])
        for c in range(DC):
            for p4 in range(4):
                if tp_fqd is not None and c == 0 and p4 == 0:
                    FH = big1[:, 0:TTW]
                    vop(lambda v, FH=FH: v.tensor_scalar(
                        out=big0[:, 0:TTW], in0=FH, scalar1=scl[:, SCL_S:SCL_S + 1],
                        scalar2=scl[:, SCL_SHIFT:SCL_SHIFT + 1], op0=ALU.mult, op1=ALU.add))
                    vop(lambda v: v.tensor_copy(out=ibuf[:, 0:TTW], in_=big0[:, 0:TTW]))
                    vop(lambda v: v.tensor_copy(out=big0[:, TTW:2 * TTW], in_=ibuf[:, 0:TTW]))
                    vop(lambda v: v.tensor_scalar(
                        out=ibuf[:, 0:TTW], in0=ibuf[:, 0:TTW], scalar1=cin[:, C_LO16:C_LO16 + 1],
                        scalar2=cin[:, C_HI16:C_HI16 + 1], op0=ALU.max, op1=ALU.min))
                    vop(lambda v: v.tensor_copy(out=big0[:, 2 * TTW:3 * TTW], in_=ibuf[:, 0:TTW]))
                    vop(lambda v: v.tensor_scalar(
                        out=big0[:, 3 * TTW:4 * TTW], in0=ibuf[:, 0:TTW],
                        scalar1=cin[:, C_16384:C_16384 + 1], scalar2=scl[:, SCL_GS:SCL_GS + 1],
                        op0=ALU.subtract, op1=ALU.mult))
                    vop(lambda v: v.tensor_copy(out=big0[:, 4 * TTW:4 * TTW + 20], in_=scl[:, 0:20]))
                    wt("sync", "ve")
                    dma("sync", "st", tp_fqd[:], big0[:, 0:4 * TTW + 20])
                    wt("vector", "st")
                FH = big1[:, c * T + p4 * TTW: c * T + (p4 + 1) * TTW]
                vop(lambda v, FH=FH: v.tensor_scalar(
                    out=big0[:, 0:TTW], in0=FH, scalar1=scl[:, SCL_S:SCL_S + 1],
                    scalar2=scl[:, SCL_SHIFT:SCL_SHIFT + 1], op0=ALU.mult, op1=ALU.add))
                vop(lambda v: v.tensor_copy(out=ibuf[:, 0:TTW], in_=big0[:, 0:TTW]))
                vop(lambda v: v.tensor_scalar(
                    out=ibuf[:, 0:TTW], in0=ibuf[:, 0:TTW], scalar1=cin[:, C_LO16:C_LO16 + 1],
                    scalar2=cin[:, C_HI16:C_HI16 + 1], op0=ALU.max, op1=ALU.min))
                vop(lambda v, c=c, p4=p4: v.tensor_scalar(
                    out=bigb[:, c * T + p4 * TTW: c * T + (p4 + 1) * TTW], in0=ibuf[:, 0:TTW],
                    scalar1=cin[:, C_16384:C_16384 + 1], scalar2=scl[:, SCL_GS:SCL_GS + 1],
                    op0=ALU.subtract, op1=ALU.mult))
        if tp_fq is not None:
            for c in range(DC):
                vop(lambda v, c=c: v.tensor_copy(out=big0[:, 0:T], in_=bigb[:, c * T:(c + 1) * T]))
                wt("sync", "ve")
                dma("sync", "st", tp_fq[:, c * T:(c + 1) * T], big0[:, 0:T])
                wt("vector", "st")

        wt("tensor", "ve")
        for tt in range(NTT):
            t0 = tt * TTW
            for mj in range(HC):
                wbuf = wsb0 if mj % 2 == 0 else wsb1
                wt("sync", "mm")
                sdma(wbuf[:, 0:DC * 128],
                     d_w1q.ap().rearrange("p (k h) -> p k h", k=DC)[:, :, mj * 128:(mj + 1) * 128])
                wt("tensor", "ld")
                wt("tensor", "ac", sems["ac"][1])
                for kc in range(DC):
                    mmop(lambda t, mj=mj, kc=kc, wbuf=wbuf, t0=t0: t.matmul(
                        ps_a[:], wbuf[:, kc * 128:(kc + 1) * 128],
                        bigb[:, kc * T + t0: kc * T + t0 + TTW],
                        start=(kc == 0), stop=(kc == DC - 1)))
                wt("scalar", "mm")
                wt("scalar", "ld")
                aop(lambda s, mj=mj: s.activation(out=sqs[:, 0:TTW], in_=ps_a[:], func=AF.Gelu))
                wt("sync", "ac")
                dma("sync", "ld", d_h2.ap()[:, mj * T + t0: mj * T + t0 + TTW], sqs[:, 0:TTW])

        wt("sync", "ve")
        wt("sync", "ld")
        for half, dst in ((0, big0), (1, big1)):
            for ch in range(8):
                src = d_h2.ap()[:, (half * 8 + ch) * T:(half * 8 + ch + 1) * T]
                sdma(dst[:, ch * T:(ch + 1) * T], src)
        wt("vector", "ld")
        for half, dst in ((0, big0), (1, big1)):
            for ch in range(8):
                CH = dst[:, ch * T:(ch + 1) * T]
                vop(lambda v, CH=CH: v.scalar_tensor_tensor(
                    out=CH, in0=CH, scalar=-1.0, in1=CH, op0=ALU.mult, op1=ALU.max))
        q2_slices = [(big0[:, c * T:(c + 1) * T], T) for c in range(DC)] + \
                    [(big1[:, c * T:(c + 1) * T], T) for c in range(DC)]
        q2_stream = [d_h2.ap()[:, ch * T + p * 1024: ch * T + (p + 1) * 1024]
                     for ch in range(16, 32) for p in range(2)]
        q2_samp = [(big0[:, 0:DC * T:16], DC * T // 16), (big1[:, 0:DC * T:16], DC * T // 16)]
        samp_target2 = int(round((1.0 - 0.995) * (128 * 2 * (DC * T // 16))))
        quantile(q2_slices, q2_samp, N2, samp_target2, SCL_NT, SCL_NTM1, SCL_FRAC, SCL_AMAX,
                 signed=False, stream=q2_stream, n_it=10)
        tp_am2 = tap("amax2", [1, 4])
        if tp_am2 is not None:
            wt("sync", "ve")
            dma("sync", "st", tp_am2[0:1, 0:1], gat[0:1, 28:29])
            dma("sync", "st", tp_am2[0:1, 1:2], scl[0:1, SCL_I:SCL_I + 1])
            dma("sync", "st", tp_am2[0:1, 2:3], scl[0:1, SCL_FRAC:SCL_FRAC + 1])
            dma("sync", "st", tp_am2[0:1, 3:4], scl[0:1, SCL_NT:SCL_NT + 1])

        vspace()
        nvop(lambda v: v.reciprocal(out=nrw[:, 3:4], in_=scl[:, SCL_AMAX:SCL_AMAX + 1]))
        nvop(lambda v: v.tensor_scalar(out=scl[:, SCL_S:SCL_S + 1], in0=nrw[:, 3:4],
                                       scalar1=cin[:, C_S127:C_S127 + 1], scalar2=None, op0=ALU.mult))
        nvop(lambda v: v.reciprocal(out=nrw[:, 3:4], in_=scl[:, SCL_S:SCL_S + 1]))
        nvop(lambda v: v.tensor_tensor(out=scl[:, SCL_GS:SCL_GS + 1], in0=scl[:, 14:15],
                                       in1=nrw[:, 3:4], op=ALU.mult))

        for tt in range(NTT):
            t0 = tt * TTW
            for hc2 in range(HC):
                wt("sync", "ve")
                sdma(wfp[:, 0:TTW], d_h2.ap()[:, hc2 * T + t0: hc2 * T + t0 + TTW])
                wt("vector", "ld")
                vop(lambda v, hc2=hc2: v.tensor_scalar(
                    out=sqs[:, 0:TTW], in0=wfp[:, 0:TTW], scalar1=scl[:, SCL_S:SCL_S + 1],
                    scalar2=scl[:, SCL_SHIFT:SCL_SHIFT + 1], op0=ALU.mult, op1=ALU.add))
                vop(lambda v, hc2=hc2: v.tensor_copy(out=ibuf[:, 0:TTW], in_=sqs[:, 0:TTW]))
                vop(lambda v, hc2=hc2: v.tensor_scalar(
                    out=ibuf[:, 0:TTW], in0=ibuf[:, 0:TTW], scalar1=cin[:, C_LO16:C_LO16 + 1],
                    scalar2=cin[:, C_HI16:C_HI16 + 1], op0=ALU.max, op1=ALU.min))
                vop(lambda v, hc2=hc2: v.tensor_scalar(
                    out=bigb[:, hc2 * TTW:(hc2 + 1) * TTW], in0=ibuf[:, 0:TTW],
                    scalar1=cin[:, C_16384:C_16384 + 1], scalar2=scl[:, SCL_GS:SCL_GS + 1],
                    op0=ALU.subtract, op1=ALU.mult))
            wt("sync", "ve")
            sdma(big0[:, 0:DC * TTW], d_x1.ap().rearrange("p (c t) -> p c t", c=DC)[:, :, t0:t0 + TTW])
            wt("tensor", "ve")
            for me in range(DC):
                for kg in range(4):
                    wbuf2 = wsb0 if (me * 4 + kg) % 2 == 0 else wsb1
                    wt("sync", "mm")
                    sdma(wbuf2[:, 0:8 * 128],
                         d_w2q.ap().rearrange("p (k e) -> p k e", k=HC)
                         [:, kg * 8:(kg + 1) * 8, me * 128:(me + 1) * 128])
                    wt("tensor", "ld")
                    wt("tensor", "ve", sems["ve"][1])
                    for kk in range(8):
                        kj = kg * 8 + kk
                        mmop(lambda t, me=me, kj=kj, kk=kk, wbuf2=wbuf2: t.matmul(
                            ps_b[:], wbuf2[:, kk * 128:(kk + 1) * 128],
                            bigb[:, kj * TTW:(kj + 1) * TTW],
                            start=(kj == 0), stop=(kj == HC - 1)))
                wt("vector", "mm")
                vop(lambda v, me=me: v.tensor_copy(
                    out=big0[:, DC * TTW + me * TTW: DC * TTW + (me + 1) * TTW], in_=ps_b[:]))
            wt("tensor", "ve")
            FOT = lambda c: big0[:, DC * TTW + c * TTW: DC * TTW + (c + 1) * TTW]
            for c in range(DC):
                mmop(lambda t, c=c: t.matmul(ps_s[:], onesc[:, 0:1], FOT(c),
                                             start=(c == 0), stop=(c == DC - 1)))
            wt("vector", "mm")
            vop(lambda v: v.tensor_copy(out=RQ[0:1, 0:TTW], in_=ps_s[0:1, :]))
            for c in range(DC):
                wt("scalar", "ve")
                wt("scalar", "mm")
                aop(lambda s, c=c: s.activation(out=sqs[:, 0:TTW],
                                                in_=FOT(c), func=AF.Square))
                wt("tensor", "ac")
                mmop(lambda t, c=c: t.matmul(ps_s[:], onesc[:, 0:1], sqs[:, 0:TTW],
                                             start=(c == 0), stop=(c == DC - 1)))
            wt("vector", "mm")
            vop(lambda v: v.tensor_copy(out=RQ[0:1, 512:512 + TTW], in_=ps_s[0:1, :]))
            S1s = RQ[0:1, 0:TTW]
            S2s = RQ[0:1, 512:512 + TTW]
            vop(lambda v: v.tensor_scalar(out=S1s, in0=S1s, scalar1=cin[:1, C_DINV:C_DINV + 1],
                                          scalar2=None, op0=ALU.mult))
            vop(lambda v: v.tensor_scalar(out=S2s, in0=S2s, scalar1=cin[:1, C_DINV:C_DINV + 1],
                                          scalar2=None, op0=ALU.mult))
            vop(lambda v: v.tensor_tensor(out=RQ[0:1, 1024:1024 + TTW], in0=S1s, in1=S1s, op=ALU.mult))
            vop(lambda v: v.tensor_tensor(out=S2s, in0=S2s, in1=RQ[0:1, 1024:1024 + TTW], op=ALU.subtract))
            vop(lambda v: v.tensor_scalar(out=RQ[0:1, 1024:1024 + TTW], in0=S2s,
                                          scalar1=EPS, scalar2=None, op0=ALU.add))
            vop(lambda v: v.reciprocal(out=S2s, in_=RQ[0:1, 1024:1024 + TTW]))
            wt("scalar", "ve")
            aop(lambda s: s.activation(out=S2s, in_=S2s, func=AF.Sqrt))
            wt("vector", "ac")
            vop(lambda v: v.tensor_tensor(out=S1s, in0=S1s, in1=S2s, op=ALU.mult))
            vop(lambda v: v.tensor_scalar(out=S1s, in0=S1s, scalar1=-1.0, scalar2=None, op0=ALU.mult))
            bcast_row_to_tile(S2s, TTW, bc_u[:])
            bcast_row_to_tile(S1s, TTW, bc_v[:])
            wt("vector", "ld")
            for c in range(DC):
                vop(lambda v, c=c: v.tensor_tensor(out=FOT(c), in0=FOT(c), in1=bc_u[:], op=ALU.mult))
                vop(lambda v, c=c: v.tensor_tensor(out=FOT(c), in0=FOT(c), in1=bc_v[:], op=ALU.add))
                vop(lambda v, c=c: v.tensor_tensor(out=FOT(c), in0=FOT(c),
                                                   in1=big0[:, c * TTW:(c + 1) * TTW], op=ALU.add))
            wt("tensor", "ve")
            tve_o = [sems["ve"][1], sems["ve"][1]]
            for sub in range(NTT):
                wt("vector", "st")
                for c in range(DC):
                    bank = ps_t[:] if c % 2 == 0 else ps_bc[:, 0:128]
                    wt("tensor", "ve", tve_o[c % 2])
                    mmop(lambda t, sub=sub, c=c, bank=bank: t.transpose(
                        bank, FOT(c)[:, sub * 128: sub * 128 + 128], ident[:]))
                    wt("vector", "mm")
                    vop(lambda v, sub=sub, c=c, bank=bank: v.tensor_copy(
                        out=wfp[:, c * 128:(c + 1) * 128], in_=bank))
                    tve_o[c % 2] = sems["ve"][1]
                wt("sync", "ve")
                dma("sync", "st", dy[t0 + sub * 128: t0 + sub * 128 + 128, :], wfp[:, 0:D])

        add("sync", lambda s: s.wait_ge(sems["st"][0], sems["st"][1]))
        add("sync", lambda s: s.wait_ge(sems["ld"][0], sems["ld"][1]))

        with nc.Block() as block:
            @block.sync
            def _(eng):
                for fn in ops["sync"]:
                    fn(eng)

            @block.vector
            def _(eng):
                for fn in ops["vector"]:
                    fn(eng)

            @block.scalar
            def _(eng):
                for fn in ops["scalar"]:
                    fn(eng)

            @block.tensor
            def _(eng):
                for fn in ops["tensor"]:
                    fn(eng)

            @block.gpsimd
            def _(eng):
                for fn in ops["gpsimd"]:
                    fn(eng)

    return nc, tap_outs


def make_consts():
    c = np.zeros(NCONST, np.float32)
    c[C_200] = 200.0
    c[C_ONE] = 1.0
    c[C_N1] = float(N1)
    c[C_N2] = float(N2)
    c[C_P23] = 8388608.0
    c[C_HALF] = 0.5
    c[C_D] = float(D)
    c[C_S127] = 127.0
    c[C_ZERO] = 0.0
    c[C_16384] = 16384.0
    c[C_LO16] = 16384.0 - 128.0
    c[C_HI16] = 16384.0 + 127.0
    c[C_RS1] = 0.0
    c[C_RS2] = 0.0
    c[C_NW1] = 1.0 / (D * H)
    c[C_NW2] = 1.0 / (D * H)
    c[C_EPSC] = EPS
    c[C_DINV] = 1.0 / D
    return np.broadcast_to(c, (128, NCONST)).copy()


def prep_inputs(L0_attn_x, L0_attn_w_xproj, L0_attn_b_xproj, L0_attn_A,
                L0_attn_D_param, L0_attn_w_out, L0_attn_b_out, L0_attn_gamma,
                L0_attn_beta, L0_ffn_ln_g, L0_ffn_ln_b, L0_ffn_w1, L0_ffn_w2):
    f = np.ascontiguousarray

    def fm(v):
        return f(np.asarray(v, np.float32).reshape(DC, 128).T)

    shared = {
        "wxT": f(np.asarray(L0_attn_w_xproj, np.float32).T),
        "woT": f(np.asarray(L0_attn_w_out, np.float32).T),
        "w1T": f(np.asarray(L0_ffn_w1, np.float32).T),
        "w2T": f(np.asarray(L0_ffn_w2, np.float32).T),
        "bxp": f(np.asarray(L0_attn_b_xproj, np.float32).reshape(EC, 128).T),
        "afm": f(np.asarray(L0_attn_A, np.float32).reshape(DC, 128, 16)
                 .transpose(1, 0, 2).reshape(128, DC * 16)),
        "vecs": f(np.concatenate([fm(L0_attn_D_param), fm(L0_attn_gamma), fm(L0_attn_beta),
                                  fm(L0_ffn_ln_g), fm(L0_ffn_ln_b), fm(L0_attn_b_out)], axis=1)),
        "ident": np.eye(128, dtype=np.float32),
        "consts": make_consts(),
    }
    xs = np.asarray(L0_attn_x, np.float32)
    return [dict(shared, x=f(xs[c])) for c in range(8)]


def kernel(**inputs):
    in_maps = prep_inputs(**inputs)
    nc, _ = build_program()
    res = run_bass_kernel_spmd(nc, in_maps, core_ids=list(range(8)))
    return np.stack([res.results[c]["y"] for c in range(8)], axis=0)


# version 08 (final)
# speedup vs baseline: 1.7076x; 1.0410x over previous
import contextlib
import math
from collections import defaultdict

import numpy as np

import concourse.bass as bass
import concourse.mybir as mybir
from concourse.bass_utils import run_bass_kernel_spmd

F32 = mybir.dt.float32
BF16 = mybir.dt.bfloat16
I16 = mybir.dt.int16
I32 = mybir.dt.int32
AF = mybir.ActivationFunctionType
ALU = mybir.AluOpType
AX = mybir.AxisListType.X

T = 2048
D = 1024
DC = D // 128
E2 = 2048
EC = E2 // 128
H = 4096
HC = H // 128
NTT = 4
TTW = T // NTT

N1 = 8 * T * D
N2 = 8 * T * H

RG = [[0, 1, 2, 3, 4, 5, 6, 7]]


def ceil_div(a, b):
    return (a + b - 1) // b
BIG = 1.0e30
EPS = 1.0e-5
KTAY = 8

B_ILL = 12

C_200, C_ONE, C_N1, C_N2, C_P23, C_HALF, C_D, C_S127, C_ZERO, C_16384, \
    C_LO16, C_HI16, C_G1HALFSC, C_RS1, C_RS2, C_NW1, C_NW2, C_EPSC, C_DINV, C_HLFD = range(20)
NCONST = 24


def build_program(taps=(), n_cores=8):
    taps = set(taps)
    nc = bass.Bass("TRN2", target_bir_lowering=False, debug=False,
                   detect_race_conditions=False, dynamic_dma_scratch_size=4096)

    dx = nc.dram_tensor("x", [T, D], F32, kind="ExternalInput").ap()
    dwxT = nc.dram_tensor("wxT", [D, E2], F32, kind="ExternalInput").ap()
    dwoT = nc.dram_tensor("woT", [D, D], F32, kind="ExternalInput").ap()
    dw1T = nc.dram_tensor("w1T", [D, H], F32, kind="ExternalInput").ap()
    dw2T = nc.dram_tensor("w2T", [H, D], F32, kind="ExternalInput").ap()
    dbxp = nc.dram_tensor("bxp", [128, EC], F32, kind="ExternalInput").ap()
    dafm = nc.dram_tensor("afm", [128, DC * 16], F32, kind="ExternalInput").ap()
    dvecs = nc.dram_tensor("vecs", [128, 6 * DC], F32, kind="ExternalInput").ap()
    dident = nc.dram_tensor("ident", [128, 128], F32, kind="ExternalInput").ap()
    dconsts = nc.dram_tensor("consts", [128, NCONST], F32, kind="ExternalInput").ap()
    dy = nc.dram_tensor("y", [T, D], F32, kind="ExternalOutput").ap()

    d_x0 = nc.dram_tensor("d_x0", [128, DC * T], F32, kind="Internal")
    d_x1 = nc.dram_tensor("d_x1", [128, DC * T], F32, kind="Internal")
    d_h2 = nc.dram_tensor("d_h2", [128, HC * T], F32, kind="Internal")
    d_wxb = nc.dram_tensor("d_wxb", [128, DC * E2], BF16, kind="Internal")
    d_wob = nc.dram_tensor("d_wob", [128, DC * D], BF16, kind="Internal")
    d_w1q = nc.dram_tensor("d_w1q", [128, DC * H], BF16, kind="Internal")
    d_w2q = nc.dram_tensor("d_w2q", [128, HC * D], BF16, kind="Internal")

    cc_n = [0]

    def new_cc(width):
        i = cc_n[0]
        cc_n[0] += 1
        ci = nc.dram_tensor(f"cci{i}", [1, width], F32, kind="Internal")
        co = nc.dram_tensor(f"cco{i}", [n_cores, width], F32, kind="Internal")
        return ci, co

    tap_outs = {}

    def tap(name, shape):
        if name in taps:
            tap_outs[name] = nc.dram_tensor("tap_" + name, shape, F32,
                                            kind="ExternalOutput").ap()
            return tap_outs[name]
        return None

    with contextlib.ExitStack() as ctx:
        sb = lambda n_, s_, dt=F32: ctx.enter_context(nc.sbuf_tensor(n_, s_, dt))
        psum = lambda n_, s_: ctx.enter_context(nc.psum_tensor(n_, s_, F32))

        big0 = sb("big0", [128, DC * T])
        big1 = sb("big1", [128, DC * T])
        bigb = sb("bigb", [128, DC * T], BF16)
        wsb0 = sb("wsb0", [128, 1024], BF16)
        wsb1 = sb("wsb1", [128, 1024], BF16)
        wfp = sb("wfp", [128, 2048])
        ibuf = sb("ibuf", [128, 512], I16)
        sqs = sb("sqs", [128, 1024])
        bc_s = sb("bc_s", [128, TTW])
        bc_t = sb("bc_t", [128, TTW])
        ones512 = sb("ones512", [128, 512])
        ident = sb("identt", [128, 128])
        onesc = sb("onesct", [128, 128])
        ones1 = sb("ones1t", [1, 128])
        cin = sb("cint", [128, NCONST])
        bxp = sb("bxpt", [128, EC])
        afm = sb("afmt", [128, DC * 16])
        vecs = sb("vecst", [128, 6 * DC])
        coef = sb("coeft", [128, KTAY * DC])
        carry = sb("carryt", [128, DC])
        acc = sb("acct", [128, 64])
        nrw = sb("nrwt", [128, 8])
        jnk = sb("jnkt", [128, 64])
        scl = sb("sclt", [128, 20])
        gat = sb("gatt", [1, 256])
        rows = sb("rowst", [1, 5632])

        sc_e = big1[:, 24 * 512:25 * 512]
        bc_u = big1[:, 25 * 512:26 * 512]
        bc_v = big1[:, 26 * 512:27 * 512]

        ps_a = psum("ps_a", [128, 512])
        ps_b = psum("ps_b", [128, 512])
        ps_t = psum("ps_t", [128, 128])
        ps_s = psum("ps_s", [1, 512])
        ps_bc = psum("ps_bc", [128, 512])
        ps_c = psum("ps_c", [128, 1])

        R_S1 = big0[0:1, 8192 + 0 * T:8192 + 1 * T]
        R_S2 = big0[0:1, 8192 + 1 * T:8192 + 2 * T]
        R_SS = big0[0:1, 8192 + 2 * T:8192 + 3 * T]
        R_TT = big0[0:1, 8192 + 3 * T:8192 + 4 * T]
        R_T1 = R_S1
        R_R1 = R_S2
        RQ = rows[0:1, 0:5632]
        RW = 512

        def Q(i):
            return RQ[0:1, i * RW:(i + 1) * RW]

        def Q1(i):
            return RQ[0:1, i * RW:i * RW + 1]

        QV_LO, QV_HI, QV_MID, QV_CNT, QV_GE, QV_T1, QV_T2, QV_RES, \
            QV_CLO, QV_CHI = range(10)
        QV_TGT = QV_RES

        SCL_MID, SCL_M1, SCL_S, SCL_GS, SCL_I, SCL_FRAC, SCL_NT, SCL_NTM1, \
            SCL_AMAX, SCL_T2, SCL_TMP, SCL_SHIFT = range(12)
        SCL_NEG = 16

        sems = {}
        for name in ["ld", "gd", "mm", "ve", "ac", "cc", "st"]:
            sems[name] = [ctx.enter_context(nc.semaphore("s_" + name)), 0]

        ops = defaultdict(list)

        def add(e, fn):
            ops[e].append(fn)

        def _mk(e, key):
            def op(fn):
                sem = sems[key][0]

                def w(eng):
                    fn(eng).then_inc(sem, 1)
                add(e, w)
                sems[key][1] += 1
            return op

        vop = _mk("vector", "ve")
        aop = _mk("scalar", "ac")
        mmop = _mk("tensor", "mm")
        gop = _mk("gpsimd", "ve")

        def wt(e, key, val=None):
            sem, v = sems[key][0], sems[key][1]
            if val is not None:
                v = val
            add(e, lambda eng: eng.wait_ge(sem, v))

        def dma(e, key, out, in_):
            sem = sems[key][0]

            def w(eng):
                eng.dma_start(out=out, in_=in_).then_inc(sem, 16)
            add(e, w)
            sems[key][1] += 16

        def sdma(out, in_):
            dma("sync", "ld", out, in_)

        def vspace():
            vop(lambda v: v.memset(sqs[:, 512:1024], 0.0))
            vop(lambda v: v.memset(sqs[:, 512:1024], 0.0))

        def nvop(fn):
            vop(fn)
            vspace()

        def apad():
            aop(lambda s: s.activation(out=jnk[:, 0:64], in_=jnk[:, 0:64], func=AF.Identity))
            aop(lambda s: s.activation(out=jnk[:, 0:64], in_=jnk[:, 0:64], func=AF.Identity))

        def gdma(out, in_):
            dma("gpsimd", "gd", out, in_)
            wt("gpsimd", "gd")

        def allgather(src_ap, width, dst_ap):
            ci, co = new_cc(width)
            wt("gpsimd", "ve")
            wt("gpsimd", "ld")
            gdma(ci.ap()[:], src_ap)
            sem = sems["cc"][0]

            def w(g):
                g.collective_compute("AllGather", ALU.bypass, replica_groups=RG,
                                     ins=[ci.ap()[:]], outs=[co.ap()[:]]).then_inc(sem, 1)
            add("gpsimd", w)
            sems["cc"][1] += 1
            wt("gpsimd", "cc")
            gdma(dst_ap, co.ap().rearrange("a b -> (a b)")[None, :])
            wt("vector", "gd")

        def bcast_row_to_tile(row_ap, width, dst):
            wt("tensor", "ve")
            wt("tensor", "ld")
            mmop(lambda t: t.matmul(ps_bc[:, 0:width], ones1[:], row_ap,
                                    start=True, stop=True))
            wt("vector", "mm")
            vop(lambda v: v.tensor_copy(out=dst, in_=ps_bc[:, 0:width]))

        def spread_scl(row11, col):
            wt("tensor", "ve")
            wt("tensor", "ac")
            mmop(lambda t: t.matmul(ps_c[:], ones1[:], row11, start=True, stop=True))
            wt("scalar", "mm")
            aop(lambda s: s.activation(out=nrw[:, 0:1], in_=ps_c[:], func=AF.Identity))
            apad()
            wt("vector", "ac")
            vop(lambda v, col=col: v.tensor_scalar(out=scl[:, col:col + 1], in0=nrw[:, 0:1],
                                                   scalar1=1.0, scalar2=None, op0=ALU.mult))
            vop(lambda v: v.tensor_scalar(out=scl[:, SCL_NEG:SCL_NEG + 1], in0=nrw[:, 0:1],
                                          scalar1=-1.0, scalar2=None, op0=ALU.mult))
            vspace()

        sdma(big1.rearrange("p (a d) -> p a d", a=16), dx.rearrange("(a p) d -> p a d", p=128))
        sdma(ident[:], dident)
        sdma(cin[:], dconsts)
        sdma(bxp[:], dbxp)
        sdma(afm[:], dafm)
        sdma(vecs[:], dvecs)
        wt("vector", "ld")

        vop(lambda v: v.memset(rows[:], 0.0))
        vop(lambda v: v.memset(scl[:], 0.0))
        vop(lambda v: v.memset(acc[:], 0.0))
        vop(lambda v: v.memset(gat[:], 0.0))
        vop(lambda v: v.memset(onesc[:], 1.0))
        vop(lambda v: v.memset(ones512[:], 1.0))
        vop(lambda v: v.memset(ones1[:], 1.0))
        vop(lambda v: v.memset(carry[:], 0.0))
        nvop(lambda v: v.tensor_scalar(out=scl[:, SCL_TMP:SCL_TMP + 1], in0=cin[:, C_P23:C_P23 + 1],
                                       scalar1=cin[:, C_HALF:C_HALF + 1], scalar2=None, op0=ALU.add))
        nvop(lambda v: v.tensor_tensor(out=scl[:, SCL_TMP:SCL_TMP + 1], in0=scl[:, SCL_TMP:SCL_TMP + 1],
                                       in1=cin[:, C_P23:C_P23 + 1], op=ALU.subtract))
        nvop(lambda v: v.tensor_scalar(out=scl[:, SCL_SHIFT:SCL_SHIFT + 1], in0=scl[:, SCL_TMP:SCL_TMP + 1],
                                       scalar1=cin[:, C_16384:C_16384 + 1], scalar2=None, op0=ALU.add))

        vop(lambda v: v.tensor_copy(out=sqs[:, 0:DC * 16], in_=afm[:]))
        for k in range(1, KTAY + 1):
            if k > 1:
                vop(lambda v: v.tensor_tensor(out=sqs[:, 0:DC * 16], in0=sqs[:, 0:DC * 16],
                                              in1=afm[:], op=ALU.mult))
            vop(lambda v, k=k: v.tensor_reduce(
                out=coef[:, (k - 1) * DC:k * DC],
                in_=sqs[:, 0:DC * 16].rearrange("p (c s) -> p c s", s=16),
                axis=AX, op=ALU.add))
        nvop(lambda v: v.tensor_scalar(out=scl[:, SCL_TMP:SCL_TMP + 1], in0=cin[:, C_ONE:C_ONE + 1],
                                       scalar1=1.0, scalar2=None, op0=ALU.mult))
        for k in range(2, KTAY + 1):
            nvop(lambda v, k=k: v.tensor_scalar(out=scl[:, SCL_TMP:SCL_TMP + 1],
                                                in0=scl[:, SCL_TMP:SCL_TMP + 1],
                                                scalar1=float(k), scalar2=None, op0=ALU.mult))
            nvop(lambda v: v.reciprocal(out=nrw[:, 1:2], in_=scl[:, SCL_TMP:SCL_TMP + 1]))
            nvop(lambda v, k=k: v.tensor_scalar(out=coef[:, (k - 1) * DC:k * DC],
                                                in0=coef[:, (k - 1) * DC:k * DC],
                                                scalar1=nrw[:, 1:2], scalar2=None, op0=ALU.mult))


        def ln_rows():
            A_, B_, C_, D_ = R_S1, R_S2, R_SS, R_TT
            vop(lambda v: v.tensor_scalar(out=A_, in0=A_, scalar1=cin[:1, C_DINV:C_DINV + 1],
                                          scalar2=None, op0=ALU.mult))
            vop(lambda v: v.tensor_scalar(out=B_, in0=B_, scalar1=cin[:1, C_DINV:C_DINV + 1],
                                          scalar2=None, op0=ALU.mult))
            vop(lambda v: v.tensor_tensor(out=D_, in0=A_, in1=A_, op=ALU.mult))
            vop(lambda v: v.tensor_tensor(out=B_, in0=B_, in1=D_, op=ALU.subtract))
            vop(lambda v: v.tensor_scalar(out=D_, in0=B_, scalar1=EPS, scalar2=None,
                                          op0=ALU.add))
            vop(lambda v: v.reciprocal(out=C_, in_=D_))
            wt("scalar", "ve")
            aop(lambda s: s.activation(out=C_, in_=C_, func=AF.Sqrt))
            wt("vector", "ac")
            vop(lambda v: v.tensor_tensor(out=D_, in0=C_, in1=C_, op=ALU.mult))
            vop(lambda v: v.tensor_tensor(out=D_, in0=D_, in1=B_, op=ALU.mult))
            vop(lambda v: v.tensor_copy(out=B_, in_=C_))
            vop(lambda v: v.tensor_scalar(out=D_, in0=D_, scalar1=EPS, scalar2=None,
                                          op0=ALU.add))
            vop(lambda v: v.reciprocal(out=C_, in_=D_))
            wt("scalar", "ve")
            aop(lambda s: s.activation(out=D_, in_=C_, func=AF.Sqrt))
            wt("vector", "ac")
            vop(lambda v: v.tensor_tensor(out=C_, in0=B_, in1=D_, op=ALU.mult))
            vop(lambda v: v.tensor_tensor(out=A_, in0=A_, in1=B_, op=ALU.mult))
            vop(lambda v: v.tensor_scalar(out=A_, in0=A_, scalar1=-1.0,
                                          scalar2=None, op0=ALU.mult))
            vop(lambda v: v.tensor_tensor(out=D_, in0=A_, in1=D_, op=ALU.mult))

        wt("tensor", "ve")
        wt("tensor", "ld")
        tve_tp = [sems["ve"][1], sems["ve"][1]]
        for idx_tp in range(16 * DC):
            tile, dc2 = divmod(idx_tp, DC)
            bank = ps_t[:] if idx_tp % 2 == 0 else ps_bc[:, 0:128]
            wt("tensor", "ve", tve_tp[idx_tp % 2])
            mmop(lambda t, tile=tile, dc2=dc2, bank=bank: t.transpose(
                bank, big1[:, tile * D + dc2 * 128: tile * D + dc2 * 128 + 128], ident[:]))
            wt("vector", "mm")
            vop(lambda v, tile=tile, dc2=dc2, bank=bank: v.tensor_copy(
                out=big0[:, dc2 * T + tile * 128: dc2 * T + tile * 128 + 128], in_=bank))
            tve_tp[idx_tp % 2] = sems["ve"][1]
        wt("sync", "ve")
        sdma(d_x0.ap()[:], big0[:])
        tp_x0 = tap("x0fm", [128, DC * T])
        if tp_x0 is not None:
            dma("sync", "st", tp_x0[:], big0[:])
        for tile in range(16):
            vop(lambda v, tile=tile: v.tensor_reduce(
                out=acc[:, 2 * tile:2 * tile + 1], in_=big1[:, tile * D:(tile + 1) * D],
                axis=AX, op=ALU.add))
            wt("scalar", "ve")
            aop(lambda s, tile=tile: s.activation(
                out=sqs[:, 0:1024], in_=big1[:, tile * D:(tile + 1) * D], func=AF.Square))
            wt("vector", "ac")
            vop(lambda v, tile=tile: v.tensor_reduce(
                out=acc[:, 2 * tile + 1:2 * tile + 2], in_=sqs[:, 0:1024], axis=AX, op=ALU.add))
            wt("sync", "ve")
            wt("sync", "ld")
            sdma(R_S1[0:1, tile * 128:(tile + 1) * 128], acc[:, 2 * tile:2 * tile + 1])
            sdma(R_S2[0:1, tile * 128:(tile + 1) * 128], acc[:, 2 * tile + 1:2 * tile + 2])
        wt("vector", "ld")
        tp_rraw = tap("rows_raw", [1, 2 * T])
        if tp_rraw is not None:
            wt("sync", "ve")
            dma("sync", "st", tp_rraw[0:1, 0:T], R_S1)
            dma("sync", "st", tp_rraw[0:1, T:2 * T], R_S2)
            wt("vector", "st")
        ln_rows()
        tp_rows = tap("rows_ln", [1, 4 * T])
        if tp_rows is not None:
            wt("sync", "ve")
            wt("sync", "ac")
            dma("sync", "st", tp_rows[0:1, 0:T], R_S1)
            dma("sync", "st", tp_rows[0:1, T:2 * T], R_S2)
            dma("sync", "st", tp_rows[0:1, 2 * T:3 * T], R_SS)
            dma("sync", "st", tp_rows[0:1, 3 * T:4 * T], R_TT)
            wt("vector", "st")

        for kc in range(DC):
            wt("sync", "ve")
            wt("sync", "ld")
            sdma(wfp[:, 0:2048], dwxT[kc * 128:(kc + 1) * 128, :])
            wt("vector", "ld")
            vop(lambda v: v.tensor_copy(out=bigb[:, 0:2048], in_=wfp[:, 0:2048]))
            wt("sync", "ve")
            sdma(d_wxb.ap()[:, kc * E2:(kc + 1) * E2], bigb[:, 0:2048])
            sdma(wfp[:, 0:1024], dwoT[kc * 128:(kc + 1) * 128, :])
            wt("vector", "ld")
            vop(lambda v: v.tensor_copy(out=bigb[:, 2048:3072], in_=wfp[:, 0:1024]))
            wt("sync", "ve")
            sdma(d_wob.ap()[:, kc * D:(kc + 1) * D], bigb[:, 2048:3072])

        def wprep(dsrc, ddst, n_chunks, width, acc_col, gcol, nwcol):
            pieces = ceil_div(width, 2048)
            pw = width // pieces
            ncols = n_chunks * pieces
            for ch in range(n_chunks):
                for pc in range(pieces):
                    j = ch * pieces + pc
                    wt("sync", "ve")
                    sdma(wfp[:, 0:pw], dsrc[ch * 128:(ch + 1) * 128, pc * pw:(pc + 1) * pw])
                    wt("vector", "ld")
                    vop(lambda v: v.scalar_tensor_tensor(
                        out=big1[:, 0:pw], in0=wfp[:, 0:pw], scalar=-1.0,
                        in1=wfp[:, 0:pw], op0=ALU.mult, op1=ALU.max))
                    vop(lambda v, j=j: v.tensor_reduce(out=acc[:, acc_col + j:acc_col + j + 1],
                                                       in_=big1[:, 0:pw], axis=AX, op=ALU.add))
            vspace()
            vop(lambda v: v.tensor_reduce(out=acc[:, acc_col + ncols:acc_col + ncols + 1],
                                          in_=acc[:, acc_col:acc_col + ncols], axis=AX, op=ALU.add))
            vspace()
            wt("tensor", "ve")
            wt("tensor", "ac")
            mmop(lambda t: t.matmul(ps_c[:], onesc[:], acc[:, acc_col + ncols:acc_col + ncols + 1],
                                    start=True, stop=True))
            wt("vector", "mm")
            nvop(lambda v: v.tensor_scalar(out=scl[:, gcol:gcol + 1], in0=ps_c[:],
                                           scalar1=cin[:, nwcol:nwcol + 1],
                                           scalar2=cin[:, C_EPSC:C_EPSC + 1],
                                           op0=ALU.mult, op1=ALU.add))
            nvop(lambda v: v.tensor_scalar(out=scl[:, gcol + 1:gcol + 2], in0=scl[:, gcol:gcol + 1],
                                           scalar1=0.5, scalar2=None, op0=ALU.mult))
            for ch in range(n_chunks):
                for pc in range(pieces):
                    wt("sync", "ve")
                    wt("sync", "ac")
                    sdma(wfp[:, 0:pw], dsrc[ch * 128:(ch + 1) * 128, pc * pw:(pc + 1) * pw])
                    wt("vector", "ld")
                    wt("scalar", "ld")
                    wt("scalar", "ve")
                    aop(lambda sx: sx.activation(out=big0[:, 0:pw], in_=wfp[:, 0:pw], func=AF.Sign))
                    vop(lambda v: v.scalar_tensor_tensor(
                        out=big1[:, 0:pw], in0=wfp[:, 0:pw], scalar=-1.0,
                        in1=wfp[:, 0:pw], op0=ALU.mult, op1=ALU.max))
                    vop(lambda v: v.tensor_scalar(out=big1[:, 0:pw], in0=big1[:, 0:pw],
                                                  scalar1=scl[:, gcol + 1:gcol + 2], scalar2=None,
                                                  op0=ALU.is_gt))
                    wt("vector", "ac")
                    vop(lambda v: v.tensor_tensor(out=bigb[:, 0:pw],
                                                  in0=big1[:, 0:pw], in1=big0[:, 0:pw], op=ALU.mult))
                    wt("sync", "ve")
                    sdma(ddst.ap()[:, ch * width + pc * pw:ch * width + (pc + 1) * pw], bigb[:, 0:pw])

        wprep(dw1T, d_w1q, DC, H, 8, 12, C_NW1)
        wprep(dw2T, d_w2q, HC, D, 25, 14, C_NW2)

        tp_xn = tap("xn", [128, DC * T])
        tp_dl = tap("delta", [128, DC * T])
        tp_gate = tap("gate", [128, DC * T])
        tp_h = tap("hcs", [128, DC * T])
        tp_ao = tap("attn_out", [128, DC * T])
        tp_x1 = tap("x1", [128, DC * T])

        X0T = lambda c: big0[:, c * TTW:(c + 1) * TTW]
        AOT = lambda c: big0[:, DC * TTW + c * TTW: DC * TTW + (c + 1) * TTW]
        XNT = lambda c: big1[:, c * TTW:(c + 1) * TTW]
        XNB = lambda c: bigb[:, c * TTW:(c + 1) * TTW]
        sc_a = big1[:, DC * TTW:2 * DC * TTW]
        sc_b = big1[:, 2 * DC * TTW:3 * DC * TTW]
        mixb = bigb[:, DC * TTW:2 * DC * TTW]

        for tt in range(NTT):
            t0 = tt * TTW
            wt("sync", "ve")
            wt("sync", "ld")
            sdma(big0[:, 0:DC * TTW], d_x0.ap().rearrange("p (c t) -> p c t", c=DC)[:, :, t0:t0 + TTW])
            bcast_row_to_tile(R_SS[0:1, t0:t0 + TTW], TTW, bc_s[:])
            bcast_row_to_tile(R_TT[0:1, t0:t0 + TTW], TTW, bc_t[:])
            bcast_row_to_tile(R_R1[0:1, t0:t0 + TTW], TTW, bc_u[:])
            bcast_row_to_tile(R_T1[0:1, t0:t0 + TTW], TTW, bc_v[:])
            wt("vector", "ld")
            for c in range(DC):
                vop(lambda v, c=c: v.tensor_tensor(out=XNT(c), in0=X0T(c), in1=bc_s[:], op=ALU.mult))
                vop(lambda v, c=c: v.tensor_tensor(out=XNT(c), in0=XNT(c), in1=bc_t[:], op=ALU.add))
                vop(lambda v, c=c: v.tensor_scalar(out=XNT(c), in0=XNT(c),
                                                   scalar1=vecs[:, DC + c:DC + c + 1],
                                                   scalar2=vecs[:, 2 * DC + c:2 * DC + c + 1],
                                                   op0=ALU.mult, op1=ALU.add))
                vop(lambda v, c=c: v.tensor_copy(out=XNB(c), in_=XNT(c)))
            if tp_xn is not None:
                wt("sync", "ve")
                for c in range(DC):
                    dma("sync", "st", tp_xn[:, c * T + t0:c * T + t0 + TTW], XNT(c))

            wt("vector", "mm")
            wt("tensor", "ve")
            for mc in range(EC):
                wbuf = wsb0 if mc % 2 == 0 else wsb1
                wt("sync", "mm")
                sdma(wbuf[:, 0:DC * 128],
                     d_wxb.ap().rearrange("p (k e) -> p k e", k=DC)[:, :, mc * 128:(mc + 1) * 128])
                wt("tensor", "ld")
                wt("tensor", "ac", sems["ac"][1])
                for kc in range(DC):
                    mmop(lambda t, mc=mc, kc=kc, wbuf=wbuf: t.matmul(
                        ps_a[:], wbuf[:, kc * 128:(kc + 1) * 128], XNB(kc),
                        start=(kc == 0), stop=(kc == DC - 1)))
                wt("scalar", "mm")
                if mc < DC:
                    aop(lambda s, mc=mc: s.activation(
                        out=sc_e[:], in_=ps_a[:], func=AF.Exp,
                        bias=bxp[:, mc:mc + 1], scale=1.0))
                    aop(lambda s, mc=mc: s.activation(
                        out=sc_a[:, mc * TTW:(mc + 1) * TTW], in_=sc_e[:],
                        func=AF.Ln, bias=1.0, scale=1.0))
                else:
                    aop(lambda s, mc=mc: s.activation(
                        out=sc_b[:, (mc - DC) * TTW:(mc - DC + 1) * TTW], in_=ps_a[:],
                        func=AF.Identity, bias=bxp[:, mc:mc + 1], scale=1.0))
            if tp_dl is not None:
                wt("sync", "ac")
                for c in range(DC):
                    dma("sync", "st", tp_dl[:, c * T + t0:c * T + t0 + TTW],
                        sc_a[:, c * TTW:(c + 1) * TTW])

            wt("vector", "ac")
            for c in range(DC):
                DL = sc_a[:, c * TTW:(c + 1) * TTW]
                BGV = sc_b[:, c * TTW:(c + 1) * TTW]
                vop(lambda v, c=c, DL=DL: v.tensor_scalar(
                    out=sc_e[:], in0=DL,
                    scalar1=coef[:, (KTAY - 1) * DC + c:(KTAY - 1) * DC + c + 1],
                    scalar2=None, op0=ALU.mult))
                for k in range(KTAY - 1, 0, -1):
                    vop(lambda v, c=c, k=k, DL=DL: v.scalar_tensor_tensor(
                        out=sc_e[:], in0=sc_e[:],
                        scalar=coef[:, (k - 1) * DC + c:(k - 1) * DC + c + 1],
                        in1=DL, op0=ALU.add, op1=ALU.mult))
                if tp_gate is not None:
                    vop(lambda v: v.tensor_scalar(out=sqs[:, 0:TTW], in0=sc_e[:],
                                                  scalar1=16.0, scalar2=None, op0=ALU.add))
                    wt("sync", "ve")
                    dma("sync", "st", tp_gate[:, c * T + t0:c * T + t0 + TTW], sqs[:, 0:TTW])
                vop(lambda v, c=c, BGV=BGV: v.tensor_tensor(out=BGV, in0=BGV, in1=XNT(c), op=ALU.mult))
                vop(lambda v, c=c, BGV=BGV: v.tensor_tensor_scan(
                    out=BGV, data0=ones512[:],
                    data1=BGV, initial=carry[:, c:c + 1], op0=ALU.mult, op1=ALU.add))
                vspace()
                vop(lambda v, c=c, BGV=BGV: v.tensor_copy(out=carry[:, c:c + 1],
                                                          in_=BGV[:, TTW - 1:TTW]))
                if tp_h is not None:
                    wt("sync", "ve")
                    dma("sync", "st", tp_h[:, c * T + t0:c * T + t0 + TTW], BGV)
                vop(lambda v, c=c, BGV=BGV: v.scalar_tensor_tensor(
                    out=sc_e[:], in0=sc_e[:], scalar=16.0, in1=BGV,
                    op0=ALU.add, op1=ALU.mult))
                vop(lambda v, c=c: v.scalar_tensor_tensor(
                    out=sc_e[:], in0=XNT(c), scalar=vecs[:, c:c + 1], in1=sc_e[:],
                    op0=ALU.mult, op1=ALU.add))
                vop(lambda v, c=c: v.tensor_copy(out=mixb[:, c * TTW:(c + 1) * TTW], in_=sc_e[:]))

            wt("tensor", "ve")
            for mc in range(DC):
                wbuf = wsb0 if mc % 2 == 0 else wsb1
                wt("sync", "mm")
                sdma(wbuf[:, 0:DC * 128],
                     d_wob.ap().rearrange("p (k e) -> p k e", k=DC)[:, :, mc * 128:(mc + 1) * 128])
                wt("tensor", "ld")
                wt("tensor", "ve", sems["ve"][1])
                for kc in range(DC):
                    mmop(lambda t, mc=mc, kc=kc, wbuf=wbuf: t.matmul(
                        ps_b[:], wbuf[:, kc * 128:(kc + 1) * 128],
                        mixb[:, kc * TTW:(kc + 1) * TTW],
                        start=(kc == 0), stop=(kc == DC - 1)))
                wt("vector", "mm")
                vop(lambda v, mc=mc: v.tensor_scalar(
                    out=AOT(mc), in0=ps_b[:],
                    scalar1=vecs[:, 5 * DC + mc:5 * DC + mc + 1], scalar2=None, op0=ALU.add))
                vop(lambda v, mc=mc: v.tensor_tensor(out=sc_e[:], in0=X0T(mc), in1=bc_u[:], op=ALU.mult))
                vop(lambda v, mc=mc: v.tensor_tensor(out=sc_e[:], in0=sc_e[:], in1=bc_v[:], op=ALU.add))
                vop(lambda v, mc=mc: v.tensor_tensor(out=AOT(mc), in0=AOT(mc), in1=sc_e[:], op=ALU.add))
            if tp_ao is not None:
                wt("sync", "ve")
                for c in range(DC):
                    dma("sync", "st", tp_ao[:, c * T + t0:c * T + t0 + TTW], AOT(c))

            wt("tensor", "ve")
            for c in range(DC):
                mmop(lambda t, c=c: t.matmul(ps_s[:], onesc[:, 0:1], AOT(c),
                                             start=(c == 0), stop=(c == DC - 1)))
            wt("vector", "mm")
            vop(lambda v: v.tensor_copy(out=RQ[0:1, 0:TTW], in_=ps_s[0:1, :]))
            for c in range(DC):
                wt("scalar", "ve")
                wt("scalar", "mm")
                aop(lambda s, c=c: s.activation(out=sqs[:, 0:TTW], in_=AOT(c), func=AF.Square))
                wt("tensor", "ac")
                mmop(lambda t, c=c: t.matmul(ps_s[:], onesc[:, 0:1], sqs[:, 0:TTW],
                                             start=(c == 0), stop=(c == DC - 1)))
            wt("vector", "mm")
            vop(lambda v: v.tensor_copy(out=RQ[0:1, 512:512 + TTW], in_=ps_s[0:1, :]))
            S1s = RQ[0:1, 0:TTW]
            S2s = RQ[0:1, 512:512 + TTW]
            vop(lambda v: v.tensor_scalar(out=S1s, in0=S1s, scalar1=cin[:1, C_DINV:C_DINV + 1],
                                          scalar2=None, op0=ALU.mult))
            vop(lambda v: v.tensor_scalar(out=S2s, in0=S2s, scalar1=cin[:1, C_DINV:C_DINV + 1],
                                          scalar2=None, op0=ALU.mult))
            vop(lambda v: v.tensor_tensor(out=RQ[0:1, 1024:1024 + TTW],
                                          in0=S1s, in1=S1s, op=ALU.mult))
            vop(lambda v: v.tensor_tensor(out=S2s, in0=S2s, in1=RQ[0:1, 1024:1024 + TTW], op=ALU.subtract))
            vop(lambda v: v.tensor_scalar(out=RQ[0:1, 1024:1024 + TTW], in0=S2s,
                                          scalar1=EPS, scalar2=None, op0=ALU.add))
            vop(lambda v: v.reciprocal(out=S2s, in_=RQ[0:1, 1024:1024 + TTW]))
            wt("scalar", "ve")
            aop(lambda s: s.activation(out=S2s, in_=S2s, func=AF.Sqrt))
            wt("vector", "ac")
            vop(lambda v: v.tensor_tensor(out=S1s, in0=S1s, in1=S2s, op=ALU.mult))
            vop(lambda v: v.tensor_scalar(out=S1s, in0=S1s, scalar1=-1.0, scalar2=None, op0=ALU.mult))
            bcast_row_to_tile(S2s, TTW, bc_u[:])
            bcast_row_to_tile(S1s, TTW, bc_v[:])
            for c in range(DC):
                vop(lambda v, c=c: v.tensor_tensor(out=AOT(c), in0=AOT(c), in1=bc_u[:], op=ALU.mult))
                vop(lambda v, c=c: v.tensor_tensor(out=AOT(c), in0=AOT(c), in1=bc_v[:], op=ALU.add))
                vop(lambda v, c=c: v.tensor_tensor(out=AOT(c), in0=AOT(c), in1=X0T(c), op=ALU.add))
            wt("sync", "ve")
            sdma(d_x1.ap().rearrange("p (c t) -> p c t", c=DC)[:, :, t0:t0 + TTW],
                 big0[:, DC * TTW:2 * DC * TTW].rearrange("p (c t) -> p c t", c=DC))
            if tp_x1 is not None:
                for c in range(DC):
                    dma("sync", "st", tp_x1[:, c * T + t0:c * T + t0 + TTW], AOT(c))

        tp_fh = tap("fh", [128, DC * T])
        for tt in range(NTT):
            t0 = tt * TTW
            wt("sync", "mm")
            wt("sync", "ac")
            sdma(big0[:, 0:DC * TTW], d_x1.ap().rearrange("p (c t) -> p c t", c=DC)[:, :, t0:t0 + TTW])
            wt("tensor", "ld")
            wt("vector", "ld")
            for c in range(DC):
                mmop(lambda t, c=c: t.matmul(ps_s[:], onesc[:, 0:1],
                                             big0[:, c * TTW:(c + 1) * TTW],
                                             start=(c == 0), stop=(c == DC - 1)))
            wt("vector", "mm")
            vop(lambda v, t0=t0: v.tensor_copy(out=R_S1[0:1, t0:t0 + TTW], in_=ps_s[0:1, :]))
            for c in range(DC):
                wt("scalar", "ve")
                wt("scalar", "mm")
                aop(lambda s, c=c: s.activation(out=sqs[:, 0:TTW],
                                                in_=big0[:, c * TTW:(c + 1) * TTW], func=AF.Square))
                wt("tensor", "ac")
                mmop(lambda t, c=c: t.matmul(ps_s[:], onesc[:, 0:1], sqs[:, 0:TTW],
                                             start=(c == 0), stop=(c == DC - 1)))
            wt("vector", "mm")
            vop(lambda v, t0=t0: v.tensor_copy(out=R_S2[0:1, t0:t0 + TTW], in_=ps_s[0:1, :]))
        ln_rows()

        for tt in range(NTT):
            t0 = tt * TTW
            bcast_row_to_tile(R_SS[0:1, t0:t0 + TTW], TTW, bc_s[:])
            bcast_row_to_tile(R_TT[0:1, t0:t0 + TTW], TTW, bc_t[:])
            wt("sync", "ve")
            sdma(big0[:, 0:DC * TTW], d_x1.ap().rearrange("p (c t) -> p c t", c=DC)[:, :, t0:t0 + TTW])
            wt("vector", "ld")
            for c in range(DC):
                FH = big1[:, c * T + t0: c * T + t0 + TTW]
                vop(lambda v, c=c, FH=FH: v.tensor_tensor(
                    out=FH, in0=big0[:, c * TTW:(c + 1) * TTW], in1=bc_s[:], op=ALU.mult))
                vop(lambda v, c=c, FH=FH: v.tensor_tensor(out=FH, in0=FH, in1=bc_t[:], op=ALU.add))
                vop(lambda v, c=c, FH=FH: v.tensor_scalar(
                    out=FH, in0=FH, scalar1=vecs[:, 3 * DC + c:3 * DC + c + 1],
                    scalar2=vecs[:, 4 * DC + c:4 * DC + c + 1], op0=ALU.mult, op1=ALU.add))
        if tp_fh is not None:
            wt("sync", "ve")
            dma("sync", "st", tp_fh[:, 0:DC * T], big1[:])

        stream_ve = [0, 0]

        def stream_piece(i2, dap):
            h = i2 % 2
            buf = wfp[:, h * 1024:(h + 1) * 1024]
            wt("sync", "ve", stream_ve[h])
            sdma(buf, dap)
            wt("vector", "ld")
            return h, buf

        def stream_done(h):
            stream_ve[h] = sems["ve"][1]

        def stream_begin():
            stream_ve[0] = stream_ve[1] = sems["ve"][1]

        def count_pass(data_slices, thr_scl_col, acc_cols, signed=False, stream=()):
            j = 0
            for ap_, w_ in data_slices:
                vop(lambda v, ap_=ap_, w_=w_, j=j: v.tensor_scalar(
                    out=big_scr(w_), in0=ap_, scalar1=scl[:, thr_scl_col:thr_scl_col + 1],
                    scalar2=None, op0=ALU.is_gt, op1=ALU.add,
                    accum_out=acc[:, acc_cols[0] + j:acc_cols[0] + j + 1]))
                j += 1
                if signed:
                    vop(lambda v, ap_=ap_, w_=w_, j=j: v.tensor_scalar(
                        out=big_scr(w_), in0=ap_, scalar1=scl[:, SCL_NEG:SCL_NEG + 1],
                        scalar2=None, op0=ALU.is_lt, op1=ALU.add,
                        accum_out=acc[:, acc_cols[0] + j:acc_cols[0] + j + 1]))
                    j += 1
            if stream:
                stream_begin()
                for i2, dap in enumerate(stream):
                    h, buf = stream_piece(i2, dap)
                    vop(lambda v, buf=buf: v.scalar_tensor_tensor(
                        out=sqs[:, 0:1024], in0=buf, scalar=-1.0, in1=buf,
                        op0=ALU.mult, op1=ALU.max))
                    vop(lambda v, j=j, tc=thr_scl_col: v.tensor_scalar(
                        out=sqs[:, 0:1024], in0=sqs[:, 0:1024],
                        scalar1=scl[:, tc:tc + 1], scalar2=None,
                        op0=ALU.is_gt, op1=ALU.add,
                        accum_out=acc[:, acc_cols[0] + j:acc_cols[0] + j + 1]))
                    stream_done(h)
                    j += 1
            vspace()
            vop(lambda v, j=j: v.tensor_reduce(
                out=acc[:, acc_cols[0] + j:acc_cols[0] + j + 1],
                in_=acc[:, acc_cols[0]:acc_cols[0] + j], axis=AX, op=ALU.add))
            vspace()
            return acc[:, acc_cols[0] + j:acc_cols[0] + j + 1]


        def big_scr(w_):
            return wfp[:, 0:w_]

        def global_count(local_col_ap, dst_row):
            wt("tensor", "ve")
            wt("tensor", "ac")
            mmop(lambda t: t.matmul(ps_c[:], onesc[:], local_col_ap, start=True, stop=True))
            wt("scalar", "mm")
            aop(lambda s: s.activation(out=nrw[:, 0:1], in_=ps_c[:], func=AF.Identity))
            apad()
            wt("sync", "ac")
            sdma(gat[0:1, 16:17], nrw[0:1, 0:1])
            allgather(gat[0:1, 16:17], 1, gat[0:1, 8:8 + n_cores])
            vop(lambda v: v.tensor_reduce(out=dst_row, in_=gat[0:1, 8:8 + n_cores], axis=AX, op=ALU.add))
            vspace()

        def quantile(data_slices, samp_slices, n_total, samp_target,
                     nt_col, ntm1_col, frac_col, amax_col, signed=False, stream=(),
                     tag="", n_it=B_ILL):
            stream = list(stream)


            """Exact global quantile of the |values| in data_slices.

            nt_col/ntm1_col/frac_col: scl columns that already hold nt, nt-1
            (per-partition) and frac; result -> scl[amax_col] (and row QV_RES).
            Target-rank values must already be in rows R_NT etc (see caller).
            """
            nsl = len(data_slices)
            for si, (ap_, w_) in enumerate(data_slices):
                if signed:
                    vop(lambda v, ap_=ap_, w_=w_: v.scalar_tensor_tensor(
                        out=big_scr(w_), in0=ap_, scalar=-1.0, in1=ap_,
                        op0=ALU.mult, op1=ALU.max))
                    vop(lambda v, w_=w_, si=si: v.tensor_reduce(
                        out=acc[:, 40 + si:41 + si], in_=big_scr(w_), axis=AX, op=ALU.max))
                else:
                    vop(lambda v, ap_=ap_, si=si: v.tensor_reduce(out=acc[:, 40 + si:41 + si],
                                                                  in_=ap_, axis=AX, op=ALU.max))
            nred = nsl
            if stream:
                stream_begin()
                vop(lambda v: v.memset(bc_s[:], 0.0))
                vop(lambda v: v.memset(bc_t[:], 0.0))
                for i2, dap in enumerate(stream):
                    h, buf = stream_piece(i2, dap)
                    vop(lambda v, buf=buf: v.tensor_tensor(
                        out=bc_s[:], in0=bc_s[:], in1=buf[:, 0:512], op=ALU.max))
                    vop(lambda v, buf=buf: v.scalar_tensor_tensor(
                        out=bc_s[:], in0=buf[:, 0:512], scalar=-1.0, in1=bc_s[:],
                        op0=ALU.mult, op1=ALU.max))
                    vop(lambda v, buf=buf: v.tensor_tensor(
                        out=bc_t[:], in0=bc_t[:], in1=buf[:, 512:1024], op=ALU.max))
                    vop(lambda v, buf=buf: v.scalar_tensor_tensor(
                        out=bc_t[:], in0=buf[:, 512:1024], scalar=-1.0, in1=bc_t[:],
                        op0=ALU.mult, op1=ALU.max))
                    stream_done(h)
                vop(lambda v, nsl=nsl: v.tensor_reduce(
                    out=acc[:, 40 + nsl:41 + nsl], in_=bc_s[:], axis=AX, op=ALU.max))
                vop(lambda v, nsl=nsl: v.tensor_reduce(
                    out=acc[:, 41 + nsl:42 + nsl], in_=bc_t[:], axis=AX, op=ALU.max))
                nred = nsl + 2
            vspace()
            vop(lambda v, nred=nred: v.tensor_reduce(out=acc[:, 40 + nred:41 + nred],
                                                     in_=acc[:, 40:40 + nred], axis=AX, op=ALU.max))
            vspace()
            wt("sync", "ve")
            sdma(gat[0:1, 128:256], acc[:, 40 + nred:41 + nred])
            wt("vector", "ld")
            vop(lambda v: v.tensor_reduce(out=gat[0:1, 4:5], in_=gat[0:1, 128:256], axis=AX, op=ALU.max))
            vspace()
            allgather(gat[0:1, 4:5], 1, gat[0:1, 8:8 + n_cores])
            vop(lambda v: v.tensor_reduce(out=gat[0:1, 5:6], in_=gat[0:1, 8:8 + n_cores], axis=AX, op=ALU.max))
            vspace()
            vop(lambda v: v.tensor_scalar(out=Q(QV_HI), in0=RQ[0:1, 0:RW], scalar1=0.0,
                                          scalar2=gat[0:1, 5:6], op0=ALU.mult, op1=ALU.add))
            vop(lambda v: v.tensor_scalar(out=Q(QV_LO), in0=Q(QV_HI), scalar1=0.0,
                                          scalar2=None, op0=ALU.mult))
            tp_qg = tap("qgat" + tag, [1, 256]) if tag else None
            if tp_qg is not None:
                wt("sync", "ve")
                dma("sync", "st", tp_qg[:], gat[:])
                tp_qa = tap("qacc" + tag, [128, 12])
                dma("sync", "st", tp_qa[:], acc[:, 40:52])
                wt("vector", "st")

            nt_py = n_total - int(math.floor(0.995 * (n_total - 1)))
            ntf = float(nt_py)
            ltgt = float(math.log(nt_py - 0.5))
            LOr, HIr, MIDr, CNTr, GEr = Q(QV_LO), Q(QV_HI), Q(QV_MID), Q(QV_CNT), Q(QV_GE)
            T1r, T2r, LCLOr, LCHIr = Q(QV_T1), Q(QV_T2), Q(QV_CLO), Q(QV_CHI)
            vop(lambda v, c_=float(math.log(n_total)): v.tensor_scalar(
                out=LCLOr, in0=LOr, scalar1=0.0, scalar2=c_, op0=ALU.mult, op1=ALU.add))
            vop(lambda v, c_=float(math.log(0.5)): v.tensor_scalar(
                out=LCHIr, in0=LOr, scalar1=0.0, scalar2=c_, op0=ALU.mult, op1=ALU.add))
            for it in range(n_it):
                if it % 2 == 0:
                    vop(lambda v: v.tensor_tensor(out=T1r, in0=LCLOr, in1=LCHIr, op=ALU.subtract))
                    vop(lambda v: v.reciprocal(out=T2r, in_=T1r))
                    vop(lambda v, c_=-ltgt: v.tensor_scalar(out=T1r, in0=LCLOr, scalar1=c_,
                                                            scalar2=None, op0=ALU.add))
                    vop(lambda v: v.tensor_tensor(out=T1r, in0=T1r, in1=T2r, op=ALU.mult))
                    vop(lambda v: v.tensor_scalar(out=T1r, in0=T1r, scalar1=0.02, scalar2=0.98,
                                                  op0=ALU.max, op1=ALU.min))
                    vop(lambda v: v.tensor_tensor(out=T2r, in0=HIr, in1=LOr, op=ALU.subtract))
                    vop(lambda v: v.tensor_tensor(out=T2r, in0=T2r, in1=T1r, op=ALU.mult))
                    vop(lambda v: v.tensor_tensor(out=MIDr, in0=LOr, in1=T2r, op=ALU.add))
                else:
                    vop(lambda v: v.tensor_tensor(out=MIDr, in0=LOr, in1=HIr, op=ALU.add))
                    vop(lambda v: v.tensor_scalar(out=MIDr, in0=MIDr, scalar1=0.5,
                                                  scalar2=None, op0=ALU.mult))
                spread_scl(Q1(QV_MID), SCL_MID)
                c_ap = count_pass(data_slices, SCL_MID, (0,), signed=signed, stream=stream)
                global_count(c_ap, gat[0:1, 60:61])
                vop(lambda v: v.tensor_scalar(out=CNTr, in0=CNTr, scalar1=0.0,
                                              scalar2=gat[0:1, 60:61], op0=ALU.mult, op1=ALU.add))
                vop(lambda v, c_=ntf: v.tensor_scalar(out=GEr, in0=CNTr, scalar1=c_,
                                                      scalar2=None, op0=ALU.is_ge))
                vop(lambda v: v.tensor_scalar(out=T1r, in0=GEr, scalar1=-1.0, scalar2=1.0,
                                              op0=ALU.mult, op1=ALU.add))
                vop(lambda v: v.tensor_tensor(out=T2r, in0=GEr, in1=MIDr, op=ALU.mult))
                vop(lambda v: v.tensor_tensor(out=LOr, in0=LOr, in1=T1r, op=ALU.mult))
                vop(lambda v: v.tensor_tensor(out=LOr, in0=LOr, in1=T2r, op=ALU.add))
                vop(lambda v: v.tensor_tensor(out=T2r, in0=T1r, in1=MIDr, op=ALU.mult))
                vop(lambda v: v.tensor_tensor(out=HIr, in0=HIr, in1=GEr, op=ALU.mult))
                vop(lambda v: v.tensor_tensor(out=HIr, in0=HIr, in1=T2r, op=ALU.add))
                vop(lambda v: v.tensor_scalar(out=T2r, in0=CNTr, scalar1=0.5, scalar2=None,
                                              op0=ALU.max))
                wt("scalar", "ve")
                aop(lambda s: s.activation(out=T2r, in_=T2r, func=AF.Ln))
                wt("vector", "ac")
                vop(lambda v, c_=ltgt / 2.0: v.tensor_scalar(out=MIDr, in0=LCLOr, scalar1=0.5,
                                                             scalar2=c_, op0=ALU.mult, op1=ALU.add))
                vop(lambda v: v.tensor_tensor(out=MIDr, in0=MIDr, in1=T1r, op=ALU.mult))
                vop(lambda v: v.tensor_tensor(out=LCLOr, in0=T2r, in1=GEr, op=ALU.mult))
                vop(lambda v: v.tensor_tensor(out=LCLOr, in0=LCLOr, in1=MIDr, op=ALU.add))
                vop(lambda v, c_=ltgt / 2.0: v.tensor_scalar(out=MIDr, in0=LCHIr, scalar1=0.5,
                                                             scalar2=c_, op0=ALU.mult, op1=ALU.add))
                vop(lambda v: v.tensor_tensor(out=MIDr, in0=MIDr, in1=GEr, op=ALU.mult))
                vop(lambda v: v.tensor_tensor(out=LCHIr, in0=T2r, in1=T1r, op=ALU.mult))
                vop(lambda v: v.tensor_tensor(out=LCHIr, in0=LCHIr, in1=MIDr, op=ALU.add))

            def masked_min(thr_col, dst_gat_col):
                j2 = 0
                for ap_, w_ in data_slices:
                    vop(lambda v, ap_=ap_, w_=w_: v.tensor_scalar(
                        out=big_scr(w_), in0=ap_, scalar1=scl[:, thr_col:thr_col + 1],
                        scalar2=BIG, op0=ALU.is_le, op1=ALU.mult))
                    vop(lambda v, ap_=ap_, w_=w_: v.tensor_tensor(
                        out=big_scr(w_), in0=big_scr(w_), in1=ap_, op=ALU.add))
                    vop(lambda v, w_=w_, j2=j2: v.tensor_reduce(out=acc[:, 0 + j2:1 + j2],
                                                                in_=big_scr(w_), axis=AX, op=ALU.min))
                    j2 += 1
                    if signed:
                        vop(lambda v, ap_=ap_, w_=w_: v.tensor_scalar(
                            out=big_scr(w_), in0=ap_, scalar1=scl[:, SCL_NEG:SCL_NEG + 1],
                            scalar2=BIG, op0=ALU.is_ge, op1=ALU.mult))
                        vop(lambda v, ap_=ap_, w_=w_: v.scalar_tensor_tensor(
                            out=big_scr(w_), in0=ap_, scalar=-1.0, in1=big_scr(w_),
                            op0=ALU.mult, op1=ALU.add))
                        vop(lambda v, w_=w_, j2=j2: v.tensor_reduce(out=acc[:, 0 + j2:1 + j2],
                                                                    in_=big_scr(w_), axis=AX, op=ALU.min))
                        j2 += 1
                if stream:
                    stream_begin()
                    for i2, dap in enumerate(stream):
                        h, buf = stream_piece(i2, dap)
                        vop(lambda v, buf=buf: v.scalar_tensor_tensor(
                            out=bc_s[:], in0=buf[:, 0:512], scalar=-1.0, in1=buf[:, 0:512],
                            op0=ALU.mult, op1=ALU.max))
                        vop(lambda v, tc=thr_col: v.tensor_scalar(
                            out=bc_t[:], in0=bc_s[:], scalar1=scl[:, tc:tc + 1],
                            scalar2=BIG, op0=ALU.is_le, op1=ALU.mult))
                        vop(lambda v: v.tensor_tensor(
                            out=bc_t[:], in0=bc_t[:], in1=bc_s[:], op=ALU.max))
                        vop(lambda v, buf=buf: v.scalar_tensor_tensor(
                            out=bc_s[:], in0=buf[:, 512:1024], scalar=-1.0, in1=buf[:, 512:1024],
                            op0=ALU.mult, op1=ALU.max))
                        vop(lambda v, tc=thr_col: v.tensor_scalar(
                            out=sqs[:, 0:512], in0=bc_s[:], scalar1=scl[:, tc:tc + 1],
                            scalar2=BIG, op0=ALU.is_le, op1=ALU.mult))
                        vop(lambda v: v.tensor_tensor(
                            out=sqs[:, 0:512], in0=sqs[:, 0:512], in1=bc_s[:], op=ALU.max))
                        vop(lambda v: v.tensor_tensor(
                            out=bc_t[:], in0=bc_t[:], in1=sqs[:, 0:512], op=ALU.min))
                        vop(lambda v, j2=j2: v.tensor_reduce(
                            out=acc[:, 0 + j2:1 + j2], in_=bc_t[:], axis=AX, op=ALU.min))
                        stream_done(h)
                        j2 += 1
                vspace()
                vop(lambda v, j2=j2: v.tensor_reduce(out=acc[:, 0 + j2:1 + j2],
                                                     in_=acc[:, 0:0 + j2], axis=AX, op=ALU.min))
                vspace()
                wt("sync", "ve")
                sdma(gat[0:1, 128:256], acc[:, 0 + j2:1 + j2])
                wt("vector", "ld")
                vop(lambda v: v.tensor_reduce(out=gat[0:1, 20:21], in_=gat[0:1, 128:256],
                                              axis=AX, op=ALU.min))
                vspace()
                allgather(gat[0:1, 20:21], 1, gat[0:1, 8:8 + n_cores])
                vop(lambda v, dc_=dst_gat_col: v.tensor_reduce(
                    out=gat[0:1, dc_:dc_ + 1], in_=gat[0:1, 8:8 + n_cores], axis=AX, op=ALU.min))
                vspace()

            spread_scl(Q1(QV_LO), SCL_MID)
            vspace()
            masked_min(SCL_MID, 21)
            nvop(lambda v: v.tensor_scalar(out=gat[0:1, 28:29], in0=gat[0:1, 21:22],
                                          scalar1=cin[0:1, C_EPSC:C_EPSC + 1], scalar2=None,
                                          op0=ALU.max))
            spread_scl(gat[0:1, 28:29], amax_col)

        i32w = sb("i32w", [1, 512], I32)

        def index_setup(n_elems, i_col, frac_col, nt_col, ntm1_col, nconst_col):
            RA = RQ[0:1, 0 * RW:1 * RW]
            RB = RQ[0:1, 1 * RW:2 * RW]
            RC = RQ[0:1, 2 * RW:3 * RW]
            RD2 = RQ[0:1, 3 * RW:4 * RW]
            vop(lambda v: v.tensor_scalar(out=RA, in0=RQ[0:1, 4 * RW:5 * RW], scalar1=0.0,
                                          scalar2=cin[0:1, C_200:C_200 + 1],
                                          op0=ALU.mult, op1=ALU.add))
            vop(lambda v: v.reciprocal(out=RB, in_=RA))
            vop(lambda v: v.tensor_scalar(out=RA, in0=RA, scalar1=0.0,
                                          scalar2=cin[0:1, C_ONE:C_ONE + 1],
                                          op0=ALU.mult, op1=ALU.add))
            vop(lambda v: v.tensor_tensor(out=RB, in0=RA, in1=RB, op=ALU.subtract))
            vop(lambda v, ncol=nconst_col: v.tensor_scalar(
                out=RC, in0=RA, scalar1=0.0, scalar2=cin[0:1, ncol:ncol + 1],
                op0=ALU.mult, op1=ALU.add))
            vop(lambda v: v.tensor_scalar(out=RD2, in0=RC, scalar1=-1.0, scalar2=None,
                                          op0=ALU.add))
            vop(lambda v: v.tensor_tensor(out=RB, in0=RB, in1=RD2, op=ALU.mult))
            vop(lambda v: v.tensor_copy(out=i32w[:], in_=RB))
            vop(lambda v: v.tensor_copy(out=RA, in_=i32w[:]))
            vop(lambda v: v.tensor_tensor(out=RD2, in0=RB, in1=RA, op=ALU.subtract))
            vop(lambda v: v.tensor_tensor(out=RC, in0=RC, in1=RA, op=ALU.subtract))
            spread_scl(RA[0:1, 0:1], i_col)
            spread_scl(RD2[0:1, 0:1], frac_col)
            spread_scl(RC[0:1, 0:1], nt_col)
            vspace()
            nvop(lambda v, ntc=nt_col, nmc=ntm1_col: v.tensor_scalar(
                out=scl[:, nmc:nmc + 1], in0=scl[:, ntc:ntc + 1], scalar1=-1.0, scalar2=None,
                op0=ALU.add))

        q1_slices = [(big1[:, c * T:(c + 1) * T], T) for c in range(DC)]
        q1_samp = [(big1[:, 0:DC * T:8], DC * T // 8)]
        samp_target1 = int(round((1.0 - 0.995) * (128 * DC * T // 8)))
        quantile(q1_slices, q1_samp, N1, samp_target1, SCL_NT, SCL_NTM1, SCL_FRAC, SCL_AMAX,
                 signed=True, tag="1")
        tp_am1 = tap("amax1", [1, 4])
        if tp_am1 is not None:
            wt("sync", "ve")
            dma("sync", "st", tp_am1[0:1, 0:1], gat[0:1, 28:29])
            dma("sync", "st", tp_am1[0:1, 1:2], scl[0:1, SCL_I:SCL_I + 1])
            dma("sync", "st", tp_am1[0:1, 2:3], scl[0:1, SCL_FRAC:SCL_FRAC + 1])
            dma("sync", "st", tp_am1[0:1, 3:4], scl[0:1, SCL_NT:SCL_NT + 1])
        tp_q1 = tap("q1dbg", [1, 8])
        if tp_q1 is not None:
            wt("sync", "ve")
            dma("sync", "st", tp_q1[0:1, 0:1], gat[0:1, 21:22])
            dma("sync", "st", tp_q1[0:1, 1:2], gat[0:1, 22:23])
            dma("sync", "st", tp_q1[0:1, 2:3], gat[0:1, 23:24])
            dma("sync", "st", tp_q1[0:1, 3:4], RQ[0:1, QV_LO * RW:QV_LO * RW + 1])
            dma("sync", "st", tp_q1[0:1, 4:5], RQ[0:1, QV_HI * RW:QV_HI * RW + 1])
            dma("sync", "st", tp_q1[0:1, 5:6], gat[0:1, 48:49])
            dma("sync", "st", tp_q1[0:1, 6:7], gat[0:1, 49:50])
            dma("sync", "st", tp_q1[0:1, 7:8], gat[0:1, 60:61])
            wt("vector", "st")

        vspace()
        nvop(lambda v: v.reciprocal(out=nrw[:, 3:4], in_=scl[:, SCL_AMAX:SCL_AMAX + 1]))
        nvop(lambda v: v.tensor_scalar(out=scl[:, SCL_S:SCL_S + 1], in0=nrw[:, 3:4],
                                       scalar1=cin[:, C_S127:C_S127 + 1], scalar2=None, op0=ALU.mult))
        nvop(lambda v: v.reciprocal(out=nrw[:, 3:4], in_=scl[:, SCL_S:SCL_S + 1]))
        nvop(lambda v: v.tensor_tensor(out=scl[:, SCL_GS:SCL_GS + 1], in0=scl[:, 12:13],
                                       in1=nrw[:, 3:4], op=ALU.mult))

        tp_fq = tap("fq", [128, DC * T])
        tp_fqd = tap("fqdbg", [128, 4 * TTW + 20])
        for c in range(DC):
            for p4 in range(4):
                if tp_fqd is not None and c == 0 and p4 == 0:
                    FH = big1[:, 0:TTW]
                    vop(lambda v, FH=FH: v.tensor_scalar(
                        out=big0[:, 0:TTW], in0=FH, scalar1=scl[:, SCL_S:SCL_S + 1],
                        scalar2=scl[:, SCL_SHIFT:SCL_SHIFT + 1], op0=ALU.mult, op1=ALU.add))
                    vop(lambda v: v.tensor_copy(out=ibuf[:, 0:TTW], in_=big0[:, 0:TTW]))
                    vop(lambda v: v.tensor_copy(out=big0[:, TTW:2 * TTW], in_=ibuf[:, 0:TTW]))
                    vop(lambda v: v.tensor_scalar(
                        out=ibuf[:, 0:TTW], in0=ibuf[:, 0:TTW], scalar1=cin[:, C_LO16:C_LO16 + 1],
                        scalar2=cin[:, C_HI16:C_HI16 + 1], op0=ALU.max, op1=ALU.min))
                    vop(lambda v: v.tensor_copy(out=big0[:, 2 * TTW:3 * TTW], in_=ibuf[:, 0:TTW]))
                    vop(lambda v: v.tensor_scalar(
                        out=big0[:, 3 * TTW:4 * TTW], in0=ibuf[:, 0:TTW],
                        scalar1=cin[:, C_16384:C_16384 + 1], scalar2=scl[:, SCL_GS:SCL_GS + 1],
                        op0=ALU.subtract, op1=ALU.mult))
                    vop(lambda v: v.tensor_copy(out=big0[:, 4 * TTW:4 * TTW + 20], in_=scl[:, 0:20]))
                    wt("sync", "ve")
                    dma("sync", "st", tp_fqd[:], big0[:, 0:4 * TTW + 20])
                    wt("vector", "st")
                FH = big1[:, c * T + p4 * TTW: c * T + (p4 + 1) * TTW]
                vop(lambda v, FH=FH: v.tensor_scalar(
                    out=big0[:, 0:TTW], in0=FH, scalar1=scl[:, SCL_S:SCL_S + 1],
                    scalar2=scl[:, SCL_SHIFT:SCL_SHIFT + 1], op0=ALU.mult, op1=ALU.add))
                vop(lambda v: v.tensor_copy(out=ibuf[:, 0:TTW], in_=big0[:, 0:TTW]))
                vop(lambda v: v.tensor_scalar(
                    out=ibuf[:, 0:TTW], in0=ibuf[:, 0:TTW], scalar1=cin[:, C_LO16:C_LO16 + 1],
                    scalar2=cin[:, C_HI16:C_HI16 + 1], op0=ALU.max, op1=ALU.min))
                vop(lambda v, c=c, p4=p4: v.tensor_scalar(
                    out=bigb[:, c * T + p4 * TTW: c * T + (p4 + 1) * TTW], in0=ibuf[:, 0:TTW],
                    scalar1=cin[:, C_16384:C_16384 + 1], scalar2=scl[:, SCL_GS:SCL_GS + 1],
                    op0=ALU.subtract, op1=ALU.mult))
        if tp_fq is not None:
            for c in range(DC):
                vop(lambda v, c=c: v.tensor_copy(out=big0[:, 0:T], in_=bigb[:, c * T:(c + 1) * T]))
                wt("sync", "ve")
                dma("sync", "st", tp_fq[:, c * T:(c + 1) * T], big0[:, 0:T])
                wt("vector", "st")

        wt("tensor", "ve")
        for tt in range(NTT):
            t0 = tt * TTW
            for mj in range(HC):
                wbuf = wsb0 if mj % 2 == 0 else wsb1
                wt("sync", "mm")
                sdma(wbuf[:, 0:DC * 128],
                     d_w1q.ap().rearrange("p (k h) -> p k h", k=DC)[:, :, mj * 128:(mj + 1) * 128])
                wt("tensor", "ld")
                wt("tensor", "ac", sems["ac"][1])
                for kc in range(DC):
                    mmop(lambda t, mj=mj, kc=kc, wbuf=wbuf, t0=t0: t.matmul(
                        ps_a[:], wbuf[:, kc * 128:(kc + 1) * 128],
                        bigb[:, kc * T + t0: kc * T + t0 + TTW],
                        start=(kc == 0), stop=(kc == DC - 1)))
                wt("scalar", "mm")
                wt("scalar", "ld")
                aop(lambda s, mj=mj: s.activation(out=sqs[:, 0:TTW], in_=ps_a[:], func=AF.Gelu))
                wt("sync", "ac")
                dma("sync", "ld", d_h2.ap()[:, mj * T + t0: mj * T + t0 + TTW], sqs[:, 0:TTW])

        wt("sync", "ve")
        wt("sync", "ld")
        for half, dst in ((0, big0), (1, big1)):
            for ch in range(8):
                src = d_h2.ap()[:, (half * 8 + ch) * T:(half * 8 + ch + 1) * T]
                sdma(dst[:, ch * T:(ch + 1) * T], src)
        wt("vector", "ld")
        for half, dst in ((0, big0), (1, big1)):
            for ch in range(8):
                CH = dst[:, ch * T:(ch + 1) * T]
                vop(lambda v, CH=CH: v.scalar_tensor_tensor(
                    out=CH, in0=CH, scalar=-1.0, in1=CH, op0=ALU.mult, op1=ALU.max))
        q2_slices = [(big0[:, c * T:(c + 1) * T], T) for c in range(DC)] + \
                    [(big1[:, c * T:(c + 1) * T], T) for c in range(DC)]
        q2_stream = [d_h2.ap()[:, ch * T + p * 1024: ch * T + (p + 1) * 1024]
                     for ch in range(16, 32) for p in range(2)]
        q2_samp = [(big0[:, 0:DC * T:16], DC * T // 16), (big1[:, 0:DC * T:16], DC * T // 16)]
        samp_target2 = int(round((1.0 - 0.995) * (128 * 2 * (DC * T // 16))))
        quantile(q2_slices, q2_samp, N2, samp_target2, SCL_NT, SCL_NTM1, SCL_FRAC, SCL_AMAX,
                 signed=False, stream=q2_stream, n_it=8)
        tp_am2 = tap("amax2", [1, 4])
        if tp_am2 is not None:
            wt("sync", "ve")
            dma("sync", "st", tp_am2[0:1, 0:1], gat[0:1, 28:29])
            dma("sync", "st", tp_am2[0:1, 1:2], scl[0:1, SCL_I:SCL_I + 1])
            dma("sync", "st", tp_am2[0:1, 2:3], scl[0:1, SCL_FRAC:SCL_FRAC + 1])
            dma("sync", "st", tp_am2[0:1, 3:4], scl[0:1, SCL_NT:SCL_NT + 1])

        vspace()
        nvop(lambda v: v.reciprocal(out=nrw[:, 3:4], in_=scl[:, SCL_AMAX:SCL_AMAX + 1]))
        nvop(lambda v: v.tensor_scalar(out=scl[:, SCL_S:SCL_S + 1], in0=nrw[:, 3:4],
                                       scalar1=cin[:, C_S127:C_S127 + 1], scalar2=None, op0=ALU.mult))
        nvop(lambda v: v.reciprocal(out=nrw[:, 3:4], in_=scl[:, SCL_S:SCL_S + 1]))
        nvop(lambda v: v.tensor_tensor(out=scl[:, SCL_GS:SCL_GS + 1], in0=scl[:, 14:15],
                                       in1=nrw[:, 3:4], op=ALU.mult))

        for tt in range(NTT):
            t0 = tt * TTW
            for hc2 in range(HC):
                wt("sync", "ve")
                sdma(wfp[:, 0:TTW], d_h2.ap()[:, hc2 * T + t0: hc2 * T + t0 + TTW])
                wt("vector", "ld")
                vop(lambda v, hc2=hc2: v.tensor_scalar(
                    out=sqs[:, 0:TTW], in0=wfp[:, 0:TTW], scalar1=scl[:, SCL_S:SCL_S + 1],
                    scalar2=scl[:, SCL_SHIFT:SCL_SHIFT + 1], op0=ALU.mult, op1=ALU.add))
                vop(lambda v, hc2=hc2: v.tensor_copy(out=ibuf[:, 0:TTW], in_=sqs[:, 0:TTW]))
                vop(lambda v, hc2=hc2: v.tensor_scalar(
                    out=ibuf[:, 0:TTW], in0=ibuf[:, 0:TTW], scalar1=cin[:, C_LO16:C_LO16 + 1],
                    scalar2=cin[:, C_HI16:C_HI16 + 1], op0=ALU.max, op1=ALU.min))
                vop(lambda v, hc2=hc2: v.tensor_scalar(
                    out=bigb[:, hc2 * TTW:(hc2 + 1) * TTW], in0=ibuf[:, 0:TTW],
                    scalar1=cin[:, C_16384:C_16384 + 1], scalar2=scl[:, SCL_GS:SCL_GS + 1],
                    op0=ALU.subtract, op1=ALU.mult))
            wt("sync", "ve")
            sdma(big0[:, 0:DC * TTW], d_x1.ap().rearrange("p (c t) -> p c t", c=DC)[:, :, t0:t0 + TTW])
            wt("tensor", "ve")
            for me in range(DC):
                for kg in range(4):
                    wbuf2 = wsb0 if (me * 4 + kg) % 2 == 0 else wsb1
                    wt("sync", "mm")
                    sdma(wbuf2[:, 0:8 * 128],
                         d_w2q.ap().rearrange("p (k e) -> p k e", k=HC)
                         [:, kg * 8:(kg + 1) * 8, me * 128:(me + 1) * 128])
                    wt("tensor", "ld")
                    wt("tensor", "ve", sems["ve"][1])
                    for kk in range(8):
                        kj = kg * 8 + kk
                        mmop(lambda t, me=me, kj=kj, kk=kk, wbuf2=wbuf2: t.matmul(
                            ps_b[:], wbuf2[:, kk * 128:(kk + 1) * 128],
                            bigb[:, kj * TTW:(kj + 1) * TTW],
                            start=(kj == 0), stop=(kj == HC - 1)))
                wt("vector", "mm")
                vop(lambda v, me=me: v.tensor_copy(
                    out=big0[:, DC * TTW + me * TTW: DC * TTW + (me + 1) * TTW], in_=ps_b[:]))
            wt("tensor", "ve")
            FOT = lambda c: big0[:, DC * TTW + c * TTW: DC * TTW + (c + 1) * TTW]
            for c in range(DC):
                mmop(lambda t, c=c: t.matmul(ps_s[:], onesc[:, 0:1], FOT(c),
                                             start=(c == 0), stop=(c == DC - 1)))
            wt("vector", "mm")
            vop(lambda v: v.tensor_copy(out=RQ[0:1, 0:TTW], in_=ps_s[0:1, :]))
            for c in range(DC):
                wt("scalar", "ve")
                wt("scalar", "mm")
                aop(lambda s, c=c: s.activation(out=sqs[:, 0:TTW],
                                                in_=FOT(c), func=AF.Square))
                wt("tensor", "ac")
                mmop(lambda t, c=c: t.matmul(ps_s[:], onesc[:, 0:1], sqs[:, 0:TTW],
                                             start=(c == 0), stop=(c == DC - 1)))
            wt("vector", "mm")
            vop(lambda v: v.tensor_copy(out=RQ[0:1, 512:512 + TTW], in_=ps_s[0:1, :]))
            S1s = RQ[0:1, 0:TTW]
            S2s = RQ[0:1, 512:512 + TTW]
            vop(lambda v: v.tensor_scalar(out=S1s, in0=S1s, scalar1=cin[:1, C_DINV:C_DINV + 1],
                                          scalar2=None, op0=ALU.mult))
            vop(lambda v: v.tensor_scalar(out=S2s, in0=S2s, scalar1=cin[:1, C_DINV:C_DINV + 1],
                                          scalar2=None, op0=ALU.mult))
            vop(lambda v: v.tensor_tensor(out=RQ[0:1, 1024:1024 + TTW], in0=S1s, in1=S1s, op=ALU.mult))
            vop(lambda v: v.tensor_tensor(out=S2s, in0=S2s, in1=RQ[0:1, 1024:1024 + TTW], op=ALU.subtract))
            vop(lambda v: v.tensor_scalar(out=RQ[0:1, 1024:1024 + TTW], in0=S2s,
                                          scalar1=EPS, scalar2=None, op0=ALU.add))
            vop(lambda v: v.reciprocal(out=S2s, in_=RQ[0:1, 1024:1024 + TTW]))
            wt("scalar", "ve")
            aop(lambda s: s.activation(out=S2s, in_=S2s, func=AF.Sqrt))
            wt("vector", "ac")
            vop(lambda v: v.tensor_tensor(out=S1s, in0=S1s, in1=S2s, op=ALU.mult))
            vop(lambda v: v.tensor_scalar(out=S1s, in0=S1s, scalar1=-1.0, scalar2=None, op0=ALU.mult))
            bcast_row_to_tile(S2s, TTW, bc_u[:])
            bcast_row_to_tile(S1s, TTW, bc_v[:])
            wt("vector", "ld")
            for c in range(DC):
                vop(lambda v, c=c: v.tensor_tensor(out=FOT(c), in0=FOT(c), in1=bc_u[:], op=ALU.mult))
                vop(lambda v, c=c: v.tensor_tensor(out=FOT(c), in0=FOT(c), in1=bc_v[:], op=ALU.add))
                vop(lambda v, c=c: v.tensor_tensor(out=FOT(c), in0=FOT(c),
                                                   in1=big0[:, c * TTW:(c + 1) * TTW], op=ALU.add))
            wt("tensor", "ve")
            tve_o = [sems["ve"][1], sems["ve"][1]]
            for sub in range(NTT):
                wt("vector", "st")
                for c in range(DC):
                    bank = ps_t[:] if c % 2 == 0 else ps_bc[:, 0:128]
                    wt("tensor", "ve", tve_o[c % 2])
                    mmop(lambda t, sub=sub, c=c, bank=bank: t.transpose(
                        bank, FOT(c)[:, sub * 128: sub * 128 + 128], ident[:]))
                    wt("vector", "mm")
                    vop(lambda v, sub=sub, c=c, bank=bank: v.tensor_copy(
                        out=wfp[:, c * 128:(c + 1) * 128], in_=bank))
                    tve_o[c % 2] = sems["ve"][1]
                wt("sync", "ve")
                dma("sync", "st", dy[t0 + sub * 128: t0 + sub * 128 + 128, :], wfp[:, 0:D])

        add("sync", lambda s: s.wait_ge(sems["st"][0], sems["st"][1]))
        add("sync", lambda s: s.wait_ge(sems["ld"][0], sems["ld"][1]))

        with nc.Block() as block:
            @block.sync
            def _(eng):
                for fn in ops["sync"]:
                    fn(eng)

            @block.vector
            def _(eng):
                for fn in ops["vector"]:
                    fn(eng)

            @block.scalar
            def _(eng):
                for fn in ops["scalar"]:
                    fn(eng)

            @block.tensor
            def _(eng):
                for fn in ops["tensor"]:
                    fn(eng)

            @block.gpsimd
            def _(eng):
                for fn in ops["gpsimd"]:
                    fn(eng)

    return nc, tap_outs


def make_consts():
    c = np.zeros(NCONST, np.float32)
    c[C_200] = 200.0
    c[C_ONE] = 1.0
    c[C_N1] = float(N1)
    c[C_N2] = float(N2)
    c[C_P23] = 8388608.0
    c[C_HALF] = 0.5
    c[C_D] = float(D)
    c[C_S127] = 127.0
    c[C_ZERO] = 0.0
    c[C_16384] = 16384.0
    c[C_LO16] = 16384.0 - 128.0
    c[C_HI16] = 16384.0 + 127.0
    c[C_RS1] = 0.0
    c[C_RS2] = 0.0
    c[C_NW1] = 1.0 / (D * H)
    c[C_NW2] = 1.0 / (D * H)
    c[C_EPSC] = EPS
    c[C_DINV] = 1.0 / D
    return np.broadcast_to(c, (128, NCONST)).copy()


def prep_inputs(L0_attn_x, L0_attn_w_xproj, L0_attn_b_xproj, L0_attn_A,
                L0_attn_D_param, L0_attn_w_out, L0_attn_b_out, L0_attn_gamma,
                L0_attn_beta, L0_ffn_ln_g, L0_ffn_ln_b, L0_ffn_w1, L0_ffn_w2):
    f = np.ascontiguousarray

    def fm(v):
        return f(np.asarray(v, np.float32).reshape(DC, 128).T)

    shared = {
        "wxT": f(np.asarray(L0_attn_w_xproj, np.float32).T),
        "woT": f(np.asarray(L0_attn_w_out, np.float32).T),
        "w1T": f(np.asarray(L0_ffn_w1, np.float32).T),
        "w2T": f(np.asarray(L0_ffn_w2, np.float32).T),
        "bxp": f(np.asarray(L0_attn_b_xproj, np.float32).reshape(EC, 128).T),
        "afm": f(np.asarray(L0_attn_A, np.float32).reshape(DC, 128, 16)
                 .transpose(1, 0, 2).reshape(128, DC * 16)),
        "vecs": f(np.concatenate([fm(L0_attn_D_param), fm(L0_attn_gamma), fm(L0_attn_beta),
                                  fm(L0_ffn_ln_g), fm(L0_ffn_ln_b), fm(L0_attn_b_out)], axis=1)),
        "ident": np.eye(128, dtype=np.float32),
        "consts": make_consts(),
    }
    xs = np.asarray(L0_attn_x, np.float32)
    return [dict(shared, x=f(xs[c])) for c in range(8)]


def kernel(**inputs):
    in_maps = prep_inputs(**inputs)
    nc, _ = build_program()
    res = run_bass_kernel_spmd(nc, in_maps, core_ids=list(range(8)))
    return np.stack([res.results[c]["y"] for c in range(8)], axis=0)
